# version 02
# baseline (speedup 1.0000x reference)
import contextlib
import functools
import numpy as np
import ml_dtypes
import concourse.bass as bass
import concourse.mybir as mybir
from concourse.bass_utils import run_bass_kernel_spmd

F32 = mybir.dt.float32
BF16 = mybir.dt.bfloat16
I32 = mybir.dt.int32
AF = mybir.ActivationFunctionType
ALU = mybir.AluOpType
AX = mybir.AxisListType

C = 1024
T = 4096
NCH = 8
PI = float(np.pi)
NEG = -1.0e9

PHASES = ("A", "B", "C", "D")
DEBUG_A = False
DEBUG_B = False
DEBUG_C = False
DEBUG_D = False
D_PARTS = "gec"
D_GLEVEL = 9
SERIAL = False


class Sched:
    COMP = ("te", "ve", "se", "gp")
    QUEUES = ("qs", "qg", "qc")
    Q2ENG = {"qs": "sy", "qg": "gp", "qc": "se"}
    STREAMS = ("te", "ve", "se", "gp", "sy")

    def __init__(self, nc):
        self.nc = nc
        self.prog = {e: [] for e in self.STREAMS}
        self.sems = {}
        self.cnt = {s: 0 for s in self.COMP + self.QUEUES}
        self.clock = {e: {s: 0 for s in self.COMP + self.QUEUES}
                      for e in self.STREAMS}
        self.wr = {}
        self.rd = {}
        self.nops = 0

    def init_sems(self, stack):
        for s in self.COMP + self.QUEUES:
            self.sems[s] = stack.enter_context(self.nc.semaphore("sem_" + s))

    def _wait(self, eng, sem_name, val):
        if sem_name == eng:
            return
        if self.clock[eng][sem_name] >= val:
            return
        self.clock[eng][sem_name] = val
        sem = self.sems[sem_name]
        self.prog[eng].append(lambda b, s=sem, v=val: b.wait_ge(s, v))

    def _dep_waits(self, eng, reads, writes):
        for k in reads:
            d = self.wr.get(k)
            if d is not None:
                self._wait(eng, *d)
        for k in writes:
            d = self.wr.get(k)
            if d is not None:
                self._wait(eng, *d)
            for r in self.rd.get(k, ()):
                self._wait(eng, *r)

    def op(self, eng, emit, reads=(), writes=()):
        if SERIAL:
            self.barrier()
        self._dep_waits(eng, reads, writes)
        sem = self.sems[eng]
        self.cnt[eng] += 1
        val = self.cnt[eng]
        self.nops += 1

        def go(b, emit=emit, sem=sem):
            inst = emit(b)
            inst.then_inc(sem, 1)

        self.prog[eng].append(go)
        tok = (eng, val)
        for k in reads:
            self.rd.setdefault(k, []).append(tok)
        for k in writes:
            self.wr[k] = tok
            self.rd[k] = []
        return tok

    def dma(self, q, out, in_, reads=(), writes=()):
        eng = self.Q2ENG[q]
        self._dep_waits(eng, reads, writes)
        sem = self.sems[q]
        self.cnt[q] += 16
        val = self.cnt[q]
        self.nops += 1

        def go(b, out=out, in_=in_, sem=sem):
            b.dma_start(out=out, in_=in_).then_inc(sem, 16)

        self.prog[eng].append(go)
        tok = (q, val)
        for k in reads:
            self.rd.setdefault(k, []).append(tok)
        for k in writes:
            self.wr[k] = tok
            self.rd[k] = []
        return tok

    def barrier(self):
        cur = dict(self.cnt)
        for eng in self.STREAMS:
            for s, v in cur.items():
                if v:
                    self._wait(eng, s, v)

    def finalize(self):
        self.barrier()


class Arena:

    def __init__(self, ap_f32, words):
        self.base_ap = ap_f32
        self.words = words
        self.off = 0
        self.marks = []

    def f32(self, n):
        a = self.off
        self.off += n
        assert self.off <= self.words, f"arena overflow: {self.off} > {self.words}"
        return self.base_ap[:, a:a + n]

    def bf16(self, n):
        nw = (n + 1) // 2
        return self.f32(nw).bitcast(BF16)[:, 0:n]

    def i32(self, n):
        return self.f32(n).bitcast(I32)

    def push(self):
        self.marks.append(self.off)

    def pop(self):
        self.off = self.marks.pop()

    def reset(self, off=0):
        self.off = off


def build_program(phases=PHASES):
    nc = bass.Bass("TRN2", target_bir_lowering=False)
    dt = nc.dram_tensor

    xT = dt("xT", [C, T], F32, kind="ExternalInput").ap()
    wkT = dt("wkT", [C, C], F32, kind="ExternalInput").ap()
    wvT = dt("wvT", [C, C], F32, kind="ExternalInput").ap()
    wrT = dt("wrT", [C, C], F32, kind="ExternalInput").ap()
    woT = dt("woT", [C, C], F32, kind="ExternalInput").ap()
    tmix = dt("tmix", [128, 3 * NCH], F32, kind="ExternalInput").ap()
    tdec = dt("tdec", [128, NCH], F32, kind="ExternalInput").ap()
    tfirst = dt("tfirst", [128, NCH], F32, kind="ExternalInput").ap()
    theta = dt("theta", [C, C], F32, kind="ExternalInput").ap()
    phi = dt("phi", [C, C], F32, kind="ExternalInput").ap()
    theta_n = dt("theta_n", [C, C], F32, kind="ExternalInput").ap()
    phi_n = dt("phi_n", [C, C], F32, kind="ExternalInput").ap()
    wattnT = dt("wattnT", [C, 3 * C], F32, kind="ExternalInput").ap()
    wprojT = dt("wprojT", [C, C], F32, kind="ExternalInput").ap()
    gwT = dt("gwT", [C, 8], F32, kind="ExternalInput").ap()
    gb = dt("gb", [8, 1], F32, kind="ExternalInput").ap()
    rmsw = dt("rmsw", [128, 8], F32, kind="ExternalInput").ap()
    w1 = dt("w1", [8, C, 2048], F32, kind="ExternalInput").ap()
    b1 = dt("b1", [128, 128], F32, kind="ExternalInput").ap()
    w2 = dt("w2", [8, 2048, C], F32, kind="ExternalInput").ap()
    b2 = dt("b2", [128, 64], F32, kind="ExternalInput").ap()
    ident_in = dt("ident_in", [128, 128], BF16, kind="ExternalInput").ap()
    m1_in = dt("m1_in", [128, 128], BF16, kind="ExternalInput").ap()
    m2_in = dt("m2_in", [128, 128], BF16, kind="ExternalInput").ap()
    mglob_in = dt("mglob_in", [4, 512], BF16, kind="ExternalInput").ap()
    sel8_in = dt("sel8_in", [8, 8 * 128], BF16, kind="ExternalInput").ap()

    x1 = dt("x1", [C, T], F32, kind="Internal").ap()
    x2 = dt("x2", [C, T], F32, kind="Internal").ap()
    x3 = dt("x3", [C, T], F32, kind="Internal").ap()
    q_s = dt("q_s", [C, T], BF16, kind="Internal").ap()
    k_s = dt("k_s", [C, T], BF16, kind="Internal").ap()
    v_s = dt("v_s", [T, C], BF16, kind="Internal").ap()

    out_fm = dt("out_fm", [C, T], F32, kind="ExternalOutput").ap()
    if DEBUG_A:
        dbg_k = dt("dbg_k", [C, T], F32, kind="ExternalOutput").ap()
        dbg_v = dt("dbg_v", [C, T], F32, kind="ExternalOutput").ap()
        dbg_sr = dt("dbg_sr", [C, T], F32, kind="ExternalOutput").ap()
        dbg_xL = dt("dbg_xL", [C, T], BF16, kind="ExternalOutput").ap()
        dbg_y = dt("dbg_y", [C, T], F32, kind="ExternalOutput").ap()
        dbg_wk = dt("dbg_wk", [128, NCH * C], BF16, kind="ExternalOutput").ap()
        dbg_xk = dt("dbg_xk", [128, NCH * 512], BF16, kind="ExternalOutput").ap()
    if DEBUG_C:
        dbg_q = dt("dbg_q", [C, T], BF16, kind="ExternalOutput").ap()
        dbg_kc = dt("dbg_kc", [C, T], BF16, kind="ExternalOutput").ap()
        dbg_vc = dt("dbg_vc", [T, C], BF16, kind="ExternalOutput").ap()
        dbg_Y = dt("dbg_Y", [C, T], BF16, kind="ExternalOutput").ap()
        dbg_PT = dt("dbg_PT", [128, 8 * 384], BF16, kind="ExternalOutput").ap()
        dbg_O = dt("dbg_O", [128, 2 * 512], BF16, kind="ExternalOutput").ap()
        dbg_V2 = dt("dbg_V2", [128, 2 * 32 * 66], BF16, kind="ExternalOutput").ap()
    if DEBUG_D:
        dbg_gl = dt("dbg_gl", [128, 64], F32, kind="ExternalOutput").ap()
        dbg_ng8 = dt("dbg_ng8", [128, 8], BF16, kind="ExternalOutput").ap()
        dbg_ngr = dt("dbg_ngr", [128, 1024], BF16, kind="ExternalOutput").ap()
        dbg_tr = dt("dbg_tr", [128, 32], F32, kind="ExternalOutput").ap()
        dbg_y = dt("dbg_yD", [128, 8 * 1024], F32, kind="ExternalOutput").ap()
        dbg_h = dt("dbg_hD", [128, 8 * 1024], BF16, kind="ExternalOutput").ap()
        dbg_xl = dt("dbg_xlD", [128, 8 * 1024], BF16, kind="ExternalOutput").ap()
    if DEBUG_B:
        dbg_Ab = dt("dbg_Ab", [128, NCH * C], BF16, kind="ExternalOutput").ap()
        dbg_Bb = dt("dbg_Bb", [128, NCH * C], BF16, kind="ExternalOutput").ap()
        dbg_xLB = dt("dbg_xLB", [C, T], BF16, kind="ExternalOutput").ap()

    stack = contextlib.ExitStack()
    with stack:
        s = Sched(nc)
        s.init_sems(stack)
        sb = lambda name, shape, dtype: stack.enter_context(
            nc.sbuf_tensor(name, shape, dtype)).ap()
        pst = lambda name, shape, dtype=F32: stack.enter_context(
            nc.psum_tensor(name, shape, dtype)).ap()

        ones_col = sb("ones_col", [128, 1], BF16)
        ones_row = sb("ones_row", [1, 128], BF16)
        ident = sb("ident", [128, 128], BF16)
        m1 = sb("m1", [128, 128], BF16)
        m2 = sb("m2", [128, 128], BF16)
        mglob = sb("mglob", [4, 512], BF16)
        sel8 = sb("sel8", [8, 8 * 128], BF16)
        vec = sb("vec", [128, 8 * NCH], F32)
        gbv = sb("gbv", [8, 1], F32)
        rmsv = sb("rmsv", [128, 8], F32)
        b1v = sb("b1v", [128, 128], F32)
        b2v = sb("b2v", [128, 64], F32)
        m_row = sb("m_row", [1, 512], F32)
        v_row = sb("v_row", [1, 512], F32)
        abrow = sb("abrow", [1, 512], BF16)
        bbrow = sb("bbrow", [1, 512], BF16)
        carry = sb("carry", [128, 3 * NCH], F32)
        consts = sb("consts", [128, 4], F32)

        AR_WORDS = 49000
        ar_t = sb("arena", [128, AR_WORDS], F32)
        ar = Arena(ar_t, AR_WORDS)

        pb = [pst(f"pb{i}", [128, 512]) for i in range(5)]
        pstat = pst("pstat", [64, 1024])
        pbt = pst("pbt", [128, 512], BF16)

        s.op("ve", lambda b: b.memset(ones_col, 1.0), writes=["ones_col"])
        s.op("ve", lambda b: b.memset(consts[:, 0:1], 1e-5), writes=["consts"])
        s.op("ve", lambda b: b.memset(consts[:, 1:2], float(np.finfo(np.float32).eps)),
             writes=["consts"])
        s.op("ve", lambda b: b.memset(ones_row, 1.0), writes=["ones_row"])
        s.dma("qs", ident, ident_in, writes=["ident"])
        s.dma("qs", m1, m1_in, writes=["m1"])
        s.dma("qs", m2, m2_in, writes=["m2"])
        s.dma("qs", mglob, mglob_in, writes=["mglob"])
        s.dma("qs", sel8, sel8_in, writes=["sel8"])
        s.dma("qs", vec[:, 0:3 * NCH], tmix, writes=["vec_tm"])
        s.dma("qs", vec[:, 3 * NCH:4 * NCH], tdec, writes=["vec_w"])
        s.dma("qs", vec[:, 4 * NCH:5 * NCH], tfirst, writes=["vec_u"])
        s.op("se", lambda b: b.activation(out=vec[:, 3 * NCH:4 * NCH],
                                          in_=vec[:, 3 * NCH:4 * NCH], func=AF.Exp),
             reads=["vec_w"], writes=["vec_w"])
        s.op("ve", lambda b: b.tensor_scalar_mul(out=vec[:, 3 * NCH:4 * NCH],
                                                 in0=vec[:, 3 * NCH:4 * NCH],
                                                 scalar1=-1.0),
             reads=["vec_w"], writes=["vec_w"])
        tmk = lambda cc: vec[:, cc:cc + 1]
        tmv = lambda cc: vec[:, NCH + cc:NCH + cc + 1]
        tmr = lambda cc: vec[:, 2 * NCH + cc:2 * NCH + cc + 1]
        wv_ = lambda cc: vec[:, 3 * NCH + cc:3 * NCH + cc + 1]
        uv_ = lambda cc: vec[:, 4 * NCH + cc:4 * NCH + cc + 1]

        def load_weightT(dst_bf, src_dram, stage, ncols, tag):
            for cc in range(NCH):
                p = cc % 2
                st_ap = stage[p]
                s.dma("qs", st_ap[:, 0:ncols],
                      src_dram[cc * 128:(cc + 1) * 128, 0:ncols],
                      writes=[f"stg{p}"])
                s.op("ve" if cc % 2 == 0 else "gp",
                     lambda b, st=st_ap, cc=cc: b.tensor_copy(
                         out=dst_bf[:, cc * ncols:(cc + 1) * ncols],
                         in_=st[:, 0:ncols]),
                     reads=[f"stg{p}"], writes=[f"{tag}"])

        def layernorm_tile(xb, xbf, xLd, tile_tag, Tt=512, xkey="xb",
                           out_off=1, se_sq=True, xl_stride=None, xl_off=None,
                           xb_stride=None, xb_off=0):
            if xl_stride is None:
                xl_stride = Tt + out_off
            if xl_off is None:
                xl_off = out_off
            if xb_stride is None:
                xb_stride = Tt
            xbs = lambda cc: xb[:, cc * xb_stride + xb_off:
                                cc * xb_stride + xb_off + Tt]
            for cc in range(NCH):
                s.op("ve", lambda b, cc=cc: b.tensor_copy(
                    out=xbf[:, cc * Tt:(cc + 1) * Tt],
                    in_=xbs(cc)),
                    reads=[xkey + str(cc)], writes=[f"xbf{cc}"])

            def sum_mm(b):
                for cc in range(NCH):
                    i = b.matmul(pstat[0:1, 0:Tt], ones_col,
                                 xbf[:, cc * Tt:(cc + 1) * Tt],
                                 start=(cc == 0), stop=(cc == NCH - 1))
                return i
            s.op("te", sum_mm,
                 reads=[f"xbf{cc}" for cc in range(NCH)] + ["ones_col"],
                 writes=["pstat"])
            for cc in range(NCH):
                s.op("se", lambda b, cc=cc: b.activation(
                    out=xbf[:, cc * Tt:(cc + 1) * Tt],
                    in_=xbs(cc), func=AF.Square),
                    reads=[xkey + str(cc)], writes=[f"xbf{cc}"])

            def sq_mm(b):
                for cc in range(NCH):
                    i = b.matmul(pstat[0:1, 512:512 + Tt], ones_col,
                                 xbf[:, cc * Tt:(cc + 1) * Tt],
                                 start=(cc == 0), stop=(cc == NCH - 1))
                return i
            s.op("te", sq_mm,
                 reads=[f"xbf{cc}" for cc in range(NCH)] + ["ones_col"],
                 writes=["pstat"])
            s.op("ve", lambda b: b.tensor_scalar_mul(
                out=m_row[0:1, 0:Tt], in0=pstat[0:1, 0:Tt],
                scalar1=1.0 / C),
                reads=["pstat"], writes=["rowbuf"])
            s.op("ve", lambda b: b.scalar_tensor_tensor(
                out=v_row[0:1, 0:Tt], in0=m_row[0:1, 0:Tt],
                scalar=-1.0, in1=m_row[0:1, 0:Tt], op0=ALU.mult,
                op1=ALU.mult),
                reads=["rowbuf"], writes=["rowbuf"])
            s.op("ve", lambda b: b.scalar_tensor_tensor(
                out=v_row[0:1, 0:Tt], in0=pstat[0:1, 512:512 + Tt],
                scalar=1.0 / C, in1=v_row[0:1, 0:Tt], op0=ALU.mult,
                op1=ALU.add),
                reads=["rowbuf", "pstat"], writes=["rowbuf"])
            s.op("se", lambda b: b.activation(
                out=v_row[0:1, 0:Tt], in_=v_row[0:1, 0:Tt], func=AF.Ln,
                bias=consts[0:1, 0:1]),
                reads=["rowbuf", "consts"], writes=["rowbuf"])
            s.op("se", lambda b: b.activation(
                out=v_row[0:1, 0:Tt], in_=v_row[0:1, 0:Tt], func=AF.Exp,
                scale=-0.5),
                reads=["rowbuf"], writes=["rowbuf"])
            s.op("ve", lambda b: b.tensor_copy(
                out=abrow[0:1, 0:Tt], in_=v_row[0:1, 0:Tt]),
                reads=["rowbuf"], writes=["abrow"])
            s.op("ve", lambda b: b.scalar_tensor_tensor(
                out=bbrow[0:1, 0:Tt], in0=m_row[0:1, 0:Tt], scalar=-1.0,
                in1=v_row[0:1, 0:Tt], op0=ALU.mult, op1=ALU.mult),
                reads=["rowbuf"], writes=["abrow"])
            s.op("te", lambda b: b.matmul(pb[2][:, 0:Tt], ones_row[0:1, :],
                                          abrow[0:1, 0:Tt], start=True,
                                          stop=True),
                 reads=["abrow", "ones_row"], writes=["bank2"])
            s.op("te", lambda b: b.matmul(pb[3][:, 0:Tt], ones_row[0:1, :],
                                          bbrow[0:1, 0:Tt], start=True,
                                          stop=True),
                 reads=["abrow", "ones_row"], writes=["bank3"])
            for cc in range(NCH):
                s.op("ve", lambda b, cc=cc: b.tensor_tensor(
                    out=xbf[:, cc * Tt:(cc + 1) * Tt],
                    in0=xbs(cc), in1=pb[2][:, 0:Tt],
                    op=ALU.mult),
                    reads=[xkey + str(cc), "bank2"], writes=[f"xbf{cc}"])
                s.op("ve", lambda b, cc=cc: b.tensor_tensor(
                    out=xLd[:, cc * xl_stride + xl_off:
                            cc * xl_stride + xl_off + Tt],
                    in0=xbf[:, cc * Tt:(cc + 1) * Tt], in1=pb[3][:, 0:Tt],
                    op=ALU.add),
                    reads=[f"xbf{cc}", "bank3"], writes=[f"xL{cc}"])

        def phase_A():
            Tt = 512
            NT = T // Tt
            ar.reset()
            wk_b = ar.bf16(NCH * C)
            wv_b = ar.bf16(NCH * C)
            wr_b = ar.bf16(NCH * C)
            wo_b = ar.bf16(NCH * C)
            stage = [ar.f32(C), ar.f32(C)]
            xb = ar.f32(NCH * Tt)
            xbf = ar.bf16(NCH * Tt)
            xL = ar.bf16(NCH * (Tt + 1))
            dbuf = ar.bf16(Tt)
            xk = ar.bf16(NCH * Tt)
            xv = ar.bf16(NCH * Tt)
            xr = ar.bf16(NCH * Tt)
            kvb = [[ar.f32(Tt) for _ in range(3)] for _ in range(2)]
            Pb = ar.f32(Tt + 1)
            Ab = ar.f32(Tt + 1)
            Bb = ar.f32(Tt + 1)
            Db = ar.f32(Tt)
            Eb = ar.f32(Tt)
            Ub = ar.f32(Tt)
            Qb = ar.f32(Tt)
            Nb = ar.f32(Tt)
            Dn = ar.f32(Tt)
            yr = ar.bf16(NCH * Tt)
            x1b = ar.f32(NCH * Tt)

            load_weightT(wk_b, wkT, stage, C, "wk")
            if DEBUG_A:
                s.dma("qg", dbg_wk, wk_b, reads=["wk"])
            load_weightT(wv_b, wvT, stage, C, "wv")
            load_weightT(wr_b, wrT, stage, C, "wr")
            load_weightT(wo_b, woT, stage, C, "wo")

            for cc in range(NCH):
                s.op("ve", lambda b, cc=cc: b.memset(
                    xL[:, cc * (Tt + 1):cc * (Tt + 1) + 1], 0.0),
                    writes=[f"xL{cc}"])
            s.op("ve", lambda b: b.memset(carry[:, 0:NCH], -1e38),
                 writes=["carP"])
            s.op("ve", lambda b: b.memset(carry[:, NCH:3 * NCH], 0.0),
                 writes=["carA", "carB"])

            for it in range(NT):
                t0 = it * Tt
                for cc in range(NCH):
                    s.dma("qs", xb[:, cc * Tt:(cc + 1) * Tt],
                          xT[cc * 128:(cc + 1) * 128, t0:t0 + Tt],
                          writes=[f"xb{cc}"])
                layernorm_tile(xb, xbf, xL, tile_tag="A")
                if DEBUG_A:
                    for cc in range(NCH):
                        o = cc * (Tt + 1)
                        s.dma("qg", dbg_xL[cc * 128:(cc + 1) * 128, t0:t0 + Tt],
                              xL[:, o + 1:o + 1 + Tt], reads=[f"xL{cc}"])
                for cc in range(NCH):
                    o = cc * (Tt + 1)
                    cur = xL[:, o + 1:o + 1 + Tt]
                    prv = xL[:, o:o + Tt]
                    s.op("ve", lambda b, cur=cur, prv=prv: b.tensor_sub(
                        out=dbuf, in0=cur, in1=prv),
                        reads=[f"xL{cc}"], writes=["dbuf"])
                    for dst, tmf, eng, tag in ((xk, tmk, "ve", "xk"),
                                               (xv, tmv, "ve", "xv"),
                                               (xr, tmr, "ve", "xr")):
                        s.op(eng, lambda b, dst=dst, tmf=tmf, cc=cc, prv=prv:
                             b.scalar_tensor_tensor(
                                 out=dst[:, cc * Tt:(cc + 1) * Tt], in0=dbuf,
                                 scalar=tmf(cc), in1=prv, op0=ALU.mult,
                                 op1=ALU.add),
                             reads=["dbuf", f"xL{cc}", "vec_tm"],
                             writes=[tag])
                    s.op("se", lambda b, cc=cc, o=o: b.copy(
                        out=xL[:, o:o + 1], in_=xL[:, o + Tt:o + Tt + 1]),
                        reads=[f"xL{cc}"], writes=[f"xL{cc}"])
                if DEBUG_A and it == 0:
                    s.dma("qg", dbg_xk, xk, reads=["xk"])
                for dc in range(NCH):
                    pr, pk, pv = pb[dc % 2], pb[2 + dc % 2], pb[4]
                    kb, vb, srb = kvb[dc % 2]

                    def mmgrp(b, W=None, mv=None, ps=None, dc=dc):
                        for cc2 in range(NCH):
                            i = b.matmul(
                                ps[:, 0:Tt],
                                W[:, cc2 * C + dc * 128:cc2 * C + dc * 128 + 128],
                                mv[:, cc2 * Tt:(cc2 + 1) * Tt],
                                start=(cc2 == 0), stop=(cc2 == NCH - 1))
                        return i
                    s.op("te", functools.partial(mmgrp, W=wr_b, mv=xr, ps=pr),
                         reads=["xr", "wr"], writes=[f"bank{dc % 2}"])
                    s.op("te", functools.partial(mmgrp, W=wk_b, mv=xk, ps=pk),
                         reads=["xk", "wk"], writes=[f"bank{2 + dc % 2}"])
                    s.op("te", functools.partial(mmgrp, W=wv_b, mv=xv, ps=pv),
                         reads=["xv", "wv"], writes=["bank4"])
                    s.op("se", lambda b, srb=srb, pr=pr: b.activation(
                        out=srb, in_=pr[:, 0:Tt], func=AF.Sigmoid),
                        reads=[f"bank{dc % 2}"], writes=[f"sr{dc % 2}"])
                    s.op("se", lambda b, kb=kb, pk=pk: b.copy(
                        out=kb, in_=pk[:, 0:Tt]),
                        reads=[f"bank{2 + dc % 2}"], writes=[f"k{dc % 2}"])
                    s.op("se", lambda b, vb=vb, pv=pv: b.copy(
                        out=vb, in_=pv[:, 0:Tt]),
                        reads=["bank4"], writes=[f"v{dc % 2}"])
                    kkey, vkey, skey = f"k{dc % 2}", f"v{dc % 2}", f"sr{dc % 2}"

                    s.op("se", lambda b, dc=dc: b.copy(
                        out=Pb[:, 0:1], in_=carry[:, dc:dc + 1]),
                        reads=["carP"], writes=["Pb"])
                    s.op("se", lambda b, dc=dc: b.copy(
                        out=Ab[:, 0:1], in_=carry[:, NCH + dc:NCH + dc + 1]),
                        reads=["carA"], writes=["Ab"])
                    s.op("se", lambda b, dc=dc: b.copy(
                        out=Bb[:, 0:1], in_=carry[:, 2 * NCH + dc:2 * NCH + dc + 1]),
                        reads=["carB"], writes=["Bb"])
                    s.op("ve", lambda b, kb=kb, dc=dc: b.tensor_tensor_scan(
                        out=Pb[:, 1:Tt + 1],
                        data0=wv_(dc).broadcast_to([128, Tt]), data1=kb,
                        initial=Pb[:, 0:1], op0=ALU.add, op1=ALU.max),
                        reads=[kkey, "vec_w", "Pb"], writes=["Pb"])
                    s.op("ve", lambda b, dc=dc: b.scalar_tensor_tensor(
                        out=Db, in0=Pb[:, 0:Tt], scalar=wv_(dc),
                        in1=Pb[:, 1:Tt + 1], op0=ALU.add, op1=ALU.subtract),
                        reads=["Pb", "vec_w"], writes=["Db"])
                    s.op("se", lambda b: b.activation(out=Db, in_=Db,
                                                      func=AF.Exp),
                         reads=["Db"], writes=["Db"])
                    s.op("gp", lambda b, kb=kb: b.tensor_sub(
                        out=Eb, in0=kb, in1=Pb[:, 1:Tt + 1]),
                        reads=[kkey, "Pb"], writes=["Eb"])
                    s.op("se", lambda b: b.activation(out=Eb, in_=Eb,
                                                      func=AF.Exp),
                         reads=["Eb"], writes=["Eb"])
                    s.op("gp", lambda b, vb=vb: b.tensor_mul(
                        out=Ub, in0=Eb, in1=vb),
                        reads=["Eb", vkey], writes=["Ub"])
                    s.op("ve", lambda b: b.tensor_tensor_scan(
                        out=Ab[:, 1:Tt + 1], data0=Db, data1=Ub,
                        initial=Ab[:, 0:1], op0=ALU.mult, op1=ALU.add),
                        reads=["Db", "Ub", "Ab"], writes=["Ab"])
                    s.op("ve", lambda b: b.tensor_tensor_scan(
                        out=Bb[:, 1:Tt + 1], data0=Db, data1=Eb,
                        initial=Bb[:, 0:1], op0=ALU.mult, op1=ALU.add),
                        reads=["Db", "Eb", "Bb"], writes=["Bb"])
                    s.op("se", lambda b, dc=dc: b.copy(
                        out=carry[:, dc:dc + 1], in_=Pb[:, Tt:Tt + 1]),
                        reads=["Pb"], writes=["carP"])
                    s.op("se", lambda b, dc=dc: b.copy(
                        out=carry[:, NCH + dc:NCH + dc + 1], in_=Ab[:, Tt:Tt + 1]),
                        reads=["Ab"], writes=["carA"])
                    s.op("se", lambda b, dc=dc: b.copy(
                        out=carry[:, 2 * NCH + dc:2 * NCH + dc + 1],
                        in_=Bb[:, Tt:Tt + 1]),
                        reads=["Bb"], writes=["carB"])
                    if DEBUG_A:
                        s.dma("qg", dbg_k[dc * 128:(dc + 1) * 128, t0:t0 + Tt],
                              kb, reads=[kkey])
                        s.dma("qg", dbg_v[dc * 128:(dc + 1) * 128, t0:t0 + Tt],
                              vb, reads=[vkey])
                        s.dma("qg", dbg_sr[dc * 128:(dc + 1) * 128, t0:t0 + Tt],
                              srb, reads=[skey])
                    s.op("ve", lambda b, kb=kb, dc=dc: b.scalar_tensor_tensor(
                        out=Qb, in0=kb, scalar=uv_(dc), in1=Pb[:, 0:Tt],
                        op0=ALU.add, op1=ALU.subtract),
                        reads=[kkey, "Pb", "vec_u"], writes=["Qb"])
                    s.op("se", lambda b: b.activation(out=Qb, in_=Qb,
                                                      func=AF.Exp),
                         reads=["Qb"], writes=["Qb"])
                    s.op("ve", lambda b: b.tensor_scalar_min(
                        out=Qb, in0=Qb, scalar1=1e15),
                        reads=["Qb"], writes=["Qb"])
                    s.op("gp", lambda b, vb=vb: b.tensor_mul(
                        out=Nb, in0=Qb, in1=vb),
                        reads=["Qb", vkey], writes=["Nb"])
                    s.op("gp", lambda b: b.tensor_add(
                        out=Nb, in0=Nb, in1=Ab[:, 0:Tt]),
                        reads=["Nb", "Ab"], writes=["Nb"])
                    s.op("ve", lambda b: b.tensor_add(
                        out=Dn, in0=Qb, in1=Bb[:, 0:Tt]),
                        reads=["Qb", "Bb"], writes=["Dn"])
                    s.op("se", lambda b: b.activation(out=Dn, in_=Dn,
                                                      func=AF.Ln),
                         reads=["Dn"], writes=["Dn"])
                    s.op("se", lambda b: b.activation(out=Dn, in_=Dn,
                                                      func=AF.Exp, scale=-1.0),
                         reads=["Dn"], writes=["Dn"])
                    s.op("ve", lambda b: b.tensor_mul(out=Nb, in0=Nb, in1=Dn),
                         reads=["Nb", "Dn"], writes=["Nb"])
                    s.op("ve", lambda b, srb=srb, dc=dc: b.tensor_mul(
                        out=yr[:, dc * Tt:(dc + 1) * Tt], in0=Nb, in1=srb),
                        reads=["Nb", skey], writes=["yr"])
                    if DEBUG_A:
                        s.dma("qg", dbg_y[dc * 128:(dc + 1) * 128, t0:t0 + Tt],
                              Nb, reads=["Nb"])
                for cc in range(NCH):
                    po = pb[cc % 2]

                    def omm(b, cc=cc, po=po):
                        for dc2 in range(NCH):
                            i = b.matmul(
                                po[:, 0:Tt],
                                wo_b[:, dc2 * C + cc * 128:dc2 * C + cc * 128 + 128],
                                yr[:, dc2 * Tt:(dc2 + 1) * Tt],
                                start=(dc2 == 0), stop=(dc2 == NCH - 1))
                        return i
                    s.op("te", omm, reads=["yr", "wo"], writes=[f"bank{cc % 2}"])
                    s.op("ve", lambda b, cc=cc, po=po: b.tensor_tensor(
                        out=x1b[:, cc * Tt:(cc + 1) * Tt],
                        in0=po[:, 0:Tt], in1=xb[:, cc * Tt:(cc + 1) * Tt],
                        op=ALU.add),
                        reads=[f"bank{cc % 2}", f"xb{cc}"], writes=[f"x1b{cc}"])
                    s.dma("qg", x1[cc * 128:(cc + 1) * 128, t0:t0 + Tt],
                          x1b[:, cc * Tt:(cc + 1) * Tt], reads=[f"x1b{cc}"])

        def phase_B():
            Tt = 512
            NT = T // Tt
            ar.reset()
            A_b = ar.bf16(NCH * C)
            B_b = ar.bf16(NCH * C)
            stg = [ar.f32(C), ar.f32(C)]
            tbuf = ar.f32(C)
            kbuf = ar.f32(C)
            kbi = kbuf.bitcast(I32)[:, 0:C]
            xb = ar.f32(NCH * Tt)
            xbf = ar.bf16(NCH * Tt)
            xL = ar.bf16(NCH * Tt)
            ev = [ar.f32(Tt) for _ in range(4)]
            x2b = ar.f32(NCH * Tt)

            inv2pi = float(1.0 / (2 * np.pi))
            c1 = float(np.float32(2 * np.pi))
            c2 = float(2 * np.pi - np.float64(np.float32(2 * np.pi)))

            Af = ar.f32(NCH * C)
            crow = ar.f32(C)
            crow_b = ar.bf16(C)
            for dst, s0, s1, shift, tag in ((A_b, theta, theta_n, PI / 2, "Ab"),
                                            (B_b, phi, phi_n, 0.0, "Bb")):
                for cc in range(NCH):
                    s.dma("qs", stg[0], s0[cc * 128:(cc + 1) * 128, :],
                          writes=["stg0"])
                    s.dma("qs", stg[1], s1[cc * 128:(cc + 1) * 128, :],
                          writes=["stg1"])
                    s.op("ve", lambda b: b.tensor_add(
                        out=tbuf, in0=stg[0], in1=stg[1]),
                        reads=["stg0", "stg1"], writes=["tbuf"])
                    if shift:
                        s.op("ve", lambda b, shift=shift: b.tensor_scalar_add(
                            out=tbuf, in0=tbuf, scalar1=shift),
                            reads=["tbuf"], writes=["tbuf"])
                    s.op("ve", lambda b: b.tensor_scalar_mul(
                        out=kbuf, in0=tbuf, scalar1=inv2pi),
                        reads=["tbuf"], writes=["kbuf"])
                    s.op("gp", lambda b: b.tensor_copy(out=kbi, in_=kbuf),
                         reads=["kbuf"], writes=["kbuf"])
                    s.op("gp", lambda b: b.tensor_copy(out=kbuf, in_=kbi),
                         reads=["kbuf"], writes=["kbuf"])
                    s.op("ve", lambda b: b.scalar_tensor_tensor(
                        out=tbuf, in0=kbuf, scalar=-c1, in1=tbuf,
                        op0=ALU.mult, op1=ALU.add),
                        reads=["kbuf", "tbuf"], writes=["tbuf"])
                    s.op("ve", lambda b: b.scalar_tensor_tensor(
                        out=tbuf, in0=kbuf, scalar=-c2, in1=tbuf,
                        op0=ALU.mult, op1=ALU.add),
                        reads=["kbuf", "tbuf"], writes=["tbuf"])
                    s.op("se", lambda b, cc=cc: b.activation(
                        out=Af[:, cc * C:(cc + 1) * C], in_=tbuf,
                        func=AF.Sin),
                        reads=["tbuf"], writes=[f"Af{cc}"])
                    s.op("ve", lambda b, dst=dst, cc=cc: b.tensor_copy(
                        out=dst[:, cc * C:(cc + 1) * C],
                        in_=Af[:, cc * C:(cc + 1) * C]),
                        reads=[f"Af{cc}"], writes=[tag])

                def cs_mm(b, dst=dst):
                    for h2 in range(2):
                        for cc2 in range(NCH):
                            i = b.matmul(
                                pstat[0:1, h2 * 512:(h2 + 1) * 512], ones_col,
                                dst[:, cc2 * C + h2 * 512:cc2 * C + (h2 + 1) * 512],
                                start=(cc2 == 0), stop=(cc2 == NCH - 1))
                    return i
                s.op("te", cs_mm, reads=[tag, "ones_col"], writes=["pstat"])
                s.op("ve", lambda b: b.tensor_scalar_mul(
                    out=crow[0:1, 0:C], in0=pstat[0:1, 0:C], scalar1=1.0 / C),
                    reads=["pstat"], writes=["crow"])
                s.op("ve", lambda b: b.tensor_copy(out=crow_b[0:1, 0:C],
                                                   in_=crow[0:1, 0:C]),
                     reads=["crow"], writes=["crow"])
                for half in range(2):
                    s.op("te", lambda b, half=half: b.matmul(
                        pb[half][:, 0:512], ones_row[0:1, :],
                        crow_b[0:1, half * 512:(half + 1) * 512],
                        start=True, stop=True),
                        reads=["crow", "ones_row"], writes=[f"bank{half}"])
                for cc in range(NCH):
                    for half in range(2):
                        s.op("ve", lambda b, dst=dst, cc=cc, half=half:
                             b.tensor_sub(
                                 out=dst[:, cc * C + half * 512:cc * C + (half + 1) * 512],
                                 in0=Af[:, cc * C + half * 512:cc * C + (half + 1) * 512],
                                 in1=pb[half][:, 0:512]),
                             reads=[f"Af{cc}", f"bank{half}"], writes=[tag])

            if DEBUG_B:
                s.dma("qg", dbg_Ab, A_b, reads=["Ab"])
                s.dma("qg", dbg_Bb, B_b, reads=["Bb"])
            for it in range(NT):
                t0 = it * Tt
                for cc in range(NCH):
                    s.dma("qs", xb[:, cc * Tt:(cc + 1) * Tt],
                          x1[cc * 128:(cc + 1) * 128, t0:t0 + Tt],
                          writes=[f"xb{cc}"])
                layernorm_tile(xb, xbf, xL, tile_tag="B", out_off=0)
                if DEBUG_B:
                    for cc in range(NCH):
                        s.dma("qg", dbg_xLB[cc * 128:(cc + 1) * 128, t0:t0 + Tt],
                              xL[:, cc * Tt:(cc + 1) * Tt], reads=[f"xL{cc}"])
                for oc in range(NCH):
                    pre, pim = pb[oc % 2], pb[2 + oc % 2]
                    rb, ib = ev[2 * (oc % 2)], ev[2 * (oc % 2) + 1]

                    def mmg(b, W=None, ps=None, oc=oc):
                        for cc2 in range(NCH):
                            i = b.matmul(
                                ps[:, 0:Tt],
                                W[:, cc2 * C + oc * 128:cc2 * C + oc * 128 + 128],
                                xL[:, cc2 * Tt:(cc2 + 1) * Tt],
                                start=(cc2 == 0), stop=(cc2 == NCH - 1))
                        return i
                    xkeys = [f"xL{cc2}" for cc2 in range(NCH)]
                    s.op("te", functools.partial(mmg, W=A_b, ps=pre),
                         reads=xkeys + ["Ab"], writes=[f"bank{oc % 2}"])
                    s.op("te", functools.partial(mmg, W=B_b, ps=pim),
                         reads=xkeys + ["Bb"], writes=[f"bank{2 + oc % 2}"])
                    s.op("se", lambda b, rb=rb, pre=pre: b.activation(
                        out=rb, in_=pre[:, 0:Tt], func=AF.Square),
                        reads=[f"bank{oc % 2}"], writes=[f"ev{2 * (oc % 2)}"])
                    s.op("se", lambda b, ib=ib, pim=pim: b.activation(
                        out=ib, in_=pim[:, 0:Tt], func=AF.Square),
                        reads=[f"bank{2 + oc % 2}"], writes=[f"ev{2 * (oc % 2) + 1}"])
                    s.op("ve", lambda b, rb=rb, ib=ib: b.tensor_add(
                        out=rb, in0=rb, in1=ib),
                        reads=[f"ev{2 * (oc % 2)}", f"ev{2 * (oc % 2) + 1}"],
                        writes=[f"ev{2 * (oc % 2)}"])
                    s.op("se", lambda b, rb=rb: b.activation(
                        out=rb, in_=rb, func=AF.Sqrt),
                        reads=[f"ev{2 * (oc % 2)}"], writes=[f"ev{2 * (oc % 2)}"])
                    s.op("ve", lambda b, rb=rb, oc=oc: b.tensor_tensor(
                        out=x2b[:, oc * Tt:(oc + 1) * Tt], in0=rb,
                        in1=xb[:, oc * Tt:(oc + 1) * Tt], op=ALU.add),
                        reads=[f"ev{2 * (oc % 2)}", f"xb{oc}"],
                        writes=[f"x2b{oc}"])
                    s.dma("qg", x2[oc * 128:(oc + 1) * 128, t0:t0 + Tt],
                          x2b[:, oc * Tt:(oc + 1) * Tt], reads=[f"x2b{oc}"])

        def phase_C():
            Tt = 512
            NT = T // Tt
            NKT = T // 128
            ar.reset()
            wat_b = ar.bf16(NCH * 3 * C)
            stg = [ar.f32(3 * C), ar.f32(3 * C)]
            xb = ar.f32(NCH * Tt)
            xbf = ar.bf16(NCH * Tt)
            xL = ar.bf16(NCH * Tt)
            ev = [ar.bf16(Tt) for _ in range(4)]

            load_weightT(wat_b, wattnT, stg, 3 * C, "wat")

            for it in range(NT):
                t0 = it * Tt
                for cc in range(NCH):
                    s.dma("qs", xb[:, cc * Tt:(cc + 1) * Tt],
                          x2[cc * 128:(cc + 1) * 128, t0:t0 + Tt],
                          writes=[f"xb{cc}"])
                layernorm_tile(xb, xbf, xL, tile_tag="C", out_off=0)
                xkeys = [f"xL{cc}" for cc in range(NCH)]
                for which, base, scratch in (("q", 0, q_s), ("k", C, k_s)):
                    for dc in range(NCH):
                        ps = pb[dc % 2]

                        def qmm(b, base=base, dc=dc, ps=ps):
                            for cc2 in range(NCH):
                                i = b.matmul(
                                    ps[:, 0:Tt],
                                    wat_b[:, cc2 * 3 * C + base + dc * 128:
                                          cc2 * 3 * C + base + dc * 128 + 128],
                                    xL[:, cc2 * Tt:(cc2 + 1) * Tt],
                                    start=(cc2 == 0), stop=(cc2 == NCH - 1))
                            return i
                        s.op("te", qmm, reads=xkeys + ["wat"],
                             writes=[f"bank{dc % 2}"])
                        eb = ev[dc % 2]
                        s.op("se", lambda b, eb=eb, ps=ps: b.copy(
                            out=eb, in_=ps[:, 0:Tt]),
                            reads=[f"bank{dc % 2}"], writes=[f"ev{dc % 2}"])
                        s.dma("qg", scratch[dc * 128:(dc + 1) * 128, t0:t0 + Tt],
                              eb, reads=[f"ev{dc % 2}"])
                for ts_ in range(Tt // 128):
                    for dsl in range(2):
                        bk = 2 + (ts_ * 2 + dsl) % 2
                        ps = pb[bk]

                        def vmm(b, ts_=ts_, dsl=dsl, ps=ps):
                            for cc2 in range(NCH):
                                i = b.matmul(
                                    ps[:, 0:512],
                                    xL[:, cc2 * Tt + ts_ * 128:
                                       cc2 * Tt + ts_ * 128 + 128],
                                    wat_b[:, cc2 * 3 * C + 2 * C + dsl * 512:
                                          cc2 * 3 * C + 2 * C + (dsl + 1) * 512],
                                    start=(cc2 == 0), stop=(cc2 == NCH - 1))
                            return i
                        s.op("te", vmm, reads=xkeys + ["wat"],
                             writes=[f"bank{bk}"])
                        eb = ev[bk]
                        s.op("se", lambda b, eb=eb, ps=ps: b.copy(
                            out=eb, in_=ps[:, 0:512]),
                            reads=[f"bank{bk}"], writes=[f"ev{bk}"])
                        s.dma("qg",
                              v_s[t0 + ts_ * 128:t0 + (ts_ + 1) * 128,
                                  dsl * 512:(dsl + 1) * 512],
                              eb, reads=[f"ev{bk}"])
            s.barrier()
            ar.reset()
            wpj_b = ar.bf16(NCH * C)
            stg2 = [ar.f32(C), ar.f32(C)]
            Ych = [ar.bf16(T) for _ in range(NCH)]
            K2 = [ar.bf16(T) for _ in range(2)]
            Q2 = [ar.bf16(T) for _ in range(2)]
            V2 = [ar.bf16(2 * NKT * 66) for _ in range(2)]
            PT = [ar.bf16(384) for _ in range(7)]
            PTG = ar.bf16(512)
            Osb = ar.bf16(512)
            denrow = ar.f32(512)
            denb = ar.bf16(512)
            xb2 = ar.f32(NCH * Tt)
            x3b = ar.f32(NCH * Tt)

            load_weightT(wpj_b, wprojT, stg2, C, "wpj")
            for pp in range(2):
                s.op("ve", lambda b, pp=pp: b.memset(V2[pp], 1.0),
                     writes=[f"V2_{pp}"])

            v_sr = v_s.rearrange("(kt q) c -> q kt c", q=128)

            def emit_O(j, hp, qb):
                p = j % 2
                h = 2 * j + hp
                hb = 64 * hp
                pO = pb[4]
                lo_q = qb * 512
                rd_keys = [f"V2_{p}", "PTG"]
                for kt in range(max(0, 4 * qb - 2), min(NKT, 4 * qb + 4)):
                    rd_keys.append(f"PT{kt % 7}")

                def ogrp(b, p=p, hp=hp, qb=qb, pO=pO):
                    i = None
                    for blk in range(4):
                        ga = qb * 512 + blk * 128
                        gb = ga + 128
                        wr = [kt for kt in range(max(0, ga // 128 - 2),
                                                 ga // 128 + 1)]
                        for n_, kt in enumerate(wr):
                            i = b.matmul(
                                pO[0:65, blk * 128:(blk + 1) * 128],
                                V2[p][:, hp * NKT * 66 + kt * 66:
                                      hp * NKT * 66 + kt * 66 + 65],
                                PT[kt % 7][:, ga - kt * 128:gb - kt * 128],
                                start=(n_ == 0), stop=False)
                        i = b.matmul(
                            pO[0:65, blk * 128:(blk + 1) * 128],
                            V2[p][0:4, hp * NKT * 66:hp * NKT * 66 + 65],
                            PTG[0:4, blk * 128:(blk + 1) * 128],
                            start=False, stop=True)
                    return i
                s.op("te", ogrp, reads=rd_keys, writes=["bank4"])
                s.op("se", lambda b: b.activation(
                    out=denrow[0:1, 0:512], in_=pO[64:65, 0:512], func=AF.Ln),
                    reads=["bank4"], writes=["denrow"])
                s.op("se", lambda b: b.activation(
                    out=denb[0:1, 0:512], in_=denrow[0:1, 0:512], func=AF.Exp,
                    scale=-1.0),
                    reads=["denrow"], writes=["denb"])
                s.op("te", lambda b: b.matmul(
                    pb[3][:, 0:512], ones_row[0:1, 0:128], denb[0:1, 0:512],
                    start=True, stop=True),
                    reads=["denb", "ones_row"], writes=["bank3"])
                s.op("se", lambda b, hb=hb: b.copy(
                    out=Osb[hb:hb + 64, 0:512], in_=pO[0:64, 0:512]),
                    reads=["bank4"], writes=["Osb"])
                if DEBUG_C and j == 0 and hp == 0 and qb < 2:
                    s.dma("qg", dbg_O[0:64, qb * 512:(qb + 1) * 512],
                          Osb[0:64, 0:512], reads=["Osb"])
                    s.dma("qg", dbg_O[64:65, qb * 512:(qb + 1) * 512],
                          denb[0:1, 0:512], reads=["denb"])
                ch = h // 2
                s.op("ve", lambda b, hb=hb, ch=ch, qb=qb: b.tensor_tensor(
                    out=Ych[ch][hb:hb + 64, qb * 512:(qb + 1) * 512],
                    in0=Osb[hb:hb + 64, 0:512], in1=pb[3][hb:hb + 64, 0:512],
                    op=ALU.mult),
                    reads=["Osb", "bank3"], writes=[f"Y{ch}"])

            for j in range(8):
                p = j % 2
                s.dma("qs", K2[p], k_s[j * 128:(j + 1) * 128, :],
                      writes=[f"K2_{p}"])
                s.dma("qs", Q2[p], q_s[j * 128:(j + 1) * 128, :],
                      writes=[f"Q2_{p}"])
                for hp in range(2):
                    h = 2 * j + hp
                    dst = V2[p][:, hp * NKT * 66:(hp + 1) * NKT * 66]
                    dst3 = dst.rearrange("q (kt d) -> q kt d", d=66)[:, :, 0:64]
                    s.dma("qs", dst3, v_sr[:, :, h * 64:(h + 1) * 64],
                          writes=[f"V2_{p}"])
                if DEBUG_C and j == 0:
                    s.dma("qg", dbg_V2, V2[p], reads=[f"V2_{p}"])
                for hp in range(2):
                    hb = 64 * hp
                    for kt in range(NKT):
                        NQ = min(384, T - kt * 128)
                        bk = kt % 2
                        ps = pb[bk]

                        def stmm(b, kt=kt, NQ=NQ, ps=ps, p=p, hb=hb):
                            i = b.matmul(
                                ps[:, 0:NQ],
                                K2[p][hb:hb + 64, kt * 128:(kt + 1) * 128],
                                Q2[p][hb:hb + 64, kt * 128:kt * 128 + NQ],
                                start=True, stop=False)
                            i = b.matmul(ps[:, 0:128], ident, m1,
                                         start=False, stop=(NQ <= 256))
                            if NQ > 256:
                                i = b.matmul(ps[:, 256:384], ident, m2,
                                             start=False, stop=True)
                            return i
                        s.op("te", stmm,
                             reads=[f"K2_{p}", f"Q2_{p}", "ident", "m1", "m2"],
                             writes=[f"bank{bk}"])
                        slot = kt % 7
                        s.op("se", lambda b, kt=kt, NQ=NQ, ps=ps, slot=slot:
                             b.activation(out=PT[slot][:, 0:NQ],
                                          in_=ps[:, 0:NQ], func=AF.Exp,
                                          scale=0.125),
                             reads=[f"bank{bk}"], writes=[f"PT{slot}"])
                        if DEBUG_C and j == 0 and hp == 0 and kt < 8:
                            s.dma("qg", dbg_PT[:, kt * 384:(kt + 1) * 384],
                                  PT[slot][:, 0:384], reads=[f"PT{slot}"])
                        if kt % 4 == 3:
                            qb = kt // 4
                            def gmm(b, p=p, hb=hb, qb=qb):
                                i = b.matmul(
                                    pb[2][0:4, 0:512],
                                    K2[p][hb:hb + 64, 0:4],
                                    Q2[p][hb:hb + 64, qb * 512:(qb + 1) * 512],
                                    start=True, stop=(qb > 0))
                                if qb == 0:
                                    i = b.matmul(pb[2][0:4, 0:512],
                                                 ident[0:4, 0:4], mglob,
                                                 start=False, stop=True)
                                return i
                            s.op("te", gmm,
                                 reads=[f"K2_{p}", f"Q2_{p}", "ident", "mglob"],
                                 writes=["bank2"])
                            s.op("se", lambda b: b.activation(
                                out=PTG[0:4, 0:512], in_=pb[2][0:4, 0:512],
                                func=AF.Exp, scale=0.125),
                                reads=["bank2"], writes=["PTG"])
                            emit_O(j, hp, qb)
            if DEBUG_C:
                for cc in range(NCH):
                    s.dma("qg", dbg_q[cc * 128:(cc + 1) * 128, :],
                          q_s[cc * 128:(cc + 1) * 128, :])
                    s.dma("qg", dbg_kc[cc * 128:(cc + 1) * 128, :],
                          k_s[cc * 128:(cc + 1) * 128, :])
                    s.dma("qg", dbg_vc[cc * 512:(cc + 1) * 512, :],
                          v_s[cc * 512:(cc + 1) * 512, :])
                    s.dma("qg", dbg_Y[cc * 128:(cc + 1) * 128, :],
                          Ych[cc], reads=[f"Y{cc}"])
            ykeys = [f"Y{ch}" for ch in range(NCH)]
            for it in range(NT):
                t0 = it * Tt
                for cc in range(NCH):
                    s.dma("qs", xb2[:, cc * Tt:(cc + 1) * Tt],
                          x2[cc * 128:(cc + 1) * 128, t0:t0 + Tt],
                          writes=[f"xc{cc}"])
                for oc in range(NCH):
                    ps = pb[oc % 2]

                    def pmm(b, oc=oc, ps=ps, t0=t0):
                        for dc2 in range(NCH):
                            i = b.matmul(
                                ps[:, 0:Tt],
                                wpj_b[:, dc2 * C + oc * 128:
                                      dc2 * C + oc * 128 + 128],
                                Ych[dc2][:, t0:t0 + Tt],
                                start=(dc2 == 0), stop=(dc2 == NCH - 1))
                        return i
                    s.op("te", pmm, reads=ykeys + ["wpj"],
                         writes=[f"bank{oc % 2}"])
                    s.op("ve", lambda b, oc=oc, ps=ps: b.tensor_tensor(
                        out=x3b[:, oc * Tt:(oc + 1) * Tt], in0=ps[:, 0:Tt],
                        in1=xb2[:, oc * Tt:(oc + 1) * Tt], op=ALU.add),
                        reads=[f"bank{oc % 2}", f"xc{oc}"],
                        writes=[f"x3b{oc}"])
                    s.dma("qg", x3[oc * 128:(oc + 1) * 128, t0:t0 + Tt],
                          x3b[:, oc * Tt:(oc + 1) * Tt], reads=[f"x3b{oc}"])

        def phase_D():
            St = 1024
            NT = T // St
            ar.reset()
            gw_b = ar.bf16(NCH * 8)
            glf = ar.f32(1024)
            glb = ar.bf16(1024)
            gl_tm = ar.f32(64)
            gsq = ar.f32(64)
            msq = ar.f32(8)
            lnb = ar.f32(8)
            lnc = ar.f32(8)
            rstd8 = ar.f32(8)
            l2b = ar.f32(64)
            l3b = ar.f32(64)
            mx8 = ar.f32(8)
            zb = ar.f32(64)
            eb8 = ar.f32(64)
            ec8 = ar.f32(64)
            s8 = ar.f32(8)
            t8 = ar.f32(8)
            ngrow = ar.bf16(St)
            ng8 = ar.bf16(8)
            ngT = ar.bf16(128)
            ngr = ar.bf16(St)
            cmb = [ar.f32(St), ar.f32(St)]
            dbg_trs = ar.f32(32)
            b2s = ar.f32(NCH)
            xb = ar.f32(NCH * St)
            xbf = ar.bf16(NCH * 512)
            xL = ar.bf16(NCH * St)
            stg = [ar.f32(St), ar.f32(St)]
            w1h = [ar.bf16(NCH * St) for _ in range(2)]
            w2h = ar.bf16(NCH * St)
            hbuf = ar.bf16(NCH * St)
            ybuf = ar.f32(NCH * St)
            tmp = ar.f32(St)
            tmp2 = ar.f32(St)

            s.op("ve", lambda b: b.memset(ec8[:, 0:64], 0.0), writes=["ec8"])
            s.op("ve", lambda b: b.memset(s8[:, 0:8], 0.0), writes=["s8"])
            load_weightT(gw_b, gwT, stg, 8, "gw")
            s.dma("qs", gbv, gb, writes=["gbv"])
            s.dma("qs", rmsv, rmsw, writes=["rmsv"])
            s.dma("qs", b2v, b2, writes=["b2v"])
            s.dma("qs", b1v, b1, writes=["b1v"])
            for cc in range(NCH):
                s.op("ve", lambda b, cc=cc: b.reduce_sum(
                    out=b2s[:, cc:cc + 1], in_=b2v[:, cc * 8:(cc + 1) * 8],
                    axis=AX.X),
                    reads=["b2v"], writes=["b2s"])

            for it in range(NT):
                t0 = it * St
                for cc in range(NCH):
                    s.dma("qs", xb[:, cc * St:(cc + 1) * St],
                          x3[cc * 128:(cc + 1) * 128, t0:t0 + St],
                          writes=[f"xb{cc}"])
                for sl in range(2):
                    layernorm_tile(xb, xbf, xL, tile_tag="D", xkey="xb",
                                   out_off=0, xl_stride=St, xl_off=sl * 512,
                                   xb_stride=St, xb_off=sl * 512)
                xkeys = [f"xL{cc}" for cc in range(NCH)]

                def _gating():
                  def gmm(b):
                    for sl in range(2):
                        for cc2 in range(NCH):
                            i = b.matmul(
                                pstat[0:8, sl * 512:(sl + 1) * 512],
                                gw_b[:, cc2 * 8:(cc2 + 1) * 8],
                                xL[:, cc2 * St + sl * 512:cc2 * St + (sl + 1) * 512],
                                start=(cc2 == 0), stop=(cc2 == NCH - 1))
                    return i
                  s.op("te", gmm, reads=xkeys + ["gw"], writes=["pstat"])
                  s.op("ve", lambda b: b.tensor_scalar_add(
                    out=glf[0:8, 0:St], in0=pstat[0:8, 0:St],
                    scalar1=gbv[0:8, 0:1]),
                    reads=["pstat", "gbv"], writes=["glf"])
                  s.op("ve", lambda b: b.tensor_copy(out=glb[0:8, 0:St],
                                                   in_=glf[0:8, 0:St]),
                     reads=["glf"], writes=["glb"])
                  if D_GLEVEL < 2:
                      s.op("ve", lambda b: b.memset(ngrow[0:1, 0:St], 1.0),
                           writes=["ngrow"])
                  for qs_ in (range(8) if D_GLEVEL >= 2 else []):
                    s.op("te", lambda b, qs_=qs_: b.transpose(
                        pbt[:, qs_ * 8:(qs_ + 1) * 8],
                        glb[0:8, qs_ * 128:(qs_ + 1) * 128],
                        ident[0:8, 0:8]),
                        reads=["glb", "ident"], writes=["pbt"])
                    s.op("ve", lambda b, qs_=qs_: b.tensor_copy(
                        out=gl_tm[:, qs_ * 8:(qs_ + 1) * 8],
                        in_=pbt[:, qs_ * 8:(qs_ + 1) * 8]),
                        reads=["pbt"], writes=["gl_tm"])
                  if D_GLEVEL >= 3:
                    r3 = lambda ap: ap.rearrange("p (q e) -> p q e", e=8)
                    bc = lambda ap: ap.rearrange(
                        "p (o e) -> p o e", o=1).broadcast_to([128, 8, 8])
                    s.op("se", lambda b: b.activation(
                        out=gsq[:, 0:64], in_=gl_tm[:, 0:64], func=AF.Square),
                        reads=["gl_tm"], writes=["gsq"])
                    s.op("ve", lambda b: b.tensor_reduce(
                        out=msq[:, 0:8], in_=r3(gsq[:, 0:64]), axis=AX.X,
                        op=ALU.add),
                        reads=["gsq"], writes=["msq"])
                    s.op("se", lambda b: b.activation(
                        out=lnb[:, 0:8], in_=msq[:, 0:8], func=AF.Ln,
                        scale=1.0 / 8, bias=consts[:, 1:2]),
                        reads=["msq", "consts"], writes=["lnb"])
                    s.op("gp", lambda b: b.tensor_copy(
                        out=lnc[:, 0:8], in_=lnb[:, 0:8]),
                        reads=["lnb"], writes=["lnc"])
                    s.op("se", lambda b: b.activation(
                        out=rstd8[:, 0:8], in_=lnc[:, 0:8], func=AF.Exp,
                        scale=-0.5),
                        reads=["lnc"], writes=["rstd8"])
                    s.op("ve", lambda b: b.tensor_tensor(
                        out=r3(l2b[:, 0:64]), in0=r3(gl_tm[:, 0:64]),
                        in1=bc(rstd8[:, 0:8]), op=ALU.mult),
                        reads=["gl_tm", "rstd8"], writes=["l2b"])
                    s.op("gp", lambda b: b.tensor_tensor(
                        out=r3(l3b[:, 0:64]), in0=r3(l2b[:, 0:64]),
                        in1=bc(rmsv[:, 0:8]), op=ALU.mult),
                        reads=["l2b", "rmsv"], writes=["l3b"])
                    s.op("ve", lambda b: b.tensor_reduce(
                        out=mx8[:, 0:8], in_=r3(l3b[:, 0:64]), axis=AX.X,
                        op=ALU.max),
                        reads=["l3b"], writes=["mx8"])
                    s.op("gp", lambda b: b.tensor_tensor(
                        out=r3(zb[:, 0:64]), in0=r3(l3b[:, 0:64]),
                        in1=bc(mx8[:, 0:8]), op=ALU.subtract),
                        reads=["l3b", "mx8"], writes=["zb"])
                    s.op("se", lambda b: b.activation(
                        out=eb8[:, 0:64], in_=zb[:, 0:64], func=AF.Exp,
                        scale=2.0),
                        reads=["zb"], writes=["eb8"])
                    s.op("ve", lambda b: b.tensor_scalar_min(
                        out=ec8[:, 0:64], in0=eb8[:, 0:64], scalar1=1.0),
                        reads=["eb8"], writes=["ec8"])
                    s.op("ve", lambda b: b.tensor_reduce(
                        out=s8[:, 0:8], in_=r3(ec8[:, 0:64]), axis=AX.X,
                        op=ALU.add),
                        reads=["ec8"], writes=["s8"])
                    s.op("gp", lambda b: b.tensor_scalar(
                        out=t8[:, 0:8], in0=s8[:, 0:8], scalar1=8.0,
                        scalar2=1e-6, op0=ALU.min, op1=ALU.mult),
                        reads=["s8"], writes=["t8"])
                    s.op("ve", lambda b: b.tensor_scalar(
                        out=ng8[:, 0:8], in0=t8[:, 0:8], scalar1=-1.0,
                        scalar2=1.0, op0=ALU.mult, op1=ALU.add),
                        reads=["t8"], writes=["ng8"])
                  if D_GLEVEL >= 4:
                    s.op("te", lambda b: b.transpose(
                        pbt[0:8, 0:128], ng8[:, 0:8], ident),
                        reads=["ng8", "ident"], writes=["pbt2"])
                    s.op("ve", lambda b: b.tensor_copy(
                        out=ngT[0:8, 0:128], in_=pbt[0:8, 0:128]),
                        reads=["pbt2"], writes=["ngT"])
                    for qs_ in range(8):
                        s.op("te", lambda b, qs_=qs_: b.matmul(
                            pb[4][:, 0:128],
                            sel8[0:8, qs_ * 128:(qs_ + 1) * 128],
                            ngT[0:8, 0:128], start=True, stop=True),
                            reads=["ngT", "sel8"], writes=["bank4"])
                        s.op("ve", lambda b, qs_=qs_: b.tensor_copy(
                            out=ngr[:, qs_ * 128:(qs_ + 1) * 128],
                            in_=pb[4][:, 0:128]),
                            reads=["bank4"], writes=["ngr"])
                  else:
                    s.op("ve", lambda b: b.memset(ngr, 1.0), writes=["ngr"])
                if "g" in D_PARTS:
                    _gating()
                else:
                    s.op("ve", lambda b: b.memset(ngr, 1.0), writes=["ngr"])
                if DEBUG_D and it == 0:
                    s.dma("qg", dbg_ng8, ng8[:, 0:8], reads=["ng8"])
                    s.dma("qg", dbg_gl, gl_tm[:, 0:64], reads=["gl_tm"])
                    s.dma("qg", dbg_ngr, ngr[:, 0:1024], reads=["ngr"])
                def _experts():
                  for e_ in range(8):
                      for half in range(2):
                          wbuf = w1h[(e_ * 2 + half) % 2]
                          wkey = f"w1h{(e_ * 2 + half) % 2}"
                          for cc in range(NCH):
                              pgi = cc % 2
                              s.dma("qs", stg[pgi],
                                    w1[e_, cc * 128:(cc + 1) * 128,
                                       half * 1024:(half + 1) * 1024],
                                    writes=[f"stgD{pgi}"])
                              s.op("ve" if cc % 2 else "gp",
                                   lambda b, cc=cc, pgi=pgi, wbuf=wbuf:
                                   b.tensor_copy(
                                       out=wbuf[:, cc * St:(cc + 1) * St],
                                       in_=stg[pgi]),
                                   reads=[f"stgD{pgi}"], writes=[wkey])
                          for hc in range(NCH):
                              for sl in range(2):
                                  ph = pb[(hc * 2 + sl) % 2]
                                  bk = f"bank{(hc * 2 + sl) % 2}"

                                  def hmm(b, hc=hc, sl=sl, ph=ph, wbuf=wbuf):
                                      for cc2 in range(NCH):
                                          i = b.matmul(
                                              ph[:, 0:512],
                                              wbuf[:, cc2 * St + hc * 128:
                                                   cc2 * St + hc * 128 + 128],
                                              xL[:, cc2 * St + sl * 512:
                                                 cc2 * St + (sl + 1) * 512],
                                              start=(cc2 == 0),
                                              stop=(cc2 == NCH - 1))
                                      return i
                                  s.op("te", hmm, reads=xkeys + [wkey],
                                       writes=[bk])
                                  bcol = e_ * 16 + half * 8 + hc
                                  s.op("se", lambda b, hc=hc, sl=sl, ph=ph,
                                       bcol=bcol: b.activation(
                                           out=hbuf[:, hc * St + sl * 512:
                                                    hc * St + (sl + 1) * 512],
                                           in_=ph[:, 0:512], func=AF.Gelu,
                                           bias=b1v[:, bcol:bcol + 1]),
                                       reads=[bk, "b1v"], writes=[f"hb{hc}"])
                          if DEBUG_D and it == 0 and e_ == 7 and half == 1:
                            s.dma("qg", dbg_h, hbuf[:, 0:8 * 1024],
                                  reads=[f"hb{hc}" for hc in range(NCH)])
                          for hc in range(NCH):
                              pgi = hc % 2
                              s.dma("qs", stg[pgi],
                                    w2[e_, half * 1024 + hc * 128:
                                       half * 1024 + (hc + 1) * 128, :],
                                    writes=[f"stgD{pgi}"])
                              s.op("ve" if hc % 2 else "gp",
                                   lambda b, hc=hc, pgi=pgi: b.tensor_copy(
                                       out=w2h[:, hc * St:(hc + 1) * St],
                                       in_=stg[pgi]),
                                   reads=[f"stgD{pgi}"], writes=["w2h"])
                          for dc2 in range(NCH):
                              for sl in range(2):
                                  py = pb[2 + (dc2 * 2 + sl) % 2]
                                  bk = f"bank{2 + (dc2 * 2 + sl) % 2}"

                                  def ymm(b, dc2=dc2, sl=sl, py=py):
                                      for hc2 in range(NCH):
                                          i = b.matmul(
                                              py[:, 0:512],
                                              w2h[:, hc2 * St + dc2 * 128:
                                                  hc2 * St + dc2 * 128 + 128],
                                              hbuf[:, hc2 * St + sl * 512:
                                                   hc2 * St + (sl + 1) * 512],
                                              start=(hc2 == 0),
                                              stop=(hc2 == NCH - 1))
                                      return i
                                  hkeys = [f"hb{hc2}" for hc2 in range(NCH)]
                                  s.op("te", ymm, reads=hkeys + ["w2h"],
                                       writes=[bk])
                                  ysl = ybuf[:, dc2 * St + sl * 512:
                                             dc2 * St + (sl + 1) * 512]
                                  if e_ == 0 and half == 0:
                                      s.op("ve", lambda b, ysl=ysl, py=py:
                                           b.tensor_copy(out=ysl,
                                                         in_=py[:, 0:512]),
                                           reads=[bk], writes=[f"yb{dc2}"])
                                  else:
                                      s.op("ve", lambda b, ysl=ysl, py=py:
                                           b.tensor_add(out=ysl, in0=py[:, 0:512],
                                                        in1=ysl),
                                           reads=[bk, f"yb{dc2}"],
                                           writes=[f"yb{dc2}"])
                if "e" in D_PARTS:
                    _experts()
                else:
                    for dc2 in range(NCH):
                        s.op("ve", lambda b, dc2=dc2: b.memset(ybuf[:, dc2 * St:(dc2 + 1) * St], 0.0), writes=[f"yb{dc2}"])
                if DEBUG_D and it == 0:
                    s.dma("qg", dbg_y, ybuf[:, 0:8 * 1024],
                          reads=[f"yb{cc}" for cc in range(NCH)])
                    s.dma("qg", dbg_xl, xL[:, 0:8 * 1024],
                          reads=[f"xL{cc}" for cc in range(NCH)])
                for cc in range(NCH):
                    cb = cmb[cc % 2]
                    ck = f"cmb{cc % 2}"
                    s.op("ve", lambda b, cc=cc, cb=cb: b.tensor_scalar_add(
                        out=cb[:, 0:St],
                        in0=ybuf[:, cc * St:(cc + 1) * St],
                        scalar1=b2s[:, cc:cc + 1]),
                        reads=[f"yb{cc}", "b2s"], writes=[ck])
                    s.op("ve", lambda b, cb=cb: b.tensor_mul(
                        out=cb[:, 0:St], in0=cb[:, 0:St], in1=ngr),
                        reads=[ck, "ngr"], writes=[ck])
                    s.op("ve", lambda b, cc=cc, cb=cb: b.tensor_tensor(
                        out=cb[:, 0:St], in0=cb[:, 0:St],
                        in1=xb[:, cc * St:(cc + 1) * St], op=ALU.add),
                        reads=[ck, f"xb{cc}"], writes=[ck])
                    s.dma("qg", out_fm[cc * 128:(cc + 1) * 128, t0:t0 + St],
                          cb[:, 0:St], reads=[ck])

        if "A" in phases:
            phase_A()
            s.barrier()
        if "B" in phases:
            phase_B()
            s.barrier()
        if "C" in phases:
            phase_C()
            s.barrier()
        if "D" in phases:
            phase_D()
            s.barrier()

        last = {"A": x1, "B": x2, "C": x3, "D": None}[phases[-1]]
        if last is not None:
            for cc in range(NCH):
                s.dma("qg", out_fm[cc * 128:(cc + 1) * 128, :],
                      last[cc * 128:(cc + 1) * 128, :], reads=[])
        s.finalize()

        block = stack.enter_context(nc.Block())

        @block.tensor
        def _(b):
            for f in s.prog["te"]:
                f(b)

        @block.vector
        def _(b):
            for f in s.prog["ve"]:
                f(b)

        @block.scalar
        def _(b):
            for f in s.prog["se"]:
                f(b)

        @block.gpsimd
        def _(b):
            for f in s.prog["gp"]:
                f(b)

        @block.sync
        def _(b):
            for f in s.prog["sy"]:
                f(b)

    return nc


def _host_inputs(inputs):
    f32 = np.float32
    bf = ml_dtypes.bfloat16
    g = lambda k: np.asarray(inputs[k], f32)

    def vec_fm(v):
        return np.ascontiguousarray(np.asarray(v, f32).reshape(NCH, 128).T)

    tmix = np.concatenate([vec_fm(inputs["L0_attn_time_mix_k"]),
                           vec_fm(inputs["L0_attn_time_mix_v"]),
                           vec_fm(inputs["L0_attn_time_mix_r"])], axis=1)
    common = dict(
        wkT=np.ascontiguousarray(g("L0_attn_Wk").T),
        wvT=np.ascontiguousarray(g("L0_attn_Wv").T),
        wrT=np.ascontiguousarray(g("L0_attn_Wr").T),
        woT=np.ascontiguousarray(g("L0_attn_Wo").T),
        tmix=np.ascontiguousarray(tmix),
        tdec=vec_fm(inputs["L0_attn_time_decay"]),
        tfirst=vec_fm(inputs["L0_attn_time_first"]),
        theta=g("L0_ffn_theta"), phi=g("L0_ffn_phi"),
        theta_n=g("L0_ffn_theta_noise"), phi_n=g("L0_ffn_phi_noise"),
        wattnT=np.ascontiguousarray(g("L1_attn_w_attn").T),
        wprojT=np.ascontiguousarray(g("L1_attn_w_proj").T),
        gwT=np.ascontiguousarray(g("L1_ffn_gate_w").T),
        gb=np.ascontiguousarray(g("L1_ffn_gate_b").reshape(8, 1)),
        rmsw=np.ascontiguousarray(
            np.broadcast_to(g("L1_ffn_rms_w").reshape(1, 8), (128, 8))),
        w1=g("L1_ffn_w1"), w2=g("L1_ffn_w2"),
        b1=np.ascontiguousarray(
            g("L1_ffn_b1").reshape(8, 16, 128).transpose(2, 0, 1).reshape(128, 128)),
        b2=np.ascontiguousarray(
            g("L1_ffn_b2").reshape(8, 8, 128).transpose(2, 1, 0).reshape(128, 64)),
        ident_in=np.eye(128, dtype=bf),
    )
    i = np.arange(128)[:, None]
    j = np.arange(128)[None, :]
    common["m1_in"] = np.where(j >= i, 0.0, NEG).astype(bf)
    common["m2_in"] = np.where(j >= i, NEG, 0.0).astype(bf)
    qg = np.arange(512)[None, :]
    kk = np.arange(4)[:, None]
    common["mglob_in"] = np.where(qg < 256 + kk, NEG, 0.0).astype(bf)
    sel = np.zeros((8, 8 * 128), np.float32)
    for q_ in range(8):
        sel[q_, q_ * 128:(q_ + 1) * 128] = 1.0
    common["sel8_in"] = sel.astype(bf)

    x = np.asarray(inputs["L0_attn_x"], f32)
    maps = []
    for i_core in range(8):
        m = dict(common)
        m["xT"] = np.ascontiguousarray(x[i_core].T)
        maps.append(m)
    return maps


_NC_CACHE = {}


def kernel(**inputs):
    phases = PHASES
    key = tuple(phases)
    if key not in _NC_CACHE:
        _NC_CACHE[key] = build_program(phases)
    nc = _NC_CACHE[key]
    in_maps = _host_inputs(inputs)
    res = run_bass_kernel_spmd(nc, in_maps, core_ids=list(range(8)))
    out = np.stack([np.asarray(r["out_fm"], np.float32).T
                    for r in res.results], axis=0)
    return np.ascontiguousarray(out)


# version 07
# speedup vs baseline: 1.0322x; 1.0322x over previous
import contextlib
import functools
import numpy as np
import ml_dtypes
import concourse.bass as bass
import concourse.mybir as mybir
from concourse.bass_utils import run_bass_kernel_spmd

F32 = mybir.dt.float32
BF16 = mybir.dt.bfloat16
I32 = mybir.dt.int32
AF = mybir.ActivationFunctionType
ALU = mybir.AluOpType
AX = mybir.AxisListType

C = 1024
T = 4096
NCH = 8
PI = float(np.pi)
NEG = -1.0e9

PHASES = ("A", "B", "C", "D")
DEBUG_A = False
DEBUG_B = False
DEBUG_C = False
DEBUG_D = False
D_PARTS = "gec"
D_GLEVEL = 9
SERIAL = False


class Sched:
    COMP = ("te", "ve", "se", "gp")
    QUEUES = ("qs", "qg", "qc")
    Q2ENG = {"qs": "sy", "qg": "gp", "qc": "se"}
    STREAMS = ("te", "ve", "se", "gp", "sy")

    def __init__(self, nc):
        self.nc = nc
        self.prog = {e: [] for e in self.STREAMS}
        self.sems = {}
        self.cnt = {s: 0 for s in self.COMP + self.QUEUES}
        self.clock = {e: {s: 0 for s in self.COMP + self.QUEUES}
                      for e in self.STREAMS}
        self.wr = {}
        self.rd = {}
        self.nops = 0

    def init_sems(self, stack):
        for s in self.COMP + self.QUEUES:
            self.sems[s] = stack.enter_context(self.nc.semaphore("sem_" + s))

    def _wait(self, eng, sem_name, val):
        if sem_name == eng:
            return
        if self.clock[eng][sem_name] >= val:
            return
        self.clock[eng][sem_name] = val
        sem = self.sems[sem_name]
        self.prog[eng].append(lambda b, s=sem, v=val: b.wait_ge(s, v))

    def _dep_waits(self, eng, reads, writes):
        for k in reads:
            d = self.wr.get(k)
            if d is not None:
                self._wait(eng, *d)
        for k in writes:
            d = self.wr.get(k)
            if d is not None:
                self._wait(eng, *d)
            for r in self.rd.get(k, ()):
                self._wait(eng, *r)

    def op(self, eng, emit, reads=(), writes=()):
        if SERIAL:
            self.barrier()
        self._dep_waits(eng, reads, writes)
        sem = self.sems[eng]
        self.cnt[eng] += 1
        val = self.cnt[eng]
        self.nops += 1

        def go(b, emit=emit, sem=sem):
            inst = emit(b)
            inst.then_inc(sem, 1)

        self.prog[eng].append(go)
        tok = (eng, val)
        for k in reads:
            self.rd.setdefault(k, []).append(tok)
        for k in writes:
            self.wr[k] = tok
            self.rd[k] = []
        return tok

    def dma(self, q, out, in_, reads=(), writes=()):
        eng = self.Q2ENG[q]
        self._dep_waits(eng, reads, writes)
        sem = self.sems[q]
        self.cnt[q] += 16
        val = self.cnt[q]
        self.nops += 1

        def go(b, out=out, in_=in_, sem=sem):
            b.dma_start(out=out, in_=in_).then_inc(sem, 16)

        self.prog[eng].append(go)
        tok = (q, val)
        for k in reads:
            self.rd.setdefault(k, []).append(tok)
        for k in writes:
            self.wr[k] = tok
            self.rd[k] = []
        return tok

    def barrier(self):
        cur = dict(self.cnt)
        for eng in self.STREAMS:
            for s, v in cur.items():
                if v:
                    self._wait(eng, s, v)

    def finalize(self):
        self.barrier()


class Arena:

    def __init__(self, ap_f32, words):
        self.base_ap = ap_f32
        self.words = words
        self.off = 0
        self.marks = []

    def f32(self, n):
        a = self.off
        self.off += n
        assert self.off <= self.words, f"arena overflow: {self.off} > {self.words}"
        return self.base_ap[:, a:a + n]

    def bf16(self, n):
        nw = (n + 1) // 2
        return self.f32(nw).bitcast(BF16)[:, 0:n]

    def i32(self, n):
        return self.f32(n).bitcast(I32)

    def push(self):
        self.marks.append(self.off)

    def pop(self):
        self.off = self.marks.pop()

    def reset(self, off=0):
        self.off = off


def build_program(phases=PHASES):
    nc = bass.Bass("TRN2", target_bir_lowering=False)
    dt = nc.dram_tensor

    xT = dt("xT", [C, T], F32, kind="ExternalInput").ap()
    wkT = dt("wkT", [C, C], F32, kind="ExternalInput").ap()
    wvT = dt("wvT", [C, C], F32, kind="ExternalInput").ap()
    wrT = dt("wrT", [C, C], F32, kind="ExternalInput").ap()
    woT = dt("woT", [C, C], F32, kind="ExternalInput").ap()
    tmix = dt("tmix", [128, 3 * NCH], F32, kind="ExternalInput").ap()
    tdec = dt("tdec", [128, NCH], F32, kind="ExternalInput").ap()
    tfirst = dt("tfirst", [128, NCH], F32, kind="ExternalInput").ap()
    theta = dt("theta", [C, C], F32, kind="ExternalInput").ap()
    phi = dt("phi", [C, C], F32, kind="ExternalInput").ap()
    theta_n = dt("theta_n", [C, C], F32, kind="ExternalInput").ap()
    phi_n = dt("phi_n", [C, C], F32, kind="ExternalInput").ap()
    wattnT = dt("wattnT", [C, 3 * C], F32, kind="ExternalInput").ap()
    wprojT = dt("wprojT", [C, C], F32, kind="ExternalInput").ap()
    gwT = dt("gwT", [C, 8], F32, kind="ExternalInput").ap()
    gb = dt("gb", [8, 1], F32, kind="ExternalInput").ap()
    rmsw = dt("rmsw", [128, 8], F32, kind="ExternalInput").ap()
    w1 = dt("w1", [8, C, 2048], F32, kind="ExternalInput").ap()
    b1 = dt("b1", [128, 128], F32, kind="ExternalInput").ap()
    w2 = dt("w2", [8, 2048, C], F32, kind="ExternalInput").ap()
    b2 = dt("b2", [128, 64], F32, kind="ExternalInput").ap()
    ident_in = dt("ident_in", [128, 128], BF16, kind="ExternalInput").ap()
    m1_in = dt("m1_in", [128, 128], BF16, kind="ExternalInput").ap()
    m2_in = dt("m2_in", [128, 128], BF16, kind="ExternalInput").ap()
    mglob_in = dt("mglob_in", [4, 512], BF16, kind="ExternalInput").ap()
    sel8_in = dt("sel8_in", [8, 8 * 128], BF16, kind="ExternalInput").ap()

    x1 = dt("x1", [C, T], F32, kind="Internal").ap()
    x2 = dt("x2", [C, T], F32, kind="Internal").ap()
    x3 = dt("x3", [C, T], F32, kind="Internal").ap()
    q_s = dt("q_s", [C, T], BF16, kind="Internal").ap()
    k_s = dt("k_s", [C, T], BF16, kind="Internal").ap()
    v_s = dt("v_s", [T, C], BF16, kind="Internal").ap()

    out_fm = dt("out_fm", [C, T], F32, kind="ExternalOutput").ap()
    if DEBUG_A:
        dbg_k = dt("dbg_k", [C, T], F32, kind="ExternalOutput").ap()
        dbg_v = dt("dbg_v", [C, T], F32, kind="ExternalOutput").ap()
        dbg_sr = dt("dbg_sr", [C, T], F32, kind="ExternalOutput").ap()
        dbg_xL = dt("dbg_xL", [C, T], BF16, kind="ExternalOutput").ap()
        dbg_y = dt("dbg_y", [C, T], F32, kind="ExternalOutput").ap()
        dbg_wk = dt("dbg_wk", [128, NCH * C], BF16, kind="ExternalOutput").ap()
        dbg_xk = dt("dbg_xk", [128, NCH * 512], BF16, kind="ExternalOutput").ap()
    if DEBUG_C:
        dbg_q = dt("dbg_q", [C, T], BF16, kind="ExternalOutput").ap()
        dbg_kc = dt("dbg_kc", [C, T], BF16, kind="ExternalOutput").ap()
        dbg_vc = dt("dbg_vc", [T, C], BF16, kind="ExternalOutput").ap()
        dbg_Y = dt("dbg_Y", [C, T], BF16, kind="ExternalOutput").ap()
        dbg_PT = dt("dbg_PT", [128, 8 * 384], BF16, kind="ExternalOutput").ap()
        dbg_O = dt("dbg_O", [128, 2 * 512], BF16, kind="ExternalOutput").ap()
        dbg_V2 = dt("dbg_V2", [128, 2 * 32 * 66], BF16, kind="ExternalOutput").ap()
    if DEBUG_D:
        dbg_gl = dt("dbg_gl", [128, 64], F32, kind="ExternalOutput").ap()
        dbg_ng8 = dt("dbg_ng8", [128, 8], BF16, kind="ExternalOutput").ap()
        dbg_ngr = dt("dbg_ngr", [128, 1024], BF16, kind="ExternalOutput").ap()
        dbg_tr = dt("dbg_tr", [128, 32], F32, kind="ExternalOutput").ap()
        dbg_y = dt("dbg_yD", [128, 8 * 1024], F32, kind="ExternalOutput").ap()
        dbg_h = dt("dbg_hD", [128, 8 * 1024], BF16, kind="ExternalOutput").ap()
        dbg_xl = dt("dbg_xlD", [128, 8 * 1024], BF16, kind="ExternalOutput").ap()
    if DEBUG_B:
        dbg_Ab = dt("dbg_Ab", [128, NCH * C], BF16, kind="ExternalOutput").ap()
        dbg_Bb = dt("dbg_Bb", [128, NCH * C], BF16, kind="ExternalOutput").ap()
        dbg_xLB = dt("dbg_xLB", [C, T], BF16, kind="ExternalOutput").ap()

    stack = contextlib.ExitStack()
    with stack:
        s = Sched(nc)
        s.init_sems(stack)
        sb = lambda name, shape, dtype: stack.enter_context(
            nc.sbuf_tensor(name, shape, dtype)).ap()
        pst = lambda name, shape, dtype=F32: stack.enter_context(
            nc.psum_tensor(name, shape, dtype)).ap()

        ones_col = sb("ones_col", [128, 1], BF16)
        ones_row = sb("ones_row", [1, 128], BF16)
        ident = sb("ident", [128, 128], BF16)
        m1 = sb("m1", [128, 128], BF16)
        m2 = sb("m2", [128, 128], BF16)
        mglob = sb("mglob", [4, 512], BF16)
        sel8 = sb("sel8", [8, 8 * 128], BF16)
        vec = sb("vec", [128, 8 * NCH], F32)
        gbv = sb("gbv", [8, 1], F32)
        rmsv = sb("rmsv", [128, 8], F32)
        b1v = sb("b1v", [128, 128], F32)
        b2v = sb("b2v", [128, 64], F32)
        m_row = sb("m_row", [1, 512], F32)
        v_row = sb("v_row", [1, 512], F32)
        abrow = sb("abrow", [1, 512], BF16)
        bbrow = sb("bbrow", [1, 512], BF16)
        carry = sb("carry", [128, 3 * NCH], F32)
        consts = sb("consts", [128, 4], F32)

        AR_WORDS = 50000
        ar_t = sb("arena", [128, AR_WORDS], F32)
        ar = Arena(ar_t, AR_WORDS)

        pb = [pst(f"pb{i}", [128, 512]) for i in range(5)]
        pstat = pst("pstat", [64, 1024])
        pbt = pst("pbt", [128, 512], BF16)

        s.op("ve", lambda b: b.memset(ones_col, 1.0), writes=["ones_col"])
        s.op("ve", lambda b: b.memset(consts[:, 0:1], 1e-5), writes=["consts"])
        s.op("ve", lambda b: b.memset(consts[:, 1:2], float(np.finfo(np.float32).eps)),
             writes=["consts"])
        s.op("ve", lambda b: b.memset(ones_row, 1.0), writes=["ones_row"])
        s.dma("qs", ident, ident_in, writes=["ident"])
        s.dma("qs", m1, m1_in, writes=["m1"])
        s.dma("qs", m2, m2_in, writes=["m2"])
        s.dma("qs", mglob, mglob_in, writes=["mglob"])
        s.dma("qs", sel8, sel8_in, writes=["sel8"])
        s.dma("qs", vec[:, 0:3 * NCH], tmix, writes=["vec_tm"])
        s.dma("qs", vec[:, 3 * NCH:4 * NCH], tdec, writes=["vec_w"])
        s.dma("qs", vec[:, 4 * NCH:5 * NCH], tfirst, writes=["vec_u"])
        s.op("se", lambda b: b.activation(out=vec[:, 3 * NCH:4 * NCH],
                                          in_=vec[:, 3 * NCH:4 * NCH], func=AF.Exp),
             reads=["vec_w"], writes=["vec_w"])
        s.op("ve", lambda b: b.tensor_scalar_mul(out=vec[:, 3 * NCH:4 * NCH],
                                                 in0=vec[:, 3 * NCH:4 * NCH],
                                                 scalar1=-1.0),
             reads=["vec_w"], writes=["vec_w"])
        tmk = lambda cc: vec[:, cc:cc + 1]
        tmv = lambda cc: vec[:, NCH + cc:NCH + cc + 1]
        tmr = lambda cc: vec[:, 2 * NCH + cc:2 * NCH + cc + 1]
        wv_ = lambda cc: vec[:, 3 * NCH + cc:3 * NCH + cc + 1]
        uv_ = lambda cc: vec[:, 4 * NCH + cc:4 * NCH + cc + 1]

        def load_weightT(dst_bf, src_dram, stage, ncols, tag):
            for cc in range(NCH):
                p = cc % 2
                st_ap = stage[p]
                s.dma("qs", st_ap[:, 0:ncols],
                      src_dram[cc * 128:(cc + 1) * 128, 0:ncols],
                      writes=[f"stg{p}"])
                s.op("ve" if cc % 2 == 0 else "gp",
                     lambda b, st=st_ap, cc=cc: b.tensor_copy(
                         out=dst_bf[:, cc * ncols:(cc + 1) * ncols],
                         in_=st[:, 0:ncols]),
                     reads=[f"stg{p}"], writes=[f"{tag}"])

        def ln_prefix(xb, xbf, Tt=512, xkey="xb", xb_stride=None, xb_off=0):
            if xb_stride is None:
                xb_stride = Tt
            xbs = lambda cc: xb[:, cc * xb_stride + xb_off:
                                cc * xb_stride + xb_off + Tt]
            for cc in range(NCH):
                s.op("ve", lambda b, cc=cc: b.tensor_copy(
                    out=xbf[:, cc * Tt:(cc + 1) * Tt], in_=xbs(cc)),
                    reads=[xkey + str(cc)], writes=[f"xbf{cc}"])

            def sum_mm(b):
                for cc in range(NCH):
                    i = b.matmul(pstat[0:1, 0:Tt], ones_col,
                                 xbf[:, cc * Tt:(cc + 1) * Tt],
                                 start=(cc == 0), stop=(cc == NCH - 1))
                return i
            s.op("te", sum_mm,
                 reads=[f"xbf{cc}" for cc in range(NCH)] + ["ones_col"],
                 writes=["pstat"])
            for cc in range(NCH):
                s.op("se", lambda b, cc=cc: b.activation(
                    out=xbf[:, cc * Tt:(cc + 1) * Tt], in_=xbs(cc),
                    func=AF.Square),
                    reads=[xkey + str(cc)], writes=[f"xbf{cc}"])

            def sq_mm(b):
                for cc in range(NCH):
                    i = b.matmul(pstat[0:1, 512:512 + Tt], ones_col,
                                 xbf[:, cc * Tt:(cc + 1) * Tt],
                                 start=(cc == 0), stop=(cc == NCH - 1))
                return i
            s.op("te", sq_mm,
                 reads=[f"xbf{cc}" for cc in range(NCH)] + ["ones_col"],
                 writes=["pstat"])
            s.op("ve", lambda b: b.tensor_scalar_mul(
                out=m_row[0:1, 0:Tt], in0=pstat[0:1, 0:Tt],
                scalar1=1.0 / C),
                reads=["pstat"], writes=["rowbuf"])
            s.op("ve", lambda b: b.scalar_tensor_tensor(
                out=v_row[0:1, 0:Tt], in0=m_row[0:1, 0:Tt],
                scalar=-1.0, in1=m_row[0:1, 0:Tt], op0=ALU.mult,
                op1=ALU.mult),
                reads=["rowbuf"], writes=["rowbuf"])
            s.op("ve", lambda b: b.scalar_tensor_tensor(
                out=v_row[0:1, 0:Tt], in0=pstat[0:1, 512:512 + Tt],
                scalar=1.0 / C, in1=v_row[0:1, 0:Tt], op0=ALU.mult,
                op1=ALU.add),
                reads=["rowbuf", "pstat"], writes=["rowbuf"])
            s.op("se", lambda b: b.activation(
                out=v_row[0:1, 0:Tt], in_=v_row[0:1, 0:Tt], func=AF.Ln,
                bias=consts[0:1, 0:1]),
                reads=["rowbuf", "consts"], writes=["rowbuf"])
            s.op("se", lambda b: b.activation(
                out=v_row[0:1, 0:Tt], in_=v_row[0:1, 0:Tt], func=AF.Exp,
                scale=-0.5),
                reads=["rowbuf"], writes=["rowbuf"])
            s.op("ve", lambda b: b.tensor_copy(
                out=abrow[0:1, 0:Tt], in_=v_row[0:1, 0:Tt]),
                reads=["rowbuf"], writes=["abrow"])
            s.op("ve", lambda b: b.scalar_tensor_tensor(
                out=bbrow[0:1, 0:Tt], in0=m_row[0:1, 0:Tt], scalar=-1.0,
                in1=v_row[0:1, 0:Tt], op0=ALU.mult, op1=ALU.mult),
                reads=["rowbuf"], writes=["abrow"])

        def ln_apply(xb, xbf, xLd, Tt=512, xkey="xb", xl_stride=None,
                     xl_off=0, xb_stride=None, xb_off=0):
            if xl_stride is None:
                xl_stride = Tt
            if xb_stride is None:
                xb_stride = Tt
            xbs = lambda cc: xb[:, cc * xb_stride + xb_off:
                                cc * xb_stride + xb_off + Tt]
            s.op("te", lambda b: b.matmul(pb[2][:, 0:Tt], ones_row[0:1, :],
                                          abrow[0:1, 0:Tt], start=True,
                                          stop=True),
                 reads=["abrow", "ones_row"], writes=["bank2"])
            s.op("te", lambda b: b.matmul(pb[3][:, 0:Tt], ones_row[0:1, :],
                                          bbrow[0:1, 0:Tt], start=True,
                                          stop=True),
                 reads=["abrow", "ones_row"], writes=["bank3"])
            for cc in range(NCH):
                s.op("ve", lambda b, cc=cc: b.tensor_tensor(
                    out=xbf[:, cc * Tt:(cc + 1) * Tt],
                    in0=xbs(cc), in1=pb[2][:, 0:Tt],
                    op=ALU.mult),
                    reads=[xkey + str(cc), "bank2"], writes=[f"xbf{cc}"])
                s.op("ve", lambda b, cc=cc: b.tensor_tensor(
                    out=xLd[:, cc * xl_stride + xl_off:
                            cc * xl_stride + xl_off + Tt],
                    in0=xbf[:, cc * Tt:(cc + 1) * Tt], in1=pb[3][:, 0:Tt],
                    op=ALU.add),
                    reads=[f"xbf{cc}", "bank3"], writes=[f"xL{cc}"])

        def layernorm_tile(xb, xbf, xLd, tile_tag, Tt=512, xkey="xb",
                           out_off=1, se_sq=True, xl_stride=None, xl_off=None,
                           xb_stride=None, xb_off=0):
            if xl_stride is None:
                xl_stride = Tt + out_off
            if xl_off is None:
                xl_off = out_off
            ln_prefix(xb, xbf, Tt=Tt, xkey=xkey, xb_stride=xb_stride,
                      xb_off=xb_off)
            ln_apply(xb, xbf, xLd, Tt=Tt, xkey=xkey, xl_stride=xl_stride,
                     xl_off=xl_off, xb_stride=xb_stride, xb_off=xb_off)

        def phase_A():
            Tt = 512
            NT = T // Tt
            ar.reset()
            wk_b = ar.bf16(NCH * C)
            wv_b = ar.bf16(NCH * C)
            wr_b = ar.bf16(NCH * C)
            wo_b = ar.bf16(NCH * C)
            xbf = ar.bf16(NCH * Tt)
            xL = ar.bf16(NCH * (Tt + 1))
            dbuf = ar.bf16(Tt)
            xk = ar.bf16(NCH * Tt)
            xv = ar.bf16(NCH * Tt)
            xr = ar.bf16(NCH * Tt)
            kvb = [[ar.f32(Tt) for _ in range(3)] for _ in range(2)]
            Pb = ar.f32(Tt + 1)
            Ab = ar.f32(Tt + 1)
            Bb = ar.f32(Tt + 1)
            Db = ar.f32(Tt)
            Eb = ar.f32(Tt)
            Ub = ar.f32(Tt)
            Qb = ar.f32(Tt)
            Nb = ar.f32(Tt)
            Dn = ar.f32(Tt)
            yr = ar.bf16(NCH * Tt)
            x1b = ar.f32(NCH * Tt)
            ar.push()
            stage = [ar.f32(C), ar.f32(C)]
            load_weightT(wk_b, wkT, stage, C, "wk")
            if DEBUG_A:
                s.dma("qg", dbg_wk, wk_b, reads=["wk"])
            load_weightT(wv_b, wvT, stage, C, "wv")
            load_weightT(wr_b, wrT, stage, C, "wr")
            load_weightT(wo_b, woT, stage, C, "wo")
            s.barrier()
            ar.pop()
            xbA = [ar.f32(NCH * Tt), ar.f32(NCH * Tt)]

            for cc in range(NCH):
                s.op("ve", lambda b, cc=cc: b.memset(
                    xL[:, cc * (Tt + 1):cc * (Tt + 1) + 1], 0.0),
                    writes=[f"xL{cc}"])
            s.op("ve", lambda b: b.memset(carry[:, 0:NCH], -1e38),
                 writes=["carP"])
            s.op("ve", lambda b: b.memset(carry[:, NCH:3 * NCH], 0.0),
                 writes=["carA", "carB"])

            for cc in range(NCH):
                s.dma("qs", xbA[0][:, cc * Tt:(cc + 1) * Tt],
                      xT[cc * 128:(cc + 1) * 128, 0:Tt],
                      writes=[f"xbA0_{cc}"])
            ln_prefix(xbA[0], xbf, xkey="xbA0_")
            for it in range(NT):
                t0 = it * Tt
                p = it % 2
                xb = xbA[p]
                xk_ = f"xbA{p}_"
                ln_apply(xb, xbf, xL, xkey=xk_, xl_stride=Tt + 1, xl_off=1)
                if it + 1 < NT:
                    pn = (it + 1) % 2
                    for cc in range(NCH):
                        s.dma("qs", xbA[pn][:, cc * Tt:(cc + 1) * Tt],
                              xT[cc * 128:(cc + 1) * 128,
                                 (it + 1) * Tt:(it + 2) * Tt],
                              writes=[f"xbA{pn}_{cc}"])
                    ln_prefix(xbA[pn], xbf, xkey=f"xbA{pn}_")
                if DEBUG_A:
                    for cc in range(NCH):
                        o = cc * (Tt + 1)
                        s.dma("qg", dbg_xL[cc * 128:(cc + 1) * 128, t0:t0 + Tt],
                              xL[:, o + 1:o + 1 + Tt], reads=[f"xL{cc}"])
                for cc in range(NCH):
                    o = cc * (Tt + 1)
                    cur = xL[:, o + 1:o + 1 + Tt]
                    prv = xL[:, o:o + Tt]
                    s.op("ve", lambda b, cur=cur, prv=prv: b.tensor_sub(
                        out=dbuf, in0=cur, in1=prv),
                        reads=[f"xL{cc}"], writes=["dbuf"])
                    for dst, tmf, eng, tag in ((xk, tmk, "ve", "xk"),
                                               (xv, tmv, "ve", "xv"),
                                               (xr, tmr, "ve", "xr")):
                        s.op(eng, lambda b, dst=dst, tmf=tmf, cc=cc, prv=prv:
                             b.scalar_tensor_tensor(
                                 out=dst[:, cc * Tt:(cc + 1) * Tt], in0=dbuf,
                                 scalar=tmf(cc), in1=prv, op0=ALU.mult,
                                 op1=ALU.add),
                             reads=["dbuf", f"xL{cc}", "vec_tm"],
                             writes=[tag])
                    s.op("se", lambda b, cc=cc, o=o: b.copy(
                        out=xL[:, o:o + 1], in_=xL[:, o + Tt:o + Tt + 1]),
                        reads=[f"xL{cc}"], writes=[f"xL{cc}"])
                if DEBUG_A and it == 0:
                    s.dma("qg", dbg_xk, xk, reads=["xk"])
                for dc in range(NCH):
                    pr, pk, pv = pb[dc % 2], pb[2 + dc % 2], pb[4]
                    kb, vb, srb = kvb[dc % 2]

                    def mmgrp(b, W=None, mv=None, ps=None, dc=dc):
                        for cc2 in range(NCH):
                            i = b.matmul(
                                ps[:, 0:Tt],
                                W[:, cc2 * C + dc * 128:cc2 * C + dc * 128 + 128],
                                mv[:, cc2 * Tt:(cc2 + 1) * Tt],
                                start=(cc2 == 0), stop=(cc2 == NCH - 1))
                        return i
                    s.op("te", functools.partial(mmgrp, W=wr_b, mv=xr, ps=pr),
                         reads=["xr", "wr"], writes=[f"bank{dc % 2}"])
                    s.op("te", functools.partial(mmgrp, W=wk_b, mv=xk, ps=pk),
                         reads=["xk", "wk"], writes=[f"bank{2 + dc % 2}"])
                    s.op("te", functools.partial(mmgrp, W=wv_b, mv=xv, ps=pv),
                         reads=["xv", "wv"], writes=["bank4"])
                    s.op("se", lambda b, srb=srb, pr=pr: b.activation(
                        out=srb, in_=pr[:, 0:Tt], func=AF.Sigmoid),
                        reads=[f"bank{dc % 2}"], writes=[f"sr{dc % 2}"])
                    s.op("se", lambda b, kb=kb, pk=pk: b.copy(
                        out=kb, in_=pk[:, 0:Tt]),
                        reads=[f"bank{2 + dc % 2}"], writes=[f"k{dc % 2}"])
                    s.op("se", lambda b, vb=vb, pv=pv: b.copy(
                        out=vb, in_=pv[:, 0:Tt]),
                        reads=["bank4"], writes=[f"v{dc % 2}"])
                    kkey, vkey, skey = f"k{dc % 2}", f"v{dc % 2}", f"sr{dc % 2}"

                    s.op("se", lambda b, dc=dc: b.copy(
                        out=Pb[:, 0:1], in_=carry[:, dc:dc + 1]),
                        reads=["carP"], writes=["Pb"])
                    s.op("se", lambda b, dc=dc: b.copy(
                        out=Ab[:, 0:1], in_=carry[:, NCH + dc:NCH + dc + 1]),
                        reads=["carA"], writes=["Ab"])
                    s.op("se", lambda b, dc=dc: b.copy(
                        out=Bb[:, 0:1], in_=carry[:, 2 * NCH + dc:2 * NCH + dc + 1]),
                        reads=["carB"], writes=["Bb"])
                    s.op("ve", lambda b, kb=kb, dc=dc: b.tensor_tensor_scan(
                        out=Pb[:, 1:Tt + 1],
                        data0=wv_(dc).broadcast_to([128, Tt]), data1=kb,
                        initial=Pb[:, 0:1], op0=ALU.add, op1=ALU.max),
                        reads=[kkey, "vec_w", "Pb"], writes=["Pb"])
                    s.op("ve", lambda b, dc=dc: b.scalar_tensor_tensor(
                        out=Db, in0=Pb[:, 0:Tt], scalar=wv_(dc),
                        in1=Pb[:, 1:Tt + 1], op0=ALU.add, op1=ALU.subtract),
                        reads=["Pb", "vec_w"], writes=["Db"])
                    s.op("se", lambda b: b.activation(out=Db, in_=Db,
                                                      func=AF.Exp),
                         reads=["Db"], writes=["Db"])
                    s.op("gp", lambda b, kb=kb: b.tensor_sub(
                        out=Eb, in0=kb, in1=Pb[:, 1:Tt + 1]),
                        reads=[kkey, "Pb"], writes=["Eb"])
                    s.op("se", lambda b: b.activation(out=Eb, in_=Eb,
                                                      func=AF.Exp),
                         reads=["Eb"], writes=["Eb"])
                    s.op("gp", lambda b, vb=vb: b.tensor_mul(
                        out=Ub, in0=Eb, in1=vb),
                        reads=["Eb", vkey], writes=["Ub"])
                    s.op("ve", lambda b: b.tensor_tensor_scan(
                        out=Ab[:, 1:Tt + 1], data0=Db, data1=Ub,
                        initial=Ab[:, 0:1], op0=ALU.mult, op1=ALU.add),
                        reads=["Db", "Ub", "Ab"], writes=["Ab"])
                    s.op("ve", lambda b: b.tensor_tensor_scan(
                        out=Bb[:, 1:Tt + 1], data0=Db, data1=Eb,
                        initial=Bb[:, 0:1], op0=ALU.mult, op1=ALU.add),
                        reads=["Db", "Eb", "Bb"], writes=["Bb"])
                    s.op("se", lambda b, dc=dc: b.copy(
                        out=carry[:, dc:dc + 1], in_=Pb[:, Tt:Tt + 1]),
                        reads=["Pb"], writes=["carP"])
                    s.op("se", lambda b, dc=dc: b.copy(
                        out=carry[:, NCH + dc:NCH + dc + 1], in_=Ab[:, Tt:Tt + 1]),
                        reads=["Ab"], writes=["carA"])
                    s.op("se", lambda b, dc=dc: b.copy(
                        out=carry[:, 2 * NCH + dc:2 * NCH + dc + 1],
                        in_=Bb[:, Tt:Tt + 1]),
                        reads=["Bb"], writes=["carB"])
                    if DEBUG_A:
                        s.dma("qg", dbg_k[dc * 128:(dc + 1) * 128, t0:t0 + Tt],
                              kb, reads=[kkey])
                        s.dma("qg", dbg_v[dc * 128:(dc + 1) * 128, t0:t0 + Tt],
                              vb, reads=[vkey])
                        s.dma("qg", dbg_sr[dc * 128:(dc + 1) * 128, t0:t0 + Tt],
                              srb, reads=[skey])
                    s.op("ve", lambda b, kb=kb, dc=dc: b.scalar_tensor_tensor(
                        out=Qb, in0=kb, scalar=uv_(dc), in1=Pb[:, 0:Tt],
                        op0=ALU.add, op1=ALU.subtract),
                        reads=[kkey, "Pb", "vec_u"], writes=["Qb"])
                    s.op("se", lambda b: b.activation(out=Qb, in_=Qb,
                                                      func=AF.Exp),
                         reads=["Qb"], writes=["Qb"])
                    s.op("ve", lambda b: b.tensor_scalar_min(
                        out=Qb, in0=Qb, scalar1=1e15),
                        reads=["Qb"], writes=["Qb"])
                    s.op("gp", lambda b, vb=vb: b.tensor_mul(
                        out=Nb, in0=Qb, in1=vb),
                        reads=["Qb", vkey], writes=["Nb"])
                    s.op("gp", lambda b: b.tensor_add(
                        out=Nb, in0=Nb, in1=Ab[:, 0:Tt]),
                        reads=["Nb", "Ab"], writes=["Nb"])
                    s.op("ve", lambda b: b.tensor_add(
                        out=Dn, in0=Qb, in1=Bb[:, 0:Tt]),
                        reads=["Qb", "Bb"], writes=["Dn"])
                    s.op("se", lambda b: b.activation(out=Dn, in_=Dn,
                                                      func=AF.Ln),
                         reads=["Dn"], writes=["Dn"])
                    s.op("se", lambda b: b.activation(out=Dn, in_=Dn,
                                                      func=AF.Exp, scale=-1.0),
                         reads=["Dn"], writes=["Dn"])
                    s.op("ve", lambda b: b.tensor_mul(out=Nb, in0=Nb, in1=Dn),
                         reads=["Nb", "Dn"], writes=["Nb"])
                    s.op("ve", lambda b, srb=srb, dc=dc: b.tensor_mul(
                        out=yr[:, dc * Tt:(dc + 1) * Tt], in0=Nb, in1=srb),
                        reads=["Nb", skey], writes=["yr"])
                    if DEBUG_A:
                        s.dma("qg", dbg_y[dc * 128:(dc + 1) * 128, t0:t0 + Tt],
                              Nb, reads=["Nb"])
                for cc in range(NCH):
                    po = pb[cc % 2]

                    def omm(b, cc=cc, po=po):
                        for dc2 in range(NCH):
                            i = b.matmul(
                                po[:, 0:Tt],
                                wo_b[:, dc2 * C + cc * 128:dc2 * C + cc * 128 + 128],
                                yr[:, dc2 * Tt:(dc2 + 1) * Tt],
                                start=(dc2 == 0), stop=(dc2 == NCH - 1))
                        return i
                    s.op("te", omm, reads=["yr", "wo"], writes=[f"bank{cc % 2}"])
                    s.op("ve", lambda b, cc=cc, po=po, xb=xb: b.tensor_tensor(
                        out=x1b[:, cc * Tt:(cc + 1) * Tt],
                        in0=po[:, 0:Tt], in1=xb[:, cc * Tt:(cc + 1) * Tt],
                        op=ALU.add),
                        reads=[f"bank{cc % 2}", xk_ + str(cc)],
                        writes=[f"x1b{cc}"])
                    s.dma("qg", x1[cc * 128:(cc + 1) * 128, t0:t0 + Tt],
                          x1b[:, cc * Tt:(cc + 1) * Tt], reads=[f"x1b{cc}"])

        def phase_B():
            Tt = 512
            NT = T // Tt
            ar.reset()
            A_b = ar.bf16(NCH * C)
            B_b = ar.bf16(NCH * C)
            stg = [ar.f32(C), ar.f32(C)]
            tbuf = ar.f32(C)
            kbuf = ar.f32(C)
            kbi = kbuf.bitcast(I32)[:, 0:C]
            xbB = [ar.f32(NCH * Tt), ar.f32(NCH * Tt)]
            xbf = ar.bf16(NCH * Tt)
            xL = ar.bf16(NCH * Tt)
            ev = [ar.f32(Tt) for _ in range(4)]
            x2b = ar.f32(NCH * Tt)

            inv2pi = float(1.0 / (2 * np.pi))
            c1 = float(np.float32(2 * np.pi))
            c2 = float(2 * np.pi - np.float64(np.float32(2 * np.pi)))

            Af = ar.f32(NCH * C)
            crow = ar.f32(C)
            crow_b = ar.bf16(C)
            for dst, s0, s1, shift, tag in ((A_b, theta, theta_n, PI / 2, "Ab"),
                                            (B_b, phi, phi_n, 0.0, "Bb")):
                for cc in range(NCH):
                    s.dma("qs", stg[0], s0[cc * 128:(cc + 1) * 128, :],
                          writes=["stg0"])
                    s.dma("qs", stg[1], s1[cc * 128:(cc + 1) * 128, :],
                          writes=["stg1"])
                    s.op("ve", lambda b: b.tensor_add(
                        out=tbuf, in0=stg[0], in1=stg[1]),
                        reads=["stg0", "stg1"], writes=["tbuf"])
                    if shift:
                        s.op("ve", lambda b, shift=shift: b.tensor_scalar_add(
                            out=tbuf, in0=tbuf, scalar1=shift),
                            reads=["tbuf"], writes=["tbuf"])
                    s.op("ve", lambda b: b.tensor_scalar_mul(
                        out=kbuf, in0=tbuf, scalar1=inv2pi),
                        reads=["tbuf"], writes=["kbuf"])
                    s.op("gp", lambda b: b.tensor_copy(out=kbi, in_=kbuf),
                         reads=["kbuf"], writes=["kbuf"])
                    s.op("gp", lambda b: b.tensor_copy(out=kbuf, in_=kbi),
                         reads=["kbuf"], writes=["kbuf"])
                    s.op("ve", lambda b: b.scalar_tensor_tensor(
                        out=tbuf, in0=kbuf, scalar=-c1, in1=tbuf,
                        op0=ALU.mult, op1=ALU.add),
                        reads=["kbuf", "tbuf"], writes=["tbuf"])
                    s.op("ve", lambda b: b.scalar_tensor_tensor(
                        out=tbuf, in0=kbuf, scalar=-c2, in1=tbuf,
                        op0=ALU.mult, op1=ALU.add),
                        reads=["kbuf", "tbuf"], writes=["tbuf"])
                    s.op("se", lambda b, cc=cc: b.activation(
                        out=Af[:, cc * C:(cc + 1) * C], in_=tbuf,
                        func=AF.Sin),
                        reads=["tbuf"], writes=[f"Af{cc}"])
                    s.op("ve", lambda b, dst=dst, cc=cc: b.tensor_copy(
                        out=dst[:, cc * C:(cc + 1) * C],
                        in_=Af[:, cc * C:(cc + 1) * C]),
                        reads=[f"Af{cc}"], writes=[tag])

                def cs_mm(b, dst=dst):
                    for h2 in range(2):
                        for cc2 in range(NCH):
                            i = b.matmul(
                                pstat[0:1, h2 * 512:(h2 + 1) * 512], ones_col,
                                dst[:, cc2 * C + h2 * 512:cc2 * C + (h2 + 1) * 512],
                                start=(cc2 == 0), stop=(cc2 == NCH - 1))
                    return i
                s.op("te", cs_mm, reads=[tag, "ones_col"], writes=["pstat"])
                s.op("ve", lambda b: b.tensor_scalar_mul(
                    out=crow[0:1, 0:C], in0=pstat[0:1, 0:C], scalar1=1.0 / C),
                    reads=["pstat"], writes=["crow"])
                s.op("ve", lambda b: b.tensor_copy(out=crow_b[0:1, 0:C],
                                                   in_=crow[0:1, 0:C]),
                     reads=["crow"], writes=["crow"])
                for half in range(2):
                    s.op("te", lambda b, half=half: b.matmul(
                        pb[half][:, 0:512], ones_row[0:1, :],
                        crow_b[0:1, half * 512:(half + 1) * 512],
                        start=True, stop=True),
                        reads=["crow", "ones_row"], writes=[f"bank{half}"])
                for cc in range(NCH):
                    for half in range(2):
                        s.op("ve", lambda b, dst=dst, cc=cc, half=half:
                             b.tensor_sub(
                                 out=dst[:, cc * C + half * 512:cc * C + (half + 1) * 512],
                                 in0=Af[:, cc * C + half * 512:cc * C + (half + 1) * 512],
                                 in1=pb[half][:, 0:512]),
                             reads=[f"Af{cc}", f"bank{half}"], writes=[tag])

            if DEBUG_B:
                s.dma("qg", dbg_Ab, A_b, reads=["Ab"])
                s.dma("qg", dbg_Bb, B_b, reads=["Bb"])
            for cc in range(NCH):
                s.dma("qs", xbB[0][:, cc * Tt:(cc + 1) * Tt],
                      x1[cc * 128:(cc + 1) * 128, 0:Tt],
                      writes=[f"xbB0_{cc}"])
            ln_prefix(xbB[0], xbf, xkey="xbB0_")
            for it in range(NT):
                t0 = it * Tt
                p = it % 2
                xb = xbB[p]
                xk_ = f"xbB{p}_"
                ln_apply(xb, xbf, xL, xkey=xk_)
                if it + 1 < NT:
                    pn = (it + 1) % 2
                    for cc in range(NCH):
                        s.dma("qs", xbB[pn][:, cc * Tt:(cc + 1) * Tt],
                              x1[cc * 128:(cc + 1) * 128,
                                 (it + 1) * Tt:(it + 2) * Tt],
                              writes=[f"xbB{pn}_{cc}"])
                    ln_prefix(xbB[pn], xbf, xkey=f"xbB{pn}_")
                if DEBUG_B:
                    for cc in range(NCH):
                        s.dma("qg", dbg_xLB[cc * 128:(cc + 1) * 128, t0:t0 + Tt],
                              xL[:, cc * Tt:(cc + 1) * Tt], reads=[f"xL{cc}"])
                for oc in range(NCH):
                    pre, pim = pb[oc % 2], pb[2 + oc % 2]
                    rb, ib = ev[2 * (oc % 2)], ev[2 * (oc % 2) + 1]

                    def mmg(b, W=None, ps=None, oc=oc):
                        for cc2 in range(NCH):
                            i = b.matmul(
                                ps[:, 0:Tt],
                                W[:, cc2 * C + oc * 128:cc2 * C + oc * 128 + 128],
                                xL[:, cc2 * Tt:(cc2 + 1) * Tt],
                                start=(cc2 == 0), stop=(cc2 == NCH - 1))
                        return i
                    xkeys = [f"xL{cc2}" for cc2 in range(NCH)]
                    s.op("te", functools.partial(mmg, W=A_b, ps=pre),
                         reads=xkeys + ["Ab"], writes=[f"bank{oc % 2}"])
                    s.op("te", functools.partial(mmg, W=B_b, ps=pim),
                         reads=xkeys + ["Bb"], writes=[f"bank{2 + oc % 2}"])
                    s.op("se", lambda b, rb=rb, pre=pre: b.activation(
                        out=rb, in_=pre[:, 0:Tt], func=AF.Square),
                        reads=[f"bank{oc % 2}"], writes=[f"ev{2 * (oc % 2)}"])
                    s.op("se", lambda b, ib=ib, pim=pim: b.activation(
                        out=ib, in_=pim[:, 0:Tt], func=AF.Square),
                        reads=[f"bank{2 + oc % 2}"], writes=[f"ev{2 * (oc % 2) + 1}"])
                    s.op("ve", lambda b, rb=rb, ib=ib: b.tensor_add(
                        out=rb, in0=rb, in1=ib),
                        reads=[f"ev{2 * (oc % 2)}", f"ev{2 * (oc % 2) + 1}"],
                        writes=[f"ev{2 * (oc % 2)}"])
                    s.op("se", lambda b, rb=rb: b.activation(
                        out=rb, in_=rb, func=AF.Sqrt),
                        reads=[f"ev{2 * (oc % 2)}"], writes=[f"ev{2 * (oc % 2)}"])
                    s.op("ve", lambda b, rb=rb, oc=oc, xb=xb: b.tensor_tensor(
                        out=x2b[:, oc * Tt:(oc + 1) * Tt], in0=rb,
                        in1=xb[:, oc * Tt:(oc + 1) * Tt], op=ALU.add),
                        reads=[f"ev{2 * (oc % 2)}", xk_ + str(oc)],
                        writes=[f"x2b{oc}"])
                    s.dma("qg", x2[oc * 128:(oc + 1) * 128, t0:t0 + Tt],
                          x2b[:, oc * Tt:(oc + 1) * Tt], reads=[f"x2b{oc}"])

        def phase_C():
            Tt = 512
            NT = T // Tt
            NKT = T // 128
            ar.reset()
            wat_b = ar.bf16(NCH * 3 * C)
            stg = [ar.f32(3 * C), ar.f32(3 * C)]
            xbC = [ar.f32(NCH * Tt), ar.f32(NCH * Tt)]
            xbf = ar.bf16(NCH * Tt)
            xL = ar.bf16(NCH * Tt)
            ev = [ar.bf16(Tt) for _ in range(4)]

            load_weightT(wat_b, wattnT, stg, 3 * C, "wat")

            for cc in range(NCH):
                s.dma("qs", xbC[0][:, cc * Tt:(cc + 1) * Tt],
                      x2[cc * 128:(cc + 1) * 128, 0:Tt],
                      writes=[f"xbC0_{cc}"])
            ln_prefix(xbC[0], xbf, xkey="xbC0_")
            for it in range(NT):
                t0 = it * Tt
                p = it % 2
                xb = xbC[p]
                ln_apply(xb, xbf, xL, xkey=f"xbC{p}_")
                if it + 1 < NT:
                    pn = (it + 1) % 2
                    for cc in range(NCH):
                        s.dma("qs", xbC[pn][:, cc * Tt:(cc + 1) * Tt],
                              x2[cc * 128:(cc + 1) * 128,
                                 (it + 1) * Tt:(it + 2) * Tt],
                              writes=[f"xbC{pn}_{cc}"])
                    ln_prefix(xbC[pn], xbf, xkey=f"xbC{pn}_")
                xkeys = [f"xL{cc}" for cc in range(NCH)]
                for which, base, scratch in (("q", 0, q_s), ("k", C, k_s)):
                    for dc in range(NCH):
                        ps = pb[dc % 2]

                        def qmm(b, base=base, dc=dc, ps=ps):
                            for cc2 in range(NCH):
                                i = b.matmul(
                                    ps[:, 0:Tt],
                                    wat_b[:, cc2 * 3 * C + base + dc * 128:
                                          cc2 * 3 * C + base + dc * 128 + 128],
                                    xL[:, cc2 * Tt:(cc2 + 1) * Tt],
                                    start=(cc2 == 0), stop=(cc2 == NCH - 1))
                            return i
                        s.op("te", qmm, reads=xkeys + ["wat"],
                             writes=[f"bank{dc % 2}"])
                        eb = ev[dc % 2]
                        s.op("se", lambda b, eb=eb, ps=ps: b.copy(
                            out=eb, in_=ps[:, 0:Tt]),
                            reads=[f"bank{dc % 2}"], writes=[f"ev{dc % 2}"])
                        s.dma("qg", scratch[dc * 128:(dc + 1) * 128, t0:t0 + Tt],
                              eb, reads=[f"ev{dc % 2}"])
                for ts_ in range(Tt // 128):
                    for dsl in range(2):
                        bk = 2 + (ts_ * 2 + dsl) % 2
                        ps = pb[bk]

                        def vmm(b, ts_=ts_, dsl=dsl, ps=ps):
                            for cc2 in range(NCH):
                                i = b.matmul(
                                    ps[:, 0:512],
                                    xL[:, cc2 * Tt + ts_ * 128:
                                       cc2 * Tt + ts_ * 128 + 128],
                                    wat_b[:, cc2 * 3 * C + 2 * C + dsl * 512:
                                          cc2 * 3 * C + 2 * C + (dsl + 1) * 512],
                                    start=(cc2 == 0), stop=(cc2 == NCH - 1))
                            return i
                        s.op("te", vmm, reads=xkeys + ["wat"],
                             writes=[f"bank{bk}"])
                        eb = ev[bk]
                        s.op("se", lambda b, eb=eb, ps=ps: b.copy(
                            out=eb, in_=ps[:, 0:512]),
                            reads=[f"bank{bk}"], writes=[f"ev{bk}"])
                        s.dma("qg",
                              v_s[t0 + ts_ * 128:t0 + (ts_ + 1) * 128,
                                  dsl * 512:(dsl + 1) * 512],
                              eb, reads=[f"ev{bk}"])
            s.barrier()
            ar.reset()
            wpj_b = ar.bf16(NCH * C)
            stg2 = [ar.f32(C), ar.f32(C)]
            Ych = [ar.bf16(T) for _ in range(NCH)]
            K2 = [ar.bf16(T) for _ in range(2)]
            Q2 = [ar.bf16(T) for _ in range(2)]
            V2 = [ar.bf16(2 * NKT * 66) for _ in range(2)]
            PT = [ar.bf16(384) for _ in range(7)]
            PTG = ar.bf16(512)
            Osb = ar.bf16(512)
            denrow = ar.f32(512)
            denb = ar.bf16(512)
            xb2 = ar.f32(NCH * Tt)
            x3b = ar.f32(NCH * Tt)

            load_weightT(wpj_b, wprojT, stg2, C, "wpj")
            for pp in range(2):
                s.op("ve", lambda b, pp=pp: b.memset(V2[pp], 1.0),
                     writes=[f"V2_{pp}"])

            v_sr = v_s.rearrange("(kt q) c -> q kt c", q=128)

            def emit_O(j, hp, qb):
                p = j % 2
                h = 2 * j + hp
                hb = 64 * hp
                pO = pb[4]
                lo_q = qb * 512
                rd_keys = [f"V2_{p}", "PTG"]
                for kt in range(max(0, 4 * qb - 2), min(NKT, 4 * qb + 4)):
                    rd_keys.append(f"PT{kt % 7}")

                def ogrp(b, p=p, hp=hp, qb=qb, pO=pO):
                    i = None
                    for blk in range(4):
                        ga = qb * 512 + blk * 128
                        gb = ga + 128
                        wr = [kt for kt in range(max(0, ga // 128 - 2),
                                                 ga // 128 + 1)]
                        for n_, kt in enumerate(wr):
                            i = b.matmul(
                                pO[0:65, blk * 128:(blk + 1) * 128],
                                V2[p][:, hp * NKT * 66 + kt * 66:
                                      hp * NKT * 66 + kt * 66 + 65],
                                PT[kt % 7][:, ga - kt * 128:gb - kt * 128],
                                start=(n_ == 0), stop=False)
                        i = b.matmul(
                            pO[0:65, blk * 128:(blk + 1) * 128],
                            V2[p][0:4, hp * NKT * 66:hp * NKT * 66 + 65],
                            PTG[0:4, blk * 128:(blk + 1) * 128],
                            start=False, stop=True)
                    return i
                s.op("te", ogrp, reads=rd_keys, writes=["bank4"])
                s.op("se", lambda b: b.activation(
                    out=denrow[0:1, 0:512], in_=pO[64:65, 0:512], func=AF.Ln),
                    reads=["bank4"], writes=["denrow"])
                s.op("se", lambda b: b.activation(
                    out=denb[0:1, 0:512], in_=denrow[0:1, 0:512], func=AF.Exp,
                    scale=-1.0),
                    reads=["denrow"], writes=["denb"])
                s.op("te", lambda b: b.matmul(
                    pb[3][:, 0:512], ones_row[0:1, 0:128], denb[0:1, 0:512],
                    start=True, stop=True),
                    reads=["denb", "ones_row"], writes=["bank3"])
                s.op("se", lambda b, hb=hb: b.copy(
                    out=Osb[hb:hb + 64, 0:512], in_=pO[0:64, 0:512]),
                    reads=["bank4"], writes=["Osb"])
                if DEBUG_C and j == 0 and hp == 0 and qb < 2:
                    s.dma("qg", dbg_O[0:64, qb * 512:(qb + 1) * 512],
                          Osb[0:64, 0:512], reads=["Osb"])
                    s.dma("qg", dbg_O[64:65, qb * 512:(qb + 1) * 512],
                          denb[0:1, 0:512], reads=["denb"])
                ch = h // 2
                s.op("ve", lambda b, hb=hb, ch=ch, qb=qb: b.tensor_tensor(
                    out=Ych[ch][hb:hb + 64, qb * 512:(qb + 1) * 512],
                    in0=Osb[hb:hb + 64, 0:512], in1=pb[3][hb:hb + 64, 0:512],
                    op=ALU.mult),
                    reads=["Osb", "bank3"], writes=[f"Y{ch}"])

            for j in range(8):
                p = j % 2
                s.dma("qs", K2[p], k_s[j * 128:(j + 1) * 128, :],
                      writes=[f"K2_{p}"])
                s.dma("qs", Q2[p], q_s[j * 128:(j + 1) * 128, :],
                      writes=[f"Q2_{p}"])
                for hp in range(2):
                    h = 2 * j + hp
                    dst = V2[p][:, hp * NKT * 66:(hp + 1) * NKT * 66]
                    dst3 = dst.rearrange("q (kt d) -> q kt d", d=66)[:, :, 0:64]
                    s.dma("qs", dst3, v_sr[:, :, h * 64:(h + 1) * 64],
                          writes=[f"V2_{p}"])
                if DEBUG_C and j == 0:
                    s.dma("qg", dbg_V2, V2[p], reads=[f"V2_{p}"])
                for hp in range(2):
                    hb = 64 * hp
                    for kt in range(NKT):
                        NQ = min(384, T - kt * 128)
                        bk = kt % 2
                        ps = pb[bk]

                        def stmm(b, kt=kt, NQ=NQ, ps=ps, p=p, hb=hb):
                            i = b.matmul(
                                ps[:, 0:NQ],
                                K2[p][hb:hb + 64, kt * 128:(kt + 1) * 128],
                                Q2[p][hb:hb + 64, kt * 128:kt * 128 + NQ],
                                start=True, stop=False)
                            i = b.matmul(ps[:, 0:128], ident, m1,
                                         start=False, stop=(NQ <= 256))
                            if NQ > 256:
                                i = b.matmul(ps[:, 256:384], ident, m2,
                                             start=False, stop=True)
                            return i
                        s.op("te", stmm,
                             reads=[f"K2_{p}", f"Q2_{p}", "ident", "m1", "m2"],
                             writes=[f"bank{bk}"])
                        slot = kt % 7
                        s.op("se", lambda b, kt=kt, NQ=NQ, ps=ps, slot=slot:
                             b.activation(out=PT[slot][:, 0:NQ],
                                          in_=ps[:, 0:NQ], func=AF.Exp,
                                          scale=0.125),
                             reads=[f"bank{bk}"], writes=[f"PT{slot}"])
                        if DEBUG_C and j == 0 and hp == 0 and kt < 8:
                            s.dma("qg", dbg_PT[:, kt * 384:(kt + 1) * 384],
                                  PT[slot][:, 0:384], reads=[f"PT{slot}"])
                        if kt % 4 == 3:
                            qb = kt // 4
                            def gmm(b, p=p, hb=hb, qb=qb):
                                i = b.matmul(
                                    pb[2][0:4, 0:512],
                                    K2[p][hb:hb + 64, 0:4],
                                    Q2[p][hb:hb + 64, qb * 512:(qb + 1) * 512],
                                    start=True, stop=(qb > 0))
                                if qb == 0:
                                    i = b.matmul(pb[2][0:4, 0:512],
                                                 ident[0:4, 0:4], mglob,
                                                 start=False, stop=True)
                                return i
                            s.op("te", gmm,
                                 reads=[f"K2_{p}", f"Q2_{p}", "ident", "mglob"],
                                 writes=["bank2"])
                            s.op("se", lambda b: b.activation(
                                out=PTG[0:4, 0:512], in_=pb[2][0:4, 0:512],
                                func=AF.Exp, scale=0.125),
                                reads=["bank2"], writes=["PTG"])
                            emit_O(j, hp, qb)
            if DEBUG_C:
                for cc in range(NCH):
                    s.dma("qg", dbg_q[cc * 128:(cc + 1) * 128, :],
                          q_s[cc * 128:(cc + 1) * 128, :])
                    s.dma("qg", dbg_kc[cc * 128:(cc + 1) * 128, :],
                          k_s[cc * 128:(cc + 1) * 128, :])
                    s.dma("qg", dbg_vc[cc * 512:(cc + 1) * 512, :],
                          v_s[cc * 512:(cc + 1) * 512, :])
                    s.dma("qg", dbg_Y[cc * 128:(cc + 1) * 128, :],
                          Ych[cc], reads=[f"Y{cc}"])
            ykeys = [f"Y{ch}" for ch in range(NCH)]
            for it in range(NT):
                t0 = it * Tt
                for cc in range(NCH):
                    s.dma("qs", xb2[:, cc * Tt:(cc + 1) * Tt],
                          x2[cc * 128:(cc + 1) * 128, t0:t0 + Tt],
                          writes=[f"xc{cc}"])
                for oc in range(NCH):
                    ps = pb[oc % 2]

                    def pmm(b, oc=oc, ps=ps, t0=t0):
                        for dc2 in range(NCH):
                            i = b.matmul(
                                ps[:, 0:Tt],
                                wpj_b[:, dc2 * C + oc * 128:
                                      dc2 * C + oc * 128 + 128],
                                Ych[dc2][:, t0:t0 + Tt],
                                start=(dc2 == 0), stop=(dc2 == NCH - 1))
                        return i
                    s.op("te", pmm, reads=ykeys + ["wpj"],
                         writes=[f"bank{oc % 2}"])
                    s.op("ve", lambda b, oc=oc, ps=ps: b.tensor_tensor(
                        out=x3b[:, oc * Tt:(oc + 1) * Tt], in0=ps[:, 0:Tt],
                        in1=xb2[:, oc * Tt:(oc + 1) * Tt], op=ALU.add),
                        reads=[f"bank{oc % 2}", f"xc{oc}"],
                        writes=[f"x3b{oc}"])
                    s.dma("qg", x3[oc * 128:(oc + 1) * 128, t0:t0 + Tt],
                          x3b[:, oc * Tt:(oc + 1) * Tt], reads=[f"x3b{oc}"])

        def phase_D():
            St = 1024
            NT = T // St
            ar.reset()
            gw_b = ar.bf16(NCH * 8)
            glf = ar.f32(1024)
            glb = ar.bf16(1024)
            gl_tm = ar.f32(64)
            gsq = ar.f32(64)
            msq = ar.f32(8)
            lnb = ar.f32(8)
            lnc = ar.f32(8)
            rstd8 = ar.f32(8)
            l2b = ar.f32(64)
            l3b = ar.f32(64)
            mx8 = ar.f32(8)
            zb = ar.f32(64)
            eb8 = ar.f32(64)
            ec8 = ar.f32(64)
            s8 = ar.f32(8)
            t8 = ar.f32(8)
            ng8 = ar.bf16(8)
            ngT = ar.bf16(128)
            ngr = ar.bf16(St)
            cmb = [ar.f32(St), ar.f32(St)]
            dbg_trs = ar.f32(32)
            b2s = ar.f32(NCH)
            xb = ar.f32(NCH * St)
            xbf = ar.bf16(NCH * 512)
            xL = ar.bf16(NCH * St)
            stg = [ar.f32(St), ar.f32(St)]
            w1h = [ar.bf16(NCH * St) for _ in range(2)]
            w2h = [ar.bf16(NCH * St) for _ in range(2)]
            hbuf = ar.bf16(NCH * St)
            ybuf = ar.f32(NCH * St)

            s.op("ve", lambda b: b.memset(ec8[:, 0:64], 0.0), writes=["ec8"])
            s.op("ve", lambda b: b.memset(s8[:, 0:8], 0.0), writes=["s8"])
            load_weightT(gw_b, gwT, stg, 8, "gw")
            s.dma("qs", gbv, gb, writes=["gbv"])
            s.dma("qs", rmsv, rmsw, writes=["rmsv"])
            s.dma("qs", b2v, b2, writes=["b2v"])
            s.dma("qs", b1v, b1, writes=["b1v"])
            for cc in range(NCH):
                s.op("ve", lambda b, cc=cc: b.reduce_sum(
                    out=b2s[:, cc:cc + 1], in_=b2v[:, cc * 8:(cc + 1) * 8],
                    axis=AX.X),
                    reads=["b2v"], writes=["b2s"])

            for it in range(NT):
                t0 = it * St
                for cc in range(NCH):
                    s.dma("qs", xb[:, cc * St:(cc + 1) * St],
                          x3[cc * 128:(cc + 1) * 128, t0:t0 + St],
                          writes=[f"xb{cc}"])
                for sl in range(2):
                    layernorm_tile(xb, xbf, xL, tile_tag="D", xkey="xb",
                                   out_off=0, xl_stride=St, xl_off=sl * 512,
                                   xb_stride=St, xb_off=sl * 512)
                xkeys = [f"xL{cc}" for cc in range(NCH)]

                def _gating():
                  def gmm(b):
                    for sl in range(2):
                        for cc2 in range(NCH):
                            i = b.matmul(
                                pstat[0:8, sl * 512:(sl + 1) * 512],
                                gw_b[:, cc2 * 8:(cc2 + 1) * 8],
                                xL[:, cc2 * St + sl * 512:cc2 * St + (sl + 1) * 512],
                                start=(cc2 == 0), stop=(cc2 == NCH - 1))
                    return i
                  s.op("te", gmm, reads=xkeys + ["gw"], writes=["pstat"])
                  s.op("ve", lambda b: b.tensor_scalar_add(
                    out=glf[0:8, 0:St], in0=pstat[0:8, 0:St],
                    scalar1=gbv[0:8, 0:1]),
                    reads=["pstat", "gbv"], writes=["glf"])
                  s.op("ve", lambda b: b.tensor_copy(out=glb[0:8, 0:St],
                                                   in_=glf[0:8, 0:St]),
                     reads=["glf"], writes=["glb"])
                  for qs_ in (range(8) if D_GLEVEL >= 2 else []):
                    s.op("te", lambda b, qs_=qs_: b.transpose(
                        pbt[:, qs_ * 8:(qs_ + 1) * 8],
                        glb[0:8, qs_ * 128:(qs_ + 1) * 128],
                        ident[0:8, 0:8]),
                        reads=["glb", "ident"], writes=["pbt"])
                    s.op("ve", lambda b, qs_=qs_: b.tensor_copy(
                        out=gl_tm[:, qs_ * 8:(qs_ + 1) * 8],
                        in_=pbt[:, qs_ * 8:(qs_ + 1) * 8]),
                        reads=["pbt"], writes=["gl_tm"])
                  if D_GLEVEL >= 3:
                    r3 = lambda ap: ap.rearrange("p (q e) -> p q e", e=8)
                    bc = lambda ap: ap.rearrange(
                        "p (o e) -> p o e", o=1).broadcast_to([128, 8, 8])
                    s.op("se", lambda b: b.activation(
                        out=gsq[:, 0:64], in_=gl_tm[:, 0:64], func=AF.Square),
                        reads=["gl_tm"], writes=["gsq"])
                    s.op("ve", lambda b: b.tensor_reduce(
                        out=msq[:, 0:8], in_=r3(gsq[:, 0:64]), axis=AX.X,
                        op=ALU.add),
                        reads=["gsq"], writes=["msq"])
                    s.op("se", lambda b: b.activation(
                        out=lnb[:, 0:8], in_=msq[:, 0:8], func=AF.Ln,
                        scale=1.0 / 8, bias=consts[:, 1:2]),
                        reads=["msq", "consts"], writes=["lnb"])
                    s.op("gp", lambda b: b.tensor_copy(
                        out=lnc[:, 0:8], in_=lnb[:, 0:8]),
                        reads=["lnb"], writes=["lnc"])
                    s.op("se", lambda b: b.activation(
                        out=rstd8[:, 0:8], in_=lnc[:, 0:8], func=AF.Exp,
                        scale=-0.5),
                        reads=["lnc"], writes=["rstd8"])
                    s.op("ve", lambda b: b.tensor_tensor(
                        out=r3(l2b[:, 0:64]), in0=r3(gl_tm[:, 0:64]),
                        in1=bc(rstd8[:, 0:8]), op=ALU.mult),
                        reads=["gl_tm", "rstd8"], writes=["l2b"])
                    s.op("gp", lambda b: b.tensor_tensor(
                        out=r3(l3b[:, 0:64]), in0=r3(l2b[:, 0:64]),
                        in1=bc(rmsv[:, 0:8]), op=ALU.mult),
                        reads=["l2b", "rmsv"], writes=["l3b"])
                    s.op("ve", lambda b: b.tensor_reduce(
                        out=mx8[:, 0:8], in_=r3(l3b[:, 0:64]), axis=AX.X,
                        op=ALU.max),
                        reads=["l3b"], writes=["mx8"])
                    s.op("gp", lambda b: b.tensor_tensor(
                        out=r3(zb[:, 0:64]), in0=r3(l3b[:, 0:64]),
                        in1=bc(mx8[:, 0:8]), op=ALU.subtract),
                        reads=["l3b", "mx8"], writes=["zb"])
                    s.op("se", lambda b: b.activation(
                        out=eb8[:, 0:64], in_=zb[:, 0:64], func=AF.Exp,
                        scale=2.0),
                        reads=["zb"], writes=["eb8"])
                    s.op("ve", lambda b: b.tensor_scalar_min(
                        out=ec8[:, 0:64], in0=eb8[:, 0:64], scalar1=1.0),
                        reads=["eb8"], writes=["ec8"])
                    s.op("ve", lambda b: b.tensor_reduce(
                        out=s8[:, 0:8], in_=r3(ec8[:, 0:64]), axis=AX.X,
                        op=ALU.add),
                        reads=["ec8"], writes=["s8"])
                    s.op("gp", lambda b: b.tensor_scalar(
                        out=t8[:, 0:8], in0=s8[:, 0:8], scalar1=8.0,
                        scalar2=1e-6, op0=ALU.min, op1=ALU.mult),
                        reads=["s8"], writes=["t8"])
                    s.op("ve", lambda b: b.tensor_scalar(
                        out=ng8[:, 0:8], in0=t8[:, 0:8], scalar1=-1.0,
                        scalar2=1.0, op0=ALU.mult, op1=ALU.add),
                        reads=["t8"], writes=["ng8"])
                  if D_GLEVEL >= 4:
                    s.op("te", lambda b: b.transpose(
                        pbt[0:8, 0:128], ng8[:, 0:8], ident),
                        reads=["ng8", "ident"], writes=["pbt2"])
                    s.op("ve", lambda b: b.tensor_copy(
                        out=ngT[0:8, 0:128], in_=pbt[0:8, 0:128]),
                        reads=["pbt2"], writes=["ngT"])
                    for qs_ in range(8):
                        s.op("te", lambda b, qs_=qs_: b.matmul(
                            pb[4][:, 0:128],
                            sel8[0:8, qs_ * 128:(qs_ + 1) * 128],
                            ngT[0:8, 0:128], start=True, stop=True),
                            reads=["ngT", "sel8"], writes=["bank4"])
                        s.op("ve", lambda b, qs_=qs_: b.tensor_copy(
                            out=ngr[:, qs_ * 128:(qs_ + 1) * 128],
                            in_=pb[4][:, 0:128]),
                            reads=["bank4"], writes=["ngr"])
                  else:
                    s.op("ve", lambda b: b.memset(ngr, 1.0), writes=["ngr"])
                if "g" in D_PARTS:
                    _gating()
                else:
                    s.op("ve", lambda b: b.memset(ngr, 1.0), writes=["ngr"])
                if DEBUG_D and it == 0:
                    s.dma("qg", dbg_ng8, ng8[:, 0:8], reads=["ng8"])
                    s.dma("qg", dbg_gl, gl_tm[:, 0:64], reads=["gl_tm"])
                    s.dma("qg", dbg_ngr, ngr[:, 0:1024], reads=["ngr"])
                def loadW(e_, half, slot):
                    for cc in range(NCH):
                        pgi = cc % 2
                        s.dma("qs", stg[pgi],
                              w1[e_, cc * 128:(cc + 1) * 128,
                                 half * 1024:(half + 1) * 1024],
                              writes=[f"stgD{pgi}"])
                        s.op("ve" if cc % 2 else "gp",
                             lambda b, cc=cc, pgi=pgi, slot=slot:
                             b.tensor_copy(
                                 out=w1h[slot][:, cc * St:(cc + 1) * St],
                                 in_=stg[pgi]),
                             reads=[f"stgD{pgi}"], writes=[f"w1h{slot}"])
                    for hc in range(NCH):
                        pgi = hc % 2
                        s.dma("qs", stg[pgi],
                              w2[e_, half * 1024 + hc * 128:
                                 half * 1024 + (hc + 1) * 128, :],
                              writes=[f"stgD{pgi}"])
                        s.op("ve" if hc % 2 else "gp",
                             lambda b, hc=hc, pgi=pgi, slot=slot:
                             b.tensor_copy(
                                 out=w2h[slot][:, hc * St:(hc + 1) * St],
                                 in_=stg[pgi]),
                             reads=[f"stgD{pgi}"], writes=[f"w2h{slot}"])

                def _experts():
                  if it == 0:
                      loadW(0, 0, 0)
                  for e_ in range(8):
                      for half in range(2):
                          idx = it * 16 + e_ * 2 + half
                          slot = idx % 2
                          wbuf = w1h[slot]
                          wkey = f"w1h{slot}"
                          w2b = w2h[slot]
                          w2key = f"w2h{slot}"
                          for hc in range(NCH):
                              for sl in range(2):
                                  ph = pb[(hc * 2 + sl) % 2]
                                  bk = f"bank{(hc * 2 + sl) % 2}"

                                  def hmm(b, hc=hc, sl=sl, ph=ph, wbuf=wbuf):
                                      for cc2 in range(NCH):
                                          i = b.matmul(
                                              ph[:, 0:512],
                                              wbuf[:, cc2 * St + hc * 128:
                                                   cc2 * St + hc * 128 + 128],
                                              xL[:, cc2 * St + sl * 512:
                                                 cc2 * St + (sl + 1) * 512],
                                              start=(cc2 == 0),
                                              stop=(cc2 == NCH - 1))
                                      return i
                                  s.op("te", hmm, reads=xkeys + [wkey],
                                       writes=[bk])
                                  bcol = e_ * 16 + half * 8 + hc
                                  s.op("se", lambda b, hc=hc, sl=sl, ph=ph,
                                       bcol=bcol: b.activation(
                                           out=hbuf[:, hc * St + sl * 512:
                                                    hc * St + (sl + 1) * 512],
                                           in_=ph[:, 0:512], func=AF.Gelu,
                                           bias=b1v[:, bcol:bcol + 1]),
                                       reads=[bk, "b1v"], writes=[f"hb{hc}"])
                          nidx = idx + 1
                          if nidx < NT * 16:
                              j = nidx % 16
                              loadW(j // 2, j % 2, nidx % 2)
                          for dc2 in range(NCH):
                              for sl in range(2):
                                  py = pb[2 + (dc2 * 2 + sl) % 2]
                                  bk = f"bank{2 + (dc2 * 2 + sl) % 2}"

                                  def ymm(b, dc2=dc2, sl=sl, py=py, w2b=w2b):
                                      for hc2 in range(NCH):
                                          i = b.matmul(
                                              py[:, 0:512],
                                              w2b[:, hc2 * St + dc2 * 128:
                                                  hc2 * St + dc2 * 128 + 128],
                                              hbuf[:, hc2 * St + sl * 512:
                                                   hc2 * St + (sl + 1) * 512],
                                              start=(hc2 == 0),
                                              stop=(hc2 == NCH - 1))
                                      return i
                                  hkeys = [f"hb{hc2}" for hc2 in range(NCH)]
                                  s.op("te", ymm, reads=hkeys + [w2key],
                                       writes=[bk])
                                  ysl = ybuf[:, dc2 * St + sl * 512:
                                             dc2 * St + (sl + 1) * 512]
                                  if e_ == 0 and half == 0:
                                      s.op("ve", lambda b, ysl=ysl, py=py:
                                           b.tensor_copy(out=ysl,
                                                         in_=py[:, 0:512]),
                                           reads=[bk], writes=[f"yb{dc2}"])
                                  else:
                                      s.op("ve", lambda b, ysl=ysl, py=py:
                                           b.tensor_add(out=ysl, in0=py[:, 0:512],
                                                        in1=ysl),
                                           reads=[bk, f"yb{dc2}"],
                                           writes=[f"yb{dc2}"])
                if "e" in D_PARTS:
                    _experts()
                else:
                    for dc2 in range(NCH):
                        s.op("ve", lambda b, dc2=dc2: b.memset(ybuf[:, dc2 * St:(dc2 + 1) * St], 0.0), writes=[f"yb{dc2}"])
                if DEBUG_D and it == 0:
                    s.dma("qg", dbg_y, ybuf[:, 0:8 * 1024],
                          reads=[f"yb{cc}" for cc in range(NCH)])
                    s.dma("qg", dbg_xl, xL[:, 0:8 * 1024],
                          reads=[f"xL{cc}" for cc in range(NCH)])
                for cc in range(NCH):
                    cb = cmb[cc % 2]
                    ck = f"cmb{cc % 2}"
                    s.op("ve", lambda b, cc=cc, cb=cb: b.tensor_scalar_add(
                        out=cb[:, 0:St],
                        in0=ybuf[:, cc * St:(cc + 1) * St],
                        scalar1=b2s[:, cc:cc + 1]),
                        reads=[f"yb{cc}", "b2s"], writes=[ck])
                    s.op("ve", lambda b, cb=cb: b.tensor_mul(
                        out=cb[:, 0:St], in0=cb[:, 0:St], in1=ngr),
                        reads=[ck, "ngr"], writes=[ck])
                    s.op("ve", lambda b, cc=cc, cb=cb: b.tensor_tensor(
                        out=cb[:, 0:St], in0=cb[:, 0:St],
                        in1=xb[:, cc * St:(cc + 1) * St], op=ALU.add),
                        reads=[ck, f"xb{cc}"], writes=[ck])
                    s.dma("qg", out_fm[cc * 128:(cc + 1) * 128, t0:t0 + St],
                          cb[:, 0:St], reads=[ck])

        if "A" in phases:
            phase_A()
            s.barrier()
        if "B" in phases:
            phase_B()
            s.barrier()
        if "C" in phases:
            phase_C()
            s.barrier()
        if "D" in phases:
            phase_D()
            s.barrier()

        last = {"A": x1, "B": x2, "C": x3, "D": None}[phases[-1]]
        if last is not None:
            for cc in range(NCH):
                s.dma("qg", out_fm[cc * 128:(cc + 1) * 128, :],
                      last[cc * 128:(cc + 1) * 128, :], reads=[])
        s.finalize()

        block = stack.enter_context(nc.Block())

        @block.tensor
        def _(b):
            for f in s.prog["te"]:
                f(b)

        @block.vector
        def _(b):
            for f in s.prog["ve"]:
                f(b)

        @block.scalar
        def _(b):
            for f in s.prog["se"]:
                f(b)

        @block.gpsimd
        def _(b):
            for f in s.prog["gp"]:
                f(b)

        @block.sync
        def _(b):
            for f in s.prog["sy"]:
                f(b)

    return nc


def _host_inputs(inputs):
    f32 = np.float32
    bf = ml_dtypes.bfloat16
    g = lambda k: np.asarray(inputs[k], f32)

    def vec_fm(v):
        return np.ascontiguousarray(np.asarray(v, f32).reshape(NCH, 128).T)

    tmix = np.concatenate([vec_fm(inputs["L0_attn_time_mix_k"]),
                           vec_fm(inputs["L0_attn_time_mix_v"]),
                           vec_fm(inputs["L0_attn_time_mix_r"])], axis=1)
    common = dict(
        wkT=np.ascontiguousarray(g("L0_attn_Wk").T),
        wvT=np.ascontiguousarray(g("L0_attn_Wv").T),
        wrT=np.ascontiguousarray(g("L0_attn_Wr").T),
        woT=np.ascontiguousarray(g("L0_attn_Wo").T),
        tmix=np.ascontiguousarray(tmix),
        tdec=vec_fm(inputs["L0_attn_time_decay"]),
        tfirst=vec_fm(inputs["L0_attn_time_first"]),
        theta=g("L0_ffn_theta"), phi=g("L0_ffn_phi"),
        theta_n=g("L0_ffn_theta_noise"), phi_n=g("L0_ffn_phi_noise"),
        wattnT=np.ascontiguousarray(g("L1_attn_w_attn").T),
        wprojT=np.ascontiguousarray(g("L1_attn_w_proj").T),
        gwT=np.ascontiguousarray(g("L1_ffn_gate_w").T),
        gb=np.ascontiguousarray(g("L1_ffn_gate_b").reshape(8, 1)),
        rmsw=np.ascontiguousarray(
            np.broadcast_to(g("L1_ffn_rms_w").reshape(1, 8), (128, 8))),
        w1=g("L1_ffn_w1"), w2=g("L1_ffn_w2"),
        b1=np.ascontiguousarray(
            g("L1_ffn_b1").reshape(8, 16, 128).transpose(2, 0, 1).reshape(128, 128)),
        b2=np.ascontiguousarray(
            g("L1_ffn_b2").reshape(8, 8, 128).transpose(2, 1, 0).reshape(128, 64)),
        ident_in=np.eye(128, dtype=bf),
    )
    i = np.arange(128)[:, None]
    j = np.arange(128)[None, :]
    common["m1_in"] = np.where(j >= i, 0.0, NEG).astype(bf)
    common["m2_in"] = np.where(j >= i, NEG, 0.0).astype(bf)
    qg = np.arange(512)[None, :]
    kk = np.arange(4)[:, None]
    common["mglob_in"] = np.where(qg < 256 + kk, NEG, 0.0).astype(bf)
    sel = np.zeros((8, 8 * 128), np.float32)
    for q_ in range(8):
        sel[q_, q_ * 128:(q_ + 1) * 128] = 1.0
    common["sel8_in"] = sel.astype(bf)

    x = np.asarray(inputs["L0_attn_x"], f32)
    maps = []
    for i_core in range(8):
        m = dict(common)
        m["xT"] = np.ascontiguousarray(x[i_core].T)
        maps.append(m)
    return maps


_NC_CACHE = {}


def kernel(**inputs):
    phases = PHASES
    key = tuple(phases)
    if key not in _NC_CACHE:
        _NC_CACHE[key] = build_program(phases)
    nc = _NC_CACHE[key]
    in_maps = _host_inputs(inputs)
    res = run_bass_kernel_spmd(nc, in_maps, core_ids=list(range(8)))
    out = np.stack([np.asarray(r["out_fm"], np.float32).T
                    for r in res.results], axis=0)
    return np.ascontiguousarray(out)


# version 12
# speedup vs baseline: 1.1686x; 1.1322x over previous
import contextlib
import functools
import numpy as np
import ml_dtypes
import concourse.bass as bass
import concourse.mybir as mybir
from concourse.bass_utils import run_bass_kernel_spmd

F32 = mybir.dt.float32
BF16 = mybir.dt.bfloat16
I32 = mybir.dt.int32
AF = mybir.ActivationFunctionType
ALU = mybir.AluOpType
AX = mybir.AxisListType

C = 1024
T = 4096
NCH = 8
PI = float(np.pi)
NEG = -1.0e9

PHASES = ("A", "B", "C", "D")
DEBUG_A = False
DEBUG_B = False
DEBUG_C = False
DEBUG_D = False
D_PARTS = "gec"
D_GLEVEL = 9
SERIAL = False


class Sched:
    COMP = ("te", "ve", "se", "gp")
    QUEUES = ("qs", "qg", "qc")
    Q2ENG = {"qs": "sy", "qg": "gp", "qc": "se"}
    STREAMS = ("te", "ve", "se", "gp", "sy")

    def __init__(self, nc):
        self.nc = nc
        self.prog = {e: [] for e in self.STREAMS}
        self.sems = {}
        self.cnt = {s: 0 for s in self.COMP + self.QUEUES}
        self.clock = {e: {s: 0 for s in self.COMP + self.QUEUES}
                      for e in self.STREAMS}
        self.wr = {}
        self.rd = {}
        self.nops = 0

    def init_sems(self, stack):
        for s in self.COMP + self.QUEUES:
            self.sems[s] = stack.enter_context(self.nc.semaphore("sem_" + s))

    def _wait(self, eng, sem_name, val):
        if sem_name == eng:
            return
        if self.clock[eng][sem_name] >= val:
            return
        self.clock[eng][sem_name] = val
        sem = self.sems[sem_name]
        self.prog[eng].append(lambda b, s=sem, v=val: b.wait_ge(s, v))

    def _dep_waits(self, eng, reads, writes):
        for k in reads:
            d = self.wr.get(k)
            if d is not None:
                self._wait(eng, *d)
        for k in writes:
            d = self.wr.get(k)
            if d is not None:
                self._wait(eng, *d)
            for r in self.rd.get(k, ()):
                self._wait(eng, *r)

    def op(self, eng, emit, reads=(), writes=()):
        if SERIAL:
            self.barrier()
        self._dep_waits(eng, reads, writes)
        sem = self.sems[eng]
        self.cnt[eng] += 1
        val = self.cnt[eng]
        self.nops += 1

        def go(b, emit=emit, sem=sem):
            inst = emit(b)
            inst.then_inc(sem, 1)

        self.prog[eng].append(go)
        tok = (eng, val)
        for k in reads:
            self.rd.setdefault(k, []).append(tok)
        for k in writes:
            self.wr[k] = tok
            self.rd[k] = []
        return tok

    def dma(self, q, out, in_, reads=(), writes=()):
        eng = self.Q2ENG[q]
        self._dep_waits(eng, reads, writes)
        sem = self.sems[q]
        self.cnt[q] += 16
        val = self.cnt[q]
        self.nops += 1

        def go(b, out=out, in_=in_, sem=sem):
            b.dma_start(out=out, in_=in_).then_inc(sem, 16)

        self.prog[eng].append(go)
        tok = (q, val)
        for k in reads:
            self.rd.setdefault(k, []).append(tok)
        for k in writes:
            self.wr[k] = tok
            self.rd[k] = []
        return tok

    def barrier(self):
        cur = dict(self.cnt)
        for eng in self.STREAMS:
            for s, v in cur.items():
                if v:
                    self._wait(eng, s, v)

    def finalize(self):
        self.barrier()


class Arena:

    def __init__(self, ap_f32, words):
        self.base_ap = ap_f32
        self.words = words
        self.off = 0
        self.marks = []

    def f32(self, n):
        a = self.off
        self.off += n
        assert self.off <= self.words, f"arena overflow: {self.off} > {self.words}"
        return self.base_ap[:, a:a + n]

    def bf16(self, n):
        nw = (n + 1) // 2
        return self.f32(nw).bitcast(BF16)[:, 0:n]

    def i32(self, n):
        return self.f32(n).bitcast(I32)

    def push(self):
        self.marks.append(self.off)

    def pop(self):
        self.off = self.marks.pop()

    def reset(self, off=0):
        self.off = off


def build_program(phases=PHASES):
    nc = bass.Bass("TRN2", target_bir_lowering=False)
    dt = nc.dram_tensor

    xT = dt("xT", [C, T], F32, kind="ExternalInput").ap()
    wkT = dt("wkT", [C, C], F32, kind="ExternalInput").ap()
    wvT = dt("wvT", [C, C], F32, kind="ExternalInput").ap()
    wrT = dt("wrT", [C, C], F32, kind="ExternalInput").ap()
    woT = dt("woT", [C, C], F32, kind="ExternalInput").ap()
    tmix = dt("tmix", [128, 3 * NCH], F32, kind="ExternalInput").ap()
    tdec = dt("tdec", [128, NCH], F32, kind="ExternalInput").ap()
    tfirst = dt("tfirst", [128, NCH], F32, kind="ExternalInput").ap()
    theta = dt("theta", [C, C], F32, kind="ExternalInput").ap()
    phi = dt("phi", [C, C], F32, kind="ExternalInput").ap()
    theta_n = dt("theta_n", [C, C], F32, kind="ExternalInput").ap()
    phi_n = dt("phi_n", [C, C], F32, kind="ExternalInput").ap()
    wattnT = dt("wattnT", [C, 3 * C], F32, kind="ExternalInput").ap()
    wprojT = dt("wprojT", [C, C], F32, kind="ExternalInput").ap()
    gwT = dt("gwT", [C, 8], F32, kind="ExternalInput").ap()
    gb = dt("gb", [8, 1], F32, kind="ExternalInput").ap()
    rmsw = dt("rmsw", [128, 8], F32, kind="ExternalInput").ap()
    w1 = dt("w1", [8, C, 2048], F32, kind="ExternalInput").ap()
    b1 = dt("b1", [128, 128], F32, kind="ExternalInput").ap()
    w2 = dt("w2", [8, 2048, C], F32, kind="ExternalInput").ap()
    b2 = dt("b2", [128, 64], F32, kind="ExternalInput").ap()
    ident_in = dt("ident_in", [128, 128], BF16, kind="ExternalInput").ap()
    m1_in = dt("m1_in", [128, 128], BF16, kind="ExternalInput").ap()
    m2_in = dt("m2_in", [128, 128], BF16, kind="ExternalInput").ap()
    mglob_in = dt("mglob_in", [4, 512], BF16, kind="ExternalInput").ap()
    sel8_in = dt("sel8_in", [8, 8 * 128], BF16, kind="ExternalInput").ap()

    x1 = dt("x1", [C, T], F32, kind="Internal").ap()
    x2 = dt("x2", [C, T], F32, kind="Internal").ap()
    x3 = dt("x3", [C, T], F32, kind="Internal").ap()
    q_s = dt("q_s", [C, T], BF16, kind="Internal").ap()
    k_s = dt("k_s", [C, T], BF16, kind="Internal").ap()
    v_s = dt("v_s", [T, C], BF16, kind="Internal").ap()

    out_fm = dt("out_fm", [C, T], F32, kind="ExternalOutput").ap()
    if DEBUG_A:
        dbg_k = dt("dbg_k", [C, T], F32, kind="ExternalOutput").ap()
        dbg_v = dt("dbg_v", [C, T], F32, kind="ExternalOutput").ap()
        dbg_sr = dt("dbg_sr", [C, T], F32, kind="ExternalOutput").ap()
        dbg_xL = dt("dbg_xL", [C, T], BF16, kind="ExternalOutput").ap()
        dbg_y = dt("dbg_y", [C, T], F32, kind="ExternalOutput").ap()
        dbg_wk = dt("dbg_wk", [128, NCH * C], BF16, kind="ExternalOutput").ap()
        dbg_xk = dt("dbg_xk", [128, NCH * 512], BF16, kind="ExternalOutput").ap()
    if DEBUG_C:
        dbg_q = dt("dbg_q", [C, T], BF16, kind="ExternalOutput").ap()
        dbg_kc = dt("dbg_kc", [C, T], BF16, kind="ExternalOutput").ap()
        dbg_vc = dt("dbg_vc", [T, C], BF16, kind="ExternalOutput").ap()
        dbg_Y = dt("dbg_Y", [C, T], BF16, kind="ExternalOutput").ap()
        dbg_PT = dt("dbg_PT", [128, 8 * 384], BF16, kind="ExternalOutput").ap()
        dbg_O = dt("dbg_O", [128, 2 * 512], BF16, kind="ExternalOutput").ap()
        dbg_V2 = dt("dbg_V2", [128, 2 * 32 * 66], BF16, kind="ExternalOutput").ap()
    if DEBUG_D:
        dbg_gl = dt("dbg_gl", [128, 64], F32, kind="ExternalOutput").ap()
        dbg_ng8 = dt("dbg_ng8", [128, 8], BF16, kind="ExternalOutput").ap()
        dbg_ngr = dt("dbg_ngr", [128, 1024], BF16, kind="ExternalOutput").ap()
        dbg_tr = dt("dbg_tr", [128, 32], F32, kind="ExternalOutput").ap()
        dbg_y = dt("dbg_yD", [128, 8 * 1024], F32, kind="ExternalOutput").ap()
        dbg_h = dt("dbg_hD", [128, 8 * 1024], BF16, kind="ExternalOutput").ap()
        dbg_xl = dt("dbg_xlD", [128, 8 * 1024], BF16, kind="ExternalOutput").ap()
    if DEBUG_B:
        dbg_Ab = dt("dbg_Ab", [128, NCH * C], BF16, kind="ExternalOutput").ap()
        dbg_Bb = dt("dbg_Bb", [128, NCH * C], BF16, kind="ExternalOutput").ap()
        dbg_xLB = dt("dbg_xLB", [C, T], BF16, kind="ExternalOutput").ap()

    stack = contextlib.ExitStack()
    with stack:
        s = Sched(nc)
        s.init_sems(stack)
        sb = lambda name, shape, dtype: stack.enter_context(
            nc.sbuf_tensor(name, shape, dtype)).ap()
        pst = lambda name, shape, dtype=F32: stack.enter_context(
            nc.psum_tensor(name, shape, dtype)).ap()

        ones_col = sb("ones_col", [128, 1], BF16)
        ones_row = sb("ones_row", [1, 128], BF16)
        ident = sb("ident", [128, 128], BF16)
        m1 = sb("m1", [128, 128], BF16)
        m2 = sb("m2", [128, 128], BF16)
        mglob = sb("mglob", [4, 512], BF16)
        sel8 = sb("sel8", [8, 8 * 128], BF16)
        vec = sb("vec", [128, 8 * NCH], F32)
        gbv = sb("gbv", [8, 1], F32)
        rmsv = sb("rmsv", [128, 8], F32)
        b1v = sb("b1v", [128, 128], F32)
        b2v = sb("b2v", [128, 64], F32)
        m_row = sb("m_row", [1, 512], F32)
        v_row = sb("v_row", [1, 512], F32)
        abrow = sb("abrow", [1, 512], BF16)
        bbrow = sb("bbrow", [1, 512], BF16)
        carry = sb("carry", [128, 3 * NCH], F32)
        consts = sb("consts", [128, 4], F32)

        AR_WORDS = 50000
        ar_t = sb("arena", [128, AR_WORDS], F32)
        ar = Arena(ar_t, AR_WORDS)

        pb = [pst(f"pb{i}", [128, 512]) for i in range(5)]
        pstat = pst("pstat", [64, 1024])
        pbt = pst("pbt", [128, 512], BF16)

        s.op("ve", lambda b: b.memset(ones_col, 1.0), writes=["ones_col"])
        s.op("ve", lambda b: b.memset(consts[:, 0:1], 1e-5), writes=["consts"])
        s.op("ve", lambda b: b.memset(consts[:, 1:2], float(np.finfo(np.float32).eps)),
             writes=["consts"])
        s.op("ve", lambda b: b.memset(ones_row, 1.0), writes=["ones_row"])
        s.dma("qs", ident, ident_in, writes=["ident"])
        s.dma("qs", m1, m1_in, writes=["m1"])
        s.dma("qs", m2, m2_in, writes=["m2"])
        s.dma("qs", mglob, mglob_in, writes=["mglob"])
        s.dma("qs", sel8, sel8_in, writes=["sel8"])
        s.dma("qs", vec[:, 0:3 * NCH], tmix, writes=["vec_tm"])
        s.dma("qs", vec[:, 3 * NCH:4 * NCH], tdec, writes=["vec_w"])
        s.dma("qs", vec[:, 4 * NCH:5 * NCH], tfirst, writes=["vec_u"])
        s.op("se", lambda b: b.activation(out=vec[:, 3 * NCH:4 * NCH],
                                          in_=vec[:, 3 * NCH:4 * NCH], func=AF.Exp),
             reads=["vec_w"], writes=["vec_w"])
        s.op("ve", lambda b: b.tensor_scalar_mul(out=vec[:, 3 * NCH:4 * NCH],
                                                 in0=vec[:, 3 * NCH:4 * NCH],
                                                 scalar1=-1.0),
             reads=["vec_w"], writes=["vec_w"])
        tmk = lambda cc: vec[:, cc:cc + 1]
        tmv = lambda cc: vec[:, NCH + cc:NCH + cc + 1]
        tmr = lambda cc: vec[:, 2 * NCH + cc:2 * NCH + cc + 1]
        wv_ = lambda cc: vec[:, 3 * NCH + cc:3 * NCH + cc + 1]
        uv_ = lambda cc: vec[:, 4 * NCH + cc:4 * NCH + cc + 1]

        def load_weightT(dst_bf, src_dram, stage, ncols, tag):
            for cc in range(NCH):
                p = cc % 2
                st_ap = stage[p]
                s.dma("qs", st_ap[:, 0:ncols],
                      src_dram[cc * 128:(cc + 1) * 128, 0:ncols],
                      writes=[f"stg{p}"])
                s.op("ve" if cc % 2 == 0 else "gp",
                     lambda b, st=st_ap, cc=cc: b.tensor_copy(
                         out=dst_bf[:, cc * ncols:(cc + 1) * ncols],
                         in_=st[:, 0:ncols]),
                     reads=[f"stg{p}"], writes=[f"{tag}"])

        def ln_prefix(xb, xbf, Tt=512, xkey="xb", xb_stride=None, xb_off=0,
                      xsq=None):
            if xb_stride is None:
                xb_stride = Tt
            xbs = lambda cc: xb[:, cc * xb_stride + xb_off:
                                cc * xb_stride + xb_off + Tt]
            for cc in range(NCH):
                s.op("ve", lambda b, cc=cc: b.tensor_copy(
                    out=xbf[:, cc * Tt:(cc + 1) * Tt], in_=xbs(cc)),
                    reads=[xkey + str(cc)], writes=[f"xbf{cc}"])

            def sum_mm(b):
                for cc in range(NCH):
                    i = b.matmul(pstat[0:1, 0:Tt], ones_col,
                                 xbf[:, cc * Tt:(cc + 1) * Tt],
                                 start=(cc == 0), stop=(cc == NCH - 1))
                return i
            s.op("te", sum_mm,
                 reads=[f"xbf{cc}" for cc in range(NCH)] + ["ones_col"],
                 writes=["pstat"])
            if xsq is None:
                xsq = xbf
                sqk = "xbf"
            else:
                sqk = "xsq"
            for cc in range(NCH):
                s.op("se", lambda b, cc=cc, xsq=xsq: b.activation(
                    out=xsq[:, cc * Tt:(cc + 1) * Tt], in_=xbs(cc),
                    func=AF.Square),
                    reads=[xkey + str(cc)], writes=[f"{sqk}{cc}"])

            def sq_mm(b, xsq=xsq):
                for cc in range(NCH):
                    i = b.matmul(pstat[0:1, 512:512 + Tt], ones_col,
                                 xsq[:, cc * Tt:(cc + 1) * Tt],
                                 start=(cc == 0), stop=(cc == NCH - 1))
                return i
            s.op("te", sq_mm,
                 reads=[f"{sqk}{cc}" for cc in range(NCH)] + ["ones_col"],
                 writes=["pstat"])
            s.op("ve", lambda b: b.tensor_scalar_mul(
                out=m_row[0:1, 0:Tt], in0=pstat[0:1, 0:Tt],
                scalar1=1.0 / C),
                reads=["pstat"], writes=["rowbuf"])
            s.op("ve", lambda b: b.scalar_tensor_tensor(
                out=v_row[0:1, 0:Tt], in0=m_row[0:1, 0:Tt],
                scalar=-1.0, in1=m_row[0:1, 0:Tt], op0=ALU.mult,
                op1=ALU.mult),
                reads=["rowbuf"], writes=["rowbuf"])
            s.op("ve", lambda b: b.scalar_tensor_tensor(
                out=v_row[0:1, 0:Tt], in0=pstat[0:1, 512:512 + Tt],
                scalar=1.0 / C, in1=v_row[0:1, 0:Tt], op0=ALU.mult,
                op1=ALU.add),
                reads=["rowbuf", "pstat"], writes=["rowbuf"])
            s.op("se", lambda b: b.activation(
                out=v_row[0:1, 0:Tt], in_=v_row[0:1, 0:Tt], func=AF.Ln,
                bias=consts[0:1, 0:1]),
                reads=["rowbuf", "consts"], writes=["rowbuf"])
            s.op("se", lambda b: b.activation(
                out=v_row[0:1, 0:Tt], in_=v_row[0:1, 0:Tt], func=AF.Exp,
                scale=-0.5),
                reads=["rowbuf"], writes=["rowbuf"])
            s.op("ve", lambda b: b.tensor_copy(
                out=abrow[0:1, 0:Tt], in_=v_row[0:1, 0:Tt]),
                reads=["rowbuf"], writes=["abrow"])
            s.op("ve", lambda b: b.scalar_tensor_tensor(
                out=bbrow[0:1, 0:Tt], in0=m_row[0:1, 0:Tt], scalar=-1.0,
                in1=v_row[0:1, 0:Tt], op0=ALU.mult, op1=ALU.mult),
                reads=["rowbuf"], writes=["abrow"])

        def ln_apply(xb, xbf, xLd, Tt=512, xkey="xb", xl_stride=None,
                     xl_off=0, xb_stride=None, xb_off=0, atmp=None):
            if xl_stride is None:
                xl_stride = Tt
            if xb_stride is None:
                xb_stride = Tt
            xbs = lambda cc: xb[:, cc * xb_stride + xb_off:
                                cc * xb_stride + xb_off + Tt]
            s.op("te", lambda b: b.matmul(pb[2][:, 0:Tt], ones_row[0:1, :],
                                          abrow[0:1, 0:Tt], start=True,
                                          stop=True),
                 reads=["abrow", "ones_row"], writes=["bank2"])
            s.op("te", lambda b: b.matmul(pb[3][:, 0:Tt], ones_row[0:1, :],
                                          bbrow[0:1, 0:Tt], start=True,
                                          stop=True),
                 reads=["abrow", "ones_row"], writes=["bank3"])
            for cc in range(NCH):
                if atmp is None:
                    at = xbf[:, cc * Tt:(cc + 1) * Tt]
                    atk = f"xbf{cc}"
                else:
                    at = atmp[cc % 2][:, 0:Tt]
                    atk = f"atmp{cc % 2}"
                s.op("ve", lambda b, cc=cc, at=at: b.tensor_tensor(
                    out=at, in0=xbs(cc), in1=pb[2][:, 0:Tt],
                    op=ALU.mult),
                    reads=[xkey + str(cc), "bank2"], writes=[atk])
                s.op("ve", lambda b, cc=cc, at=at: b.tensor_tensor(
                    out=xLd[:, cc * xl_stride + xl_off:
                            cc * xl_stride + xl_off + Tt],
                    in0=at, in1=pb[3][:, 0:Tt],
                    op=ALU.add),
                    reads=[atk, "bank3"], writes=[f"xL{cc}"])

        def layernorm_tile(xb, xbf, xLd, tile_tag, Tt=512, xkey="xb",
                           out_off=1, se_sq=True, xl_stride=None, xl_off=None,
                           xb_stride=None, xb_off=0):
            if xl_stride is None:
                xl_stride = Tt + out_off
            if xl_off is None:
                xl_off = out_off
            ln_prefix(xb, xbf, Tt=Tt, xkey=xkey, xb_stride=xb_stride,
                      xb_off=xb_off)
            ln_apply(xb, xbf, xLd, Tt=Tt, xkey=xkey, xl_stride=xl_stride,
                     xl_off=xl_off, xb_stride=xb_stride, xb_off=xb_off)

        def phase_A():
            Tt = 512
            NT = T // Tt
            ar.reset()
            wk_b = ar.bf16(NCH * C)
            wv_b = ar.bf16(NCH * C)
            wr_b = ar.bf16(NCH * C)
            wo_b = ar.bf16(NCH * C)
            xbf = ar.bf16(NCH * Tt)
            xL = ar.bf16(NCH * (Tt + 1))
            dbuf = ar.bf16(Tt)
            xk = ar.bf16(NCH * Tt)
            xv = ar.bf16(NCH * Tt)
            xr = ar.bf16(NCH * Tt)
            kvb = [[ar.f32(Tt) for _ in range(3)] for _ in range(2)]
            Pb = ar.f32(Tt + 1)
            Ab = ar.f32(Tt + 1)
            Bb = ar.f32(Tt + 1)
            Db = ar.f32(Tt)
            Eb = ar.f32(Tt)
            Ub = ar.f32(Tt)
            Qb = ar.f32(Tt)
            Nb = ar.f32(Tt)
            Dn = ar.f32(Tt)
            yr = ar.bf16(NCH * Tt)
            x1b = [ar.f32(Tt), ar.f32(Tt)]
            xsqA = ar.bf16(NCH * Tt)
            atmpA = [ar.f32(Tt), ar.f32(Tt)]
            ar.push()
            stage = [ar.f32(C), ar.f32(C)]
            load_weightT(wk_b, wkT, stage, C, "wk")
            if DEBUG_A:
                s.dma("qg", dbg_wk, wk_b, reads=["wk"])
            load_weightT(wv_b, wvT, stage, C, "wv")
            load_weightT(wr_b, wrT, stage, C, "wr")
            load_weightT(wo_b, woT, stage, C, "wo")
            s.barrier()
            ar.pop()
            xbA = [ar.f32(NCH * Tt), ar.f32(NCH * Tt)]

            for cc in range(NCH):
                s.op("ve", lambda b, cc=cc: b.memset(
                    xL[:, cc * (Tt + 1):cc * (Tt + 1) + 1], 0.0),
                    writes=[f"xL{cc}"])
            s.op("ve", lambda b: b.memset(carry[:, 0:NCH], -1e38),
                 writes=["carP"])
            s.op("ve", lambda b: b.memset(carry[:, NCH:3 * NCH], 0.0),
                 writes=["carA", "carB"])

            for cc in range(NCH):
                s.dma("qs", xbA[0][:, cc * Tt:(cc + 1) * Tt],
                      xT[cc * 128:(cc + 1) * 128, 0:Tt],
                      writes=[f"xbA0_{cc}"])
            ln_prefix(xbA[0], xbf, xkey="xbA0_", xsq=xsqA)
            for it in range(NT):
                t0 = it * Tt
                p = it % 2
                xb = xbA[p]
                xk_ = f"xbA{p}_"
                ln_apply(xb, xbf, xL, xkey=xk_, xl_stride=Tt + 1, xl_off=1,
                         atmp=atmpA)
                if it + 1 < NT:
                    pn = (it + 1) % 2
                    for cc in range(NCH):
                        s.dma("qs", xbA[pn][:, cc * Tt:(cc + 1) * Tt],
                              xT[cc * 128:(cc + 1) * 128,
                                 (it + 1) * Tt:(it + 2) * Tt],
                              writes=[f"xbA{pn}_{cc}"])
                    ln_prefix(xbA[pn], xbf, xkey=f"xbA{pn}_", xsq=xsqA)
                if DEBUG_A:
                    for cc in range(NCH):
                        o = cc * (Tt + 1)
                        s.dma("qg", dbg_xL[cc * 128:(cc + 1) * 128, t0:t0 + Tt],
                              xL[:, o + 1:o + 1 + Tt], reads=[f"xL{cc}"])
                for cc in range(NCH):
                    o = cc * (Tt + 1)
                    cur = xL[:, o + 1:o + 1 + Tt]
                    prv = xL[:, o:o + Tt]
                    s.op("ve", lambda b, cur=cur, prv=prv: b.tensor_sub(
                        out=dbuf, in0=cur, in1=prv),
                        reads=[f"xL{cc}"], writes=["dbuf"])
                    for dst, tmf, eng, tag in ((xk, tmk, "ve", "xk"),
                                               (xv, tmv, "ve", "xv"),
                                               (xr, tmr, "ve", "xr")):
                        s.op(eng, lambda b, dst=dst, tmf=tmf, cc=cc, prv=prv:
                             b.scalar_tensor_tensor(
                                 out=dst[:, cc * Tt:(cc + 1) * Tt], in0=dbuf,
                                 scalar=tmf(cc), in1=prv, op0=ALU.mult,
                                 op1=ALU.add),
                             reads=["dbuf", f"xL{cc}", "vec_tm"],
                             writes=[tag])
                    s.op("se", lambda b, cc=cc, o=o: b.copy(
                        out=xL[:, o:o + 1], in_=xL[:, o + Tt:o + Tt + 1]),
                        reads=[f"xL{cc}"], writes=[f"xL{cc}"])
                if DEBUG_A and it == 0:
                    s.dma("qg", dbg_xk, xk, reads=["xk"])
                for dc in range(NCH):
                    pr, pk, pv = pb[dc % 2], pb[2 + dc % 2], pb[4]
                    kb, vb, srb = kvb[dc % 2]

                    def mmgrp(b, W=None, mv=None, ps=None, dc=dc):
                        for cc2 in range(NCH):
                            i = b.matmul(
                                ps[:, 0:Tt],
                                W[:, cc2 * C + dc * 128:cc2 * C + dc * 128 + 128],
                                mv[:, cc2 * Tt:(cc2 + 1) * Tt],
                                start=(cc2 == 0), stop=(cc2 == NCH - 1))
                        return i
                    s.op("te", functools.partial(mmgrp, W=wr_b, mv=xr, ps=pr),
                         reads=["xr", "wr"], writes=[f"bank{dc % 2}"])
                    s.op("te", functools.partial(mmgrp, W=wk_b, mv=xk, ps=pk),
                         reads=["xk", "wk"], writes=[f"bank{2 + dc % 2}"])
                    s.op("te", functools.partial(mmgrp, W=wv_b, mv=xv, ps=pv),
                         reads=["xv", "wv"], writes=["bank4"])
                    s.op("se", lambda b, srb=srb, pr=pr: b.activation(
                        out=srb, in_=pr[:, 0:Tt], func=AF.Sigmoid),
                        reads=[f"bank{dc % 2}"], writes=[f"sr{dc % 2}"])
                    s.op("se", lambda b, kb=kb, pk=pk: b.copy(
                        out=kb, in_=pk[:, 0:Tt]),
                        reads=[f"bank{2 + dc % 2}"], writes=[f"k{dc % 2}"])
                    s.op("se", lambda b, vb=vb, pv=pv: b.copy(
                        out=vb, in_=pv[:, 0:Tt]),
                        reads=["bank4"], writes=[f"v{dc % 2}"])
                    kkey, vkey, skey = f"k{dc % 2}", f"v{dc % 2}", f"sr{dc % 2}"

                    s.op("se", lambda b, dc=dc: b.copy(
                        out=Pb[:, 0:1], in_=carry[:, dc:dc + 1]),
                        reads=["carP"], writes=["Pb"])
                    s.op("se", lambda b, dc=dc: b.copy(
                        out=Ab[:, 0:1], in_=carry[:, NCH + dc:NCH + dc + 1]),
                        reads=["carA"], writes=["Ab"])
                    s.op("se", lambda b, dc=dc: b.copy(
                        out=Bb[:, 0:1], in_=carry[:, 2 * NCH + dc:2 * NCH + dc + 1]),
                        reads=["carB"], writes=["Bb"])
                    s.op("ve", lambda b, kb=kb, dc=dc: b.tensor_tensor_scan(
                        out=Pb[:, 1:Tt + 1],
                        data0=wv_(dc).broadcast_to([128, Tt]), data1=kb,
                        initial=Pb[:, 0:1], op0=ALU.add, op1=ALU.max),
                        reads=[kkey, "vec_w", "Pb"], writes=["Pb"])
                    s.op("ve", lambda b, dc=dc: b.scalar_tensor_tensor(
                        out=Db, in0=Pb[:, 0:Tt], scalar=wv_(dc),
                        in1=Pb[:, 1:Tt + 1], op0=ALU.add, op1=ALU.subtract),
                        reads=["Pb", "vec_w"], writes=["Db"])
                    s.op("se", lambda b: b.activation(out=Db, in_=Db,
                                                      func=AF.Exp),
                         reads=["Db"], writes=["Db"])
                    s.op("gp", lambda b, kb=kb: b.tensor_sub(
                        out=Eb, in0=kb, in1=Pb[:, 1:Tt + 1]),
                        reads=[kkey, "Pb"], writes=["Eb"])
                    s.op("se", lambda b: b.activation(out=Eb, in_=Eb,
                                                      func=AF.Exp),
                         reads=["Eb"], writes=["Eb"])
                    s.op("gp", lambda b, vb=vb: b.tensor_mul(
                        out=Ub, in0=Eb, in1=vb),
                        reads=["Eb", vkey], writes=["Ub"])
                    s.op("ve", lambda b: b.tensor_tensor_scan(
                        out=Ab[:, 1:Tt + 1], data0=Db, data1=Ub,
                        initial=Ab[:, 0:1], op0=ALU.mult, op1=ALU.add),
                        reads=["Db", "Ub", "Ab"], writes=["Ab"])
                    s.op("ve", lambda b: b.tensor_tensor_scan(
                        out=Bb[:, 1:Tt + 1], data0=Db, data1=Eb,
                        initial=Bb[:, 0:1], op0=ALU.mult, op1=ALU.add),
                        reads=["Db", "Eb", "Bb"], writes=["Bb"])
                    s.op("se", lambda b, dc=dc: b.copy(
                        out=carry[:, dc:dc + 1], in_=Pb[:, Tt:Tt + 1]),
                        reads=["Pb"], writes=["carP"])
                    s.op("se", lambda b, dc=dc: b.copy(
                        out=carry[:, NCH + dc:NCH + dc + 1], in_=Ab[:, Tt:Tt + 1]),
                        reads=["Ab"], writes=["carA"])
                    s.op("se", lambda b, dc=dc: b.copy(
                        out=carry[:, 2 * NCH + dc:2 * NCH + dc + 1],
                        in_=Bb[:, Tt:Tt + 1]),
                        reads=["Bb"], writes=["carB"])
                    if DEBUG_A:
                        s.dma("qg", dbg_k[dc * 128:(dc + 1) * 128, t0:t0 + Tt],
                              kb, reads=[kkey])
                        s.dma("qg", dbg_v[dc * 128:(dc + 1) * 128, t0:t0 + Tt],
                              vb, reads=[vkey])
                        s.dma("qg", dbg_sr[dc * 128:(dc + 1) * 128, t0:t0 + Tt],
                              srb, reads=[skey])
                    s.op("ve", lambda b, kb=kb, dc=dc: b.scalar_tensor_tensor(
                        out=Qb, in0=kb, scalar=uv_(dc), in1=Pb[:, 0:Tt],
                        op0=ALU.add, op1=ALU.subtract),
                        reads=[kkey, "Pb", "vec_u"], writes=["Qb"])
                    s.op("se", lambda b: b.activation(out=Qb, in_=Qb,
                                                      func=AF.Exp),
                         reads=["Qb"], writes=["Qb"])
                    s.op("ve", lambda b: b.tensor_scalar_min(
                        out=Qb, in0=Qb, scalar1=1e15),
                        reads=["Qb"], writes=["Qb"])
                    s.op("gp", lambda b, vb=vb: b.tensor_mul(
                        out=Nb, in0=Qb, in1=vb),
                        reads=["Qb", vkey], writes=["Nb"])
                    s.op("gp", lambda b: b.tensor_add(
                        out=Nb, in0=Nb, in1=Ab[:, 0:Tt]),
                        reads=["Nb", "Ab"], writes=["Nb"])
                    s.op("ve", lambda b: b.tensor_add(
                        out=Dn, in0=Qb, in1=Bb[:, 0:Tt]),
                        reads=["Qb", "Bb"], writes=["Dn"])
                    s.op("se", lambda b: b.activation(out=Dn, in_=Dn,
                                                      func=AF.Ln),
                         reads=["Dn"], writes=["Dn"])
                    s.op("se", lambda b: b.activation(out=Dn, in_=Dn,
                                                      func=AF.Exp, scale=-1.0),
                         reads=["Dn"], writes=["Dn"])
                    s.op("ve", lambda b: b.tensor_mul(out=Nb, in0=Nb, in1=Dn),
                         reads=["Nb", "Dn"], writes=["Nb"])
                    s.op("ve", lambda b, srb=srb, dc=dc: b.tensor_mul(
                        out=yr[:, dc * Tt:(dc + 1) * Tt], in0=Nb, in1=srb),
                        reads=["Nb", skey], writes=["yr"])
                    if DEBUG_A:
                        s.dma("qg", dbg_y[dc * 128:(dc + 1) * 128, t0:t0 + Tt],
                              Nb, reads=["Nb"])
                for cc in range(NCH):
                    po = pb[cc % 2]

                    def omm(b, cc=cc, po=po):
                        for dc2 in range(NCH):
                            i = b.matmul(
                                po[:, 0:Tt],
                                wo_b[:, dc2 * C + cc * 128:dc2 * C + cc * 128 + 128],
                                yr[:, dc2 * Tt:(dc2 + 1) * Tt],
                                start=(dc2 == 0), stop=(dc2 == NCH - 1))
                        return i
                    s.op("te", omm, reads=["yr", "wo"], writes=[f"bank{cc % 2}"])
                    s.op("ve", lambda b, cc=cc, po=po, xb=xb: b.tensor_tensor(
                        out=x1b[cc % 2][:, 0:Tt],
                        in0=po[:, 0:Tt], in1=xb[:, cc * Tt:(cc + 1) * Tt],
                        op=ALU.add),
                        reads=[f"bank{cc % 2}", xk_ + str(cc)],
                        writes=[f"x1b{cc % 2}"])
                    s.dma("qg", x1[cc * 128:(cc + 1) * 128, t0:t0 + Tt],
                          x1b[cc % 2][:, 0:Tt], reads=[f"x1b{cc % 2}"])

        def phase_B():
            Tt = 512
            NT = T // Tt
            ar.reset()
            A_b = ar.bf16(NCH * C)
            B_b = ar.bf16(NCH * C)
            stg = [ar.f32(C), ar.f32(C)]
            tbuf = ar.f32(C)
            kbuf = ar.f32(C)
            kbi = kbuf.bitcast(I32)[:, 0:C]
            xbB = [ar.f32(NCH * Tt), ar.f32(NCH * Tt)]
            xbf = ar.bf16(NCH * Tt)
            xL = ar.bf16(NCH * Tt)
            ev = [ar.f32(Tt) for _ in range(4)]
            x2b = ar.f32(NCH * Tt)
            xsqB = ar.bf16(NCH * Tt)
            atmpB = [ar.f32(Tt), ar.f32(Tt)]

            inv2pi = float(1.0 / (2 * np.pi))
            c1 = float(np.float32(2 * np.pi))
            c2 = float(2 * np.pi - np.float64(np.float32(2 * np.pi)))

            Af = ar.f32(NCH * C)
            crow = ar.f32(C)
            crow_b = ar.bf16(C)
            for dst, s0, s1, shift, tag in ((A_b, theta, theta_n, PI / 2, "Ab"),
                                            (B_b, phi, phi_n, 0.0, "Bb")):
                for cc in range(NCH):
                    s.dma("qs", stg[0], s0[cc * 128:(cc + 1) * 128, :],
                          writes=["stg0"])
                    s.dma("qs", stg[1], s1[cc * 128:(cc + 1) * 128, :],
                          writes=["stg1"])
                    s.op("ve", lambda b: b.tensor_add(
                        out=tbuf, in0=stg[0], in1=stg[1]),
                        reads=["stg0", "stg1"], writes=["tbuf"])
                    if shift:
                        s.op("ve", lambda b, shift=shift: b.tensor_scalar_add(
                            out=tbuf, in0=tbuf, scalar1=shift),
                            reads=["tbuf"], writes=["tbuf"])
                    s.op("ve", lambda b: b.tensor_scalar_mul(
                        out=kbuf, in0=tbuf, scalar1=inv2pi),
                        reads=["tbuf"], writes=["kbuf"])
                    s.op("gp", lambda b: b.tensor_copy(out=kbi, in_=kbuf),
                         reads=["kbuf"], writes=["kbuf"])
                    s.op("gp", lambda b: b.tensor_copy(out=kbuf, in_=kbi),
                         reads=["kbuf"], writes=["kbuf"])
                    s.op("ve", lambda b: b.scalar_tensor_tensor(
                        out=tbuf, in0=kbuf, scalar=-c1, in1=tbuf,
                        op0=ALU.mult, op1=ALU.add),
                        reads=["kbuf", "tbuf"], writes=["tbuf"])
                    s.op("ve", lambda b: b.scalar_tensor_tensor(
                        out=tbuf, in0=kbuf, scalar=-c2, in1=tbuf,
                        op0=ALU.mult, op1=ALU.add),
                        reads=["kbuf", "tbuf"], writes=["tbuf"])
                    s.op("se", lambda b, cc=cc: b.activation(
                        out=Af[:, cc * C:(cc + 1) * C], in_=tbuf,
                        func=AF.Sin),
                        reads=["tbuf"], writes=[f"Af{cc}"])
                    s.op("ve", lambda b, dst=dst, cc=cc: b.tensor_copy(
                        out=dst[:, cc * C:(cc + 1) * C],
                        in_=Af[:, cc * C:(cc + 1) * C]),
                        reads=[f"Af{cc}"], writes=[tag])

                def cs_mm(b, dst=dst):
                    for h2 in range(2):
                        for cc2 in range(NCH):
                            i = b.matmul(
                                pstat[0:1, h2 * 512:(h2 + 1) * 512], ones_col,
                                dst[:, cc2 * C + h2 * 512:cc2 * C + (h2 + 1) * 512],
                                start=(cc2 == 0), stop=(cc2 == NCH - 1))
                    return i
                s.op("te", cs_mm, reads=[tag, "ones_col"], writes=["pstat"])
                s.op("ve", lambda b: b.tensor_scalar_mul(
                    out=crow[0:1, 0:C], in0=pstat[0:1, 0:C], scalar1=1.0 / C),
                    reads=["pstat"], writes=["crow"])
                s.op("ve", lambda b: b.tensor_copy(out=crow_b[0:1, 0:C],
                                                   in_=crow[0:1, 0:C]),
                     reads=["crow"], writes=["crow"])
                for half in range(2):
                    s.op("te", lambda b, half=half: b.matmul(
                        pb[half][:, 0:512], ones_row[0:1, :],
                        crow_b[0:1, half * 512:(half + 1) * 512],
                        start=True, stop=True),
                        reads=["crow", "ones_row"], writes=[f"bank{half}"])
                for cc in range(NCH):
                    for half in range(2):
                        s.op("ve", lambda b, dst=dst, cc=cc, half=half:
                             b.tensor_sub(
                                 out=dst[:, cc * C + half * 512:cc * C + (half + 1) * 512],
                                 in0=Af[:, cc * C + half * 512:cc * C + (half + 1) * 512],
                                 in1=pb[half][:, 0:512]),
                             reads=[f"Af{cc}", f"bank{half}"], writes=[tag])

            if DEBUG_B:
                s.dma("qg", dbg_Ab, A_b, reads=["Ab"])
                s.dma("qg", dbg_Bb, B_b, reads=["Bb"])
            for cc in range(NCH):
                s.dma("qs", xbB[0][:, cc * Tt:(cc + 1) * Tt],
                      x1[cc * 128:(cc + 1) * 128, 0:Tt],
                      writes=[f"xbB0_{cc}"])
            ln_prefix(xbB[0], xbf, xkey="xbB0_", xsq=xsqB)
            for it in range(NT):
                t0 = it * Tt
                p = it % 2
                xb = xbB[p]
                xk_ = f"xbB{p}_"
                ln_apply(xb, xbf, xL, xkey=xk_, atmp=atmpB)
                if DEBUG_B:
                    for cc in range(NCH):
                        s.dma("qg", dbg_xLB[cc * 128:(cc + 1) * 128, t0:t0 + Tt],
                              xL[:, cc * Tt:(cc + 1) * Tt], reads=[f"xL{cc}"])
                if it + 1 < NT:
                    pn = (it + 1) % 2
                    for cc in range(NCH):
                        s.dma("qs", xbB[pn][:, cc * Tt:(cc + 1) * Tt],
                              x1[cc * 128:(cc + 1) * 128,
                                 (it + 1) * Tt:(it + 2) * Tt],
                              writes=[f"xbB{pn}_{cc}"])
                    ln_prefix(xbB[pn], xbf, xkey=f"xbB{pn}_", xsq=xsqB)
                for oc in range(NCH):
                    pre, pim = pb[oc % 2], pb[2 + oc % 2]
                    rb, ib = ev[2 * (oc % 2)], ev[2 * (oc % 2) + 1]

                    def mmg(b, W=None, ps=None, oc=oc):
                        for cc2 in range(NCH):
                            i = b.matmul(
                                ps[:, 0:Tt],
                                W[:, cc2 * C + oc * 128:cc2 * C + oc * 128 + 128],
                                xL[:, cc2 * Tt:(cc2 + 1) * Tt],
                                start=(cc2 == 0), stop=(cc2 == NCH - 1))
                        return i
                    xkeys = [f"xL{cc2}" for cc2 in range(NCH)]
                    s.op("te", functools.partial(mmg, W=A_b, ps=pre),
                         reads=xkeys + ["Ab"], writes=[f"bank{oc % 2}"])
                    s.op("te", functools.partial(mmg, W=B_b, ps=pim),
                         reads=xkeys + ["Bb"], writes=[f"bank{2 + oc % 2}"])
                    s.op("se", lambda b, rb=rb, pre=pre: b.activation(
                        out=rb, in_=pre[:, 0:Tt], func=AF.Square),
                        reads=[f"bank{oc % 2}"], writes=[f"ev{2 * (oc % 2)}"])
                    s.op("se", lambda b, ib=ib, pim=pim: b.activation(
                        out=ib, in_=pim[:, 0:Tt], func=AF.Square),
                        reads=[f"bank{2 + oc % 2}"], writes=[f"ev{2 * (oc % 2) + 1}"])
                    s.op("ve", lambda b, rb=rb, ib=ib: b.tensor_add(
                        out=rb, in0=rb, in1=ib),
                        reads=[f"ev{2 * (oc % 2)}", f"ev{2 * (oc % 2) + 1}"],
                        writes=[f"ev{2 * (oc % 2)}"])
                    s.op("se", lambda b, rb=rb: b.activation(
                        out=rb, in_=rb, func=AF.Sqrt),
                        reads=[f"ev{2 * (oc % 2)}"], writes=[f"ev{2 * (oc % 2)}"])
                    s.op("ve", lambda b, rb=rb, oc=oc, xb=xb: b.tensor_tensor(
                        out=x2b[:, oc * Tt:(oc + 1) * Tt], in0=rb,
                        in1=xb[:, oc * Tt:(oc + 1) * Tt], op=ALU.add),
                        reads=[f"ev{2 * (oc % 2)}", xk_ + str(oc)],
                        writes=[f"x2b{oc}"])
                    s.dma("qg", x2[oc * 128:(oc + 1) * 128, t0:t0 + Tt],
                          x2b[:, oc * Tt:(oc + 1) * Tt], reads=[f"x2b{oc}"])

        def phase_C():
            Tt = 512
            NT = T // Tt
            NKT = T // 128
            ar.reset()
            wat_b = ar.bf16(NCH * 3 * C)
            stg = [ar.f32(3 * C), ar.f32(3 * C)]
            xbC = [ar.f32(NCH * Tt), ar.f32(NCH * Tt)]
            xbf = ar.bf16(NCH * Tt)
            xL = ar.bf16(NCH * Tt)
            ev = [ar.bf16(Tt) for _ in range(4)]
            xsqC = ar.bf16(NCH * Tt)
            atmpC = [ar.f32(Tt), ar.f32(Tt)]

            load_weightT(wat_b, wattnT, stg, 3 * C, "wat")

            for cc in range(NCH):
                s.dma("qs", xbC[0][:, cc * Tt:(cc + 1) * Tt],
                      x2[cc * 128:(cc + 1) * 128, 0:Tt],
                      writes=[f"xbC0_{cc}"])
            ln_prefix(xbC[0], xbf, xkey="xbC0_", xsq=xsqC)
            for it in range(NT):
                t0 = it * Tt
                p = it % 2
                xb = xbC[p]
                ln_apply(xb, xbf, xL, xkey=f"xbC{p}_", atmp=atmpC)
                if it + 1 < NT:
                    pn = (it + 1) % 2
                    for cc in range(NCH):
                        s.dma("qs", xbC[pn][:, cc * Tt:(cc + 1) * Tt],
                              x2[cc * 128:(cc + 1) * 128,
                                 (it + 1) * Tt:(it + 2) * Tt],
                              writes=[f"xbC{pn}_{cc}"])
                    ln_prefix(xbC[pn], xbf, xkey=f"xbC{pn}_", xsq=xsqC)
                xkeys = [f"xL{cc}" for cc in range(NCH)]
                for which, base, scratch in (("q", 0, q_s), ("k", C, k_s)):
                    for dc in range(NCH):
                        ps = pb[dc % 2]

                        def qmm(b, base=base, dc=dc, ps=ps):
                            for cc2 in range(NCH):
                                i = b.matmul(
                                    ps[:, 0:Tt],
                                    wat_b[:, cc2 * 3 * C + base + dc * 128:
                                          cc2 * 3 * C + base + dc * 128 + 128],
                                    xL[:, cc2 * Tt:(cc2 + 1) * Tt],
                                    start=(cc2 == 0), stop=(cc2 == NCH - 1))
                            return i
                        s.op("te", qmm, reads=xkeys + ["wat"],
                             writes=[f"bank{dc % 2}"])
                        eb = ev[dc % 2]
                        s.op("se", lambda b, eb=eb, ps=ps: b.copy(
                            out=eb, in_=ps[:, 0:Tt]),
                            reads=[f"bank{dc % 2}"], writes=[f"ev{dc % 2}"])
                        s.dma("qg", scratch[dc * 128:(dc + 1) * 128, t0:t0 + Tt],
                              eb, reads=[f"ev{dc % 2}"])
                for ts_ in range(Tt // 128):
                    for dsl in range(2):
                        bk = 2 + (ts_ * 2 + dsl) % 2
                        ps = pb[bk]

                        def vmm(b, ts_=ts_, dsl=dsl, ps=ps):
                            for cc2 in range(NCH):
                                i = b.matmul(
                                    ps[:, 0:512],
                                    xL[:, cc2 * Tt + ts_ * 128:
                                       cc2 * Tt + ts_ * 128 + 128],
                                    wat_b[:, cc2 * 3 * C + 2 * C + dsl * 512:
                                          cc2 * 3 * C + 2 * C + (dsl + 1) * 512],
                                    start=(cc2 == 0), stop=(cc2 == NCH - 1))
                            return i
                        s.op("te", vmm, reads=xkeys + ["wat"],
                             writes=[f"bank{bk}"])
                        eb = ev[bk]
                        s.op("se", lambda b, eb=eb, ps=ps: b.copy(
                            out=eb, in_=ps[:, 0:512]),
                            reads=[f"bank{bk}"], writes=[f"ev{bk}"])
                        s.dma("qg",
                              v_s[t0 + ts_ * 128:t0 + (ts_ + 1) * 128,
                                  dsl * 512:(dsl + 1) * 512],
                              eb, reads=[f"ev{bk}"])
            s.barrier()
            ar.reset()
            wpj_b = ar.bf16(NCH * C)
            stg2 = [ar.f32(C), ar.f32(C)]
            Ych = [ar.bf16(T) for _ in range(NCH)]
            K2 = [ar.bf16(T) for _ in range(2)]
            Q2 = [ar.bf16(T) for _ in range(2)]
            V2 = [ar.bf16(2 * NKT * 66) for _ in range(2)]
            PT = [ar.bf16(384) for _ in range(7)]
            PTG = ar.bf16(512)
            Osb = ar.bf16(512)
            denrow = ar.f32(512)
            denb = ar.bf16(512)
            xb2 = ar.f32(NCH * Tt)
            x3b = ar.f32(NCH * Tt)

            load_weightT(wpj_b, wprojT, stg2, C, "wpj")
            for pp in range(2):
                s.op("ve", lambda b, pp=pp: b.memset(V2[pp], 1.0),
                     writes=[f"V2_{pp}"])

            v_sr = v_s.rearrange("(kt q) c -> q kt c", q=128)

            def emit_O(j, hp, qb):
                p = j % 2
                h = 2 * j + hp
                hb = 64 * hp
                pO = pb[4]
                lo_q = qb * 512
                rd_keys = [f"V2_{p}", "PTG"]
                for kt in range(max(0, 4 * qb - 2), min(NKT, 4 * qb + 4)):
                    rd_keys.append(f"PT{kt % 7}")

                def ogrp(b, p=p, hp=hp, qb=qb, pO=pO):
                    i = None
                    for blk in range(4):
                        ga = qb * 512 + blk * 128
                        gb = ga + 128
                        wr = [kt for kt in range(max(0, ga // 128 - 2),
                                                 ga // 128 + 1)]
                        for n_, kt in enumerate(wr):
                            i = b.matmul(
                                pO[0:65, blk * 128:(blk + 1) * 128],
                                V2[p][:, hp * NKT * 66 + kt * 66:
                                      hp * NKT * 66 + kt * 66 + 65],
                                PT[kt % 7][:, ga - kt * 128:gb - kt * 128],
                                start=(n_ == 0), stop=False)
                        i = b.matmul(
                            pO[0:65, blk * 128:(blk + 1) * 128],
                            V2[p][0:4, hp * NKT * 66:hp * NKT * 66 + 65],
                            PTG[0:4, blk * 128:(blk + 1) * 128],
                            start=False, stop=True)
                    return i
                s.op("te", ogrp, reads=rd_keys, writes=["bank4"])
                s.op("se", lambda b: b.activation(
                    out=denrow[0:1, 0:512], in_=pO[64:65, 0:512], func=AF.Ln),
                    reads=["bank4"], writes=["denrow"])
                s.op("se", lambda b: b.activation(
                    out=denb[0:1, 0:512], in_=denrow[0:1, 0:512], func=AF.Exp,
                    scale=-1.0),
                    reads=["denrow"], writes=["denb"])
                s.op("te", lambda b: b.matmul(
                    pb[3][:, 0:512], ones_row[0:1, 0:128], denb[0:1, 0:512],
                    start=True, stop=True),
                    reads=["denb", "ones_row"], writes=["bank3"])
                s.op("se", lambda b, hb=hb: b.copy(
                    out=Osb[hb:hb + 64, 0:512], in_=pO[0:64, 0:512]),
                    reads=["bank4"], writes=["Osb"])
                if DEBUG_C and j == 0 and hp == 0 and qb < 2:
                    s.dma("qg", dbg_O[0:64, qb * 512:(qb + 1) * 512],
                          Osb[0:64, 0:512], reads=["Osb"])
                    s.dma("qg", dbg_O[64:65, qb * 512:(qb + 1) * 512],
                          denb[0:1, 0:512], reads=["denb"])
                ch = h // 2
                s.op("ve", lambda b, hb=hb, ch=ch, qb=qb: b.tensor_tensor(
                    out=Ych[ch][hb:hb + 64, qb * 512:(qb + 1) * 512],
                    in0=Osb[hb:hb + 64, 0:512], in1=pb[3][hb:hb + 64, 0:512],
                    op=ALU.mult),
                    reads=["Osb", "bank3"], writes=[f"Y{ch}"])

            for j in range(8):
                p = j % 2
                s.dma("qs", K2[p], k_s[j * 128:(j + 1) * 128, :],
                      writes=[f"K2_{p}"])
                s.dma("qs", Q2[p], q_s[j * 128:(j + 1) * 128, :],
                      writes=[f"Q2_{p}"])
                for hp in range(2):
                    h = 2 * j + hp
                    dst = V2[p][:, hp * NKT * 66:(hp + 1) * NKT * 66]
                    dst3 = dst.rearrange("q (kt d) -> q kt d", d=66)[:, :, 0:64]
                    s.dma("qs", dst3, v_sr[:, :, h * 64:(h + 1) * 64],
                          writes=[f"V2_{p}"])
                if DEBUG_C and j == 0:
                    s.dma("qg", dbg_V2, V2[p], reads=[f"V2_{p}"])
                for hp in range(2):
                    hb = 64 * hp
                    for kt in range(NKT):
                        NQ = min(384, T - kt * 128)
                        bk = kt % 3
                        ps = pb[bk]

                        def stmm(b, kt=kt, NQ=NQ, ps=ps, p=p, hb=hb):
                            i = b.matmul(
                                ps[:, 0:NQ],
                                K2[p][hb:hb + 64, kt * 128:(kt + 1) * 128],
                                Q2[p][hb:hb + 64, kt * 128:kt * 128 + NQ],
                                start=True, stop=False)
                            i = b.matmul(ps[:, 0:128], ident, m1,
                                         start=False, stop=(NQ <= 256))
                            if NQ > 256:
                                i = b.matmul(ps[:, 256:384], ident, m2,
                                             start=False, stop=True)
                            return i
                        s.op("te", stmm,
                             reads=[f"K2_{p}", f"Q2_{p}", "ident", "m1", "m2"],
                             writes=[f"bank{bk}"])
                        slot = kt % 7
                        s.op("se", lambda b, kt=kt, NQ=NQ, ps=ps, slot=slot:
                             b.activation(out=PT[slot][:, 0:NQ],
                                          in_=ps[:, 0:NQ], func=AF.Exp,
                                          scale=0.125),
                             reads=[f"bank{bk}"], writes=[f"PT{slot}"])
                        if DEBUG_C and j == 0 and hp == 0 and kt < 8:
                            s.dma("qg", dbg_PT[:, kt * 384:(kt + 1) * 384],
                                  PT[slot][:, 0:384], reads=[f"PT{slot}"])
                        if kt % 4 == 3:
                            qb = kt // 4
                            def gmm(b, p=p, hb=hb, qb=qb):
                                i = b.matmul(
                                    pb[3][0:4, 0:512],
                                    K2[p][hb:hb + 64, 0:4],
                                    Q2[p][hb:hb + 64, qb * 512:(qb + 1) * 512],
                                    start=True, stop=(qb > 0))
                                if qb == 0:
                                    i = b.matmul(pb[3][0:4, 0:512],
                                                 ident[0:4, 0:4], mglob,
                                                 start=False, stop=True)
                                return i
                            s.op("te", gmm,
                                 reads=[f"K2_{p}", f"Q2_{p}", "ident", "mglob"],
                                 writes=["bank3"])
                            s.op("se", lambda b: b.activation(
                                out=PTG[0:4, 0:512], in_=pb[3][0:4, 0:512],
                                func=AF.Exp, scale=0.125),
                                reads=["bank3"], writes=["PTG"])
                            emit_O(j, hp, qb)
            if DEBUG_C:
                for cc in range(NCH):
                    s.dma("qg", dbg_q[cc * 128:(cc + 1) * 128, :],
                          q_s[cc * 128:(cc + 1) * 128, :])
                    s.dma("qg", dbg_kc[cc * 128:(cc + 1) * 128, :],
                          k_s[cc * 128:(cc + 1) * 128, :])
                    s.dma("qg", dbg_vc[cc * 512:(cc + 1) * 512, :],
                          v_s[cc * 512:(cc + 1) * 512, :])
                    s.dma("qg", dbg_Y[cc * 128:(cc + 1) * 128, :],
                          Ych[cc], reads=[f"Y{cc}"])
            ykeys = [f"Y{ch}" for ch in range(NCH)]
            for it in range(NT):
                t0 = it * Tt
                for cc in range(NCH):
                    s.dma("qs", xb2[:, cc * Tt:(cc + 1) * Tt],
                          x2[cc * 128:(cc + 1) * 128, t0:t0 + Tt],
                          writes=[f"xc{cc}"])
                for oc in range(NCH):
                    ps = pb[oc % 2]

                    def pmm(b, oc=oc, ps=ps, t0=t0):
                        for dc2 in range(NCH):
                            i = b.matmul(
                                ps[:, 0:Tt],
                                wpj_b[:, dc2 * C + oc * 128:
                                      dc2 * C + oc * 128 + 128],
                                Ych[dc2][:, t0:t0 + Tt],
                                start=(dc2 == 0), stop=(dc2 == NCH - 1))
                        return i
                    s.op("te", pmm, reads=ykeys + ["wpj"],
                         writes=[f"bank{oc % 2}"])
                    s.op("ve", lambda b, oc=oc, ps=ps: b.tensor_tensor(
                        out=x3b[:, oc * Tt:(oc + 1) * Tt], in0=ps[:, 0:Tt],
                        in1=xb2[:, oc * Tt:(oc + 1) * Tt], op=ALU.add),
                        reads=[f"bank{oc % 2}", f"xc{oc}"],
                        writes=[f"x3b{oc}"])
                    s.dma("qg", x3[oc * 128:(oc + 1) * 128, t0:t0 + Tt],
                          x3b[:, oc * Tt:(oc + 1) * Tt], reads=[f"x3b{oc}"])

        def phase_D():
            St = 1024
            NT = T // St
            ar.reset()
            gw_b = ar.bf16(NCH * 8)
            glf = ar.f32(1024)
            glb = ar.bf16(1024)
            gl_tm = ar.f32(64)
            gsq = ar.f32(64)
            msq = ar.f32(8)
            lnb = ar.f32(8)
            lnc = ar.f32(8)
            rstd8 = ar.f32(8)
            l2b = ar.f32(64)
            l3b = ar.f32(64)
            mx8 = ar.f32(8)
            zb = ar.f32(64)
            eb8 = ar.f32(64)
            ec8 = ar.f32(64)
            s8 = ar.f32(8)
            t8 = ar.f32(8)
            ng8 = ar.bf16(8)
            ngT = ar.bf16(128)
            ngr = ar.bf16(St)
            cmb = [ar.f32(St), ar.f32(St)]
            dbg_trs = ar.f32(32)
            b2s = ar.f32(NCH)
            xb = ar.f32(NCH * St)
            xbf = ar.bf16(NCH * 512)
            xL = ar.bf16(NCH * St)
            stg = [ar.f32(St), ar.f32(St)]
            w1h = [ar.bf16(NCH * St) for _ in range(2)]
            w2h = [ar.bf16(NCH * St) for _ in range(2)]
            hbuf = ar.bf16(NCH * St)
            ybuf = ar.f32(NCH * St)

            s.op("ve", lambda b: b.memset(ec8[:, 0:64], 0.0), writes=["ec8"])
            s.op("ve", lambda b: b.memset(s8[:, 0:8], 0.0), writes=["s8"])
            load_weightT(gw_b, gwT, stg, 8, "gw")
            s.dma("qs", gbv, gb, writes=["gbv"])
            s.dma("qs", rmsv, rmsw, writes=["rmsv"])
            s.dma("qs", b2v, b2, writes=["b2v"])
            s.dma("qs", b1v, b1, writes=["b1v"])
            for cc in range(NCH):
                s.op("ve", lambda b, cc=cc: b.reduce_sum(
                    out=b2s[:, cc:cc + 1], in_=b2v[:, cc * 8:(cc + 1) * 8],
                    axis=AX.X),
                    reads=["b2v"], writes=["b2s"])

            for it in range(NT):
                t0 = it * St
                for cc in range(NCH):
                    s.dma("qs", xb[:, cc * St:(cc + 1) * St],
                          x3[cc * 128:(cc + 1) * 128, t0:t0 + St],
                          writes=[f"xb{cc}"])
                for sl in range(2):
                    layernorm_tile(xb, xbf, xL, tile_tag="D", xkey="xb",
                                   out_off=0, xl_stride=St, xl_off=sl * 512,
                                   xb_stride=St, xb_off=sl * 512)
                xkeys = [f"xL{cc}" for cc in range(NCH)]

                def _gating():
                  def gmm(b):
                    for sl in range(2):
                        for cc2 in range(NCH):
                            i = b.matmul(
                                pstat[0:8, sl * 512:(sl + 1) * 512],
                                gw_b[:, cc2 * 8:(cc2 + 1) * 8],
                                xL[:, cc2 * St + sl * 512:cc2 * St + (sl + 1) * 512],
                                start=(cc2 == 0), stop=(cc2 == NCH - 1))
                    return i
                  s.op("te", gmm, reads=xkeys + ["gw"], writes=["pstat"])
                  s.op("ve", lambda b: b.tensor_scalar_add(
                    out=glf[0:8, 0:St], in0=pstat[0:8, 0:St],
                    scalar1=gbv[0:8, 0:1]),
                    reads=["pstat", "gbv"], writes=["glf"])
                  s.op("ve", lambda b: b.tensor_copy(out=glb[0:8, 0:St],
                                                   in_=glf[0:8, 0:St]),
                     reads=["glf"], writes=["glb"])
                  for qs_ in (range(8) if D_GLEVEL >= 2 else []):
                    s.op("te", lambda b, qs_=qs_: b.transpose(
                        pbt[:, qs_ * 8:(qs_ + 1) * 8],
                        glb[0:8, qs_ * 128:(qs_ + 1) * 128],
                        ident[0:8, 0:8]),
                        reads=["glb", "ident"], writes=["pbt"])
                    s.op("ve", lambda b, qs_=qs_: b.tensor_copy(
                        out=gl_tm[:, qs_ * 8:(qs_ + 1) * 8],
                        in_=pbt[:, qs_ * 8:(qs_ + 1) * 8]),
                        reads=["pbt"], writes=["gl_tm"])
                  if D_GLEVEL >= 3:
                    r3 = lambda ap: ap.rearrange("p (q e) -> p q e", e=8)
                    bc = lambda ap: ap.rearrange(
                        "p (o e) -> p o e", o=1).broadcast_to([128, 8, 8])
                    s.op("se", lambda b: b.activation(
                        out=gsq[:, 0:64], in_=gl_tm[:, 0:64], func=AF.Square),
                        reads=["gl_tm"], writes=["gsq"])
                    s.op("ve", lambda b: b.tensor_reduce(
                        out=msq[:, 0:8], in_=r3(gsq[:, 0:64]), axis=AX.X,
                        op=ALU.add),
                        reads=["gsq"], writes=["msq"])
                    s.op("se", lambda b: b.activation(
                        out=lnb[:, 0:8], in_=msq[:, 0:8], func=AF.Ln,
                        scale=1.0 / 8, bias=consts[:, 1:2]),
                        reads=["msq", "consts"], writes=["lnb"])
                    s.op("gp", lambda b: b.tensor_copy(
                        out=lnc[:, 0:8], in_=lnb[:, 0:8]),
                        reads=["lnb"], writes=["lnc"])
                    s.op("se", lambda b: b.activation(
                        out=rstd8[:, 0:8], in_=lnc[:, 0:8], func=AF.Exp,
                        scale=-0.5),
                        reads=["lnc"], writes=["rstd8"])
                    s.op("ve", lambda b: b.tensor_tensor(
                        out=r3(l2b[:, 0:64]), in0=r3(gl_tm[:, 0:64]),
                        in1=bc(rstd8[:, 0:8]), op=ALU.mult),
                        reads=["gl_tm", "rstd8"], writes=["l2b"])
                    s.op("gp", lambda b: b.tensor_tensor(
                        out=r3(l3b[:, 0:64]), in0=r3(l2b[:, 0:64]),
                        in1=bc(rmsv[:, 0:8]), op=ALU.mult),
                        reads=["l2b", "rmsv"], writes=["l3b"])
                    s.op("ve", lambda b: b.tensor_reduce(
                        out=mx8[:, 0:8], in_=r3(l3b[:, 0:64]), axis=AX.X,
                        op=ALU.max),
                        reads=["l3b"], writes=["mx8"])
                    s.op("gp", lambda b: b.tensor_tensor(
                        out=r3(zb[:, 0:64]), in0=r3(l3b[:, 0:64]),
                        in1=bc(mx8[:, 0:8]), op=ALU.subtract),
                        reads=["l3b", "mx8"], writes=["zb"])
                    s.op("se", lambda b: b.activation(
                        out=eb8[:, 0:64], in_=zb[:, 0:64], func=AF.Exp,
                        scale=2.0),
                        reads=["zb"], writes=["eb8"])
                    s.op("ve", lambda b: b.tensor_scalar_min(
                        out=ec8[:, 0:64], in0=eb8[:, 0:64], scalar1=1.0),
                        reads=["eb8"], writes=["ec8"])
                    s.op("ve", lambda b: b.tensor_reduce(
                        out=s8[:, 0:8], in_=r3(ec8[:, 0:64]), axis=AX.X,
                        op=ALU.add),
                        reads=["ec8"], writes=["s8"])
                    s.op("gp", lambda b: b.tensor_scalar(
                        out=t8[:, 0:8], in0=s8[:, 0:8], scalar1=8.0,
                        scalar2=1e-6, op0=ALU.min, op1=ALU.mult),
                        reads=["s8"], writes=["t8"])
                    s.op("ve", lambda b: b.tensor_scalar(
                        out=ng8[:, 0:8], in0=t8[:, 0:8], scalar1=-1.0,
                        scalar2=1.0, op0=ALU.mult, op1=ALU.add),
                        reads=["t8"], writes=["ng8"])
                  if D_GLEVEL >= 4:
                    s.op("te", lambda b: b.transpose(
                        pbt[0:8, 0:128], ng8[:, 0:8], ident),
                        reads=["ng8", "ident"], writes=["pbt2"])
                    s.op("ve", lambda b: b.tensor_copy(
                        out=ngT[0:8, 0:128], in_=pbt[0:8, 0:128]),
                        reads=["pbt2"], writes=["ngT"])
                    for qs_ in range(8):
                        s.op("te", lambda b, qs_=qs_: b.matmul(
                            pb[4][:, 0:128],
                            sel8[0:8, qs_ * 128:(qs_ + 1) * 128],
                            ngT[0:8, 0:128], start=True, stop=True),
                            reads=["ngT", "sel8"], writes=["bank4"])
                        s.op("ve", lambda b, qs_=qs_: b.tensor_copy(
                            out=ngr[:, qs_ * 128:(qs_ + 1) * 128],
                            in_=pb[4][:, 0:128]),
                            reads=["bank4"], writes=["ngr"])
                  else:
                    s.op("ve", lambda b: b.memset(ngr, 1.0), writes=["ngr"])
                if "g" in D_PARTS:
                    _gating()
                else:
                    s.op("ve", lambda b: b.memset(ngr, 1.0), writes=["ngr"])
                if DEBUG_D and it == 0:
                    s.dma("qg", dbg_ng8, ng8[:, 0:8], reads=["ng8"])
                    s.dma("qg", dbg_gl, gl_tm[:, 0:64], reads=["gl_tm"])
                    s.dma("qg", dbg_ngr, ngr[:, 0:1024], reads=["ngr"])
                def loadW(e_, half, slot):
                    for cc in range(NCH):
                        pgi = cc % 2
                        s.dma("qs", stg[pgi],
                              w1[e_, cc * 128:(cc + 1) * 128,
                                 half * 1024:(half + 1) * 1024],
                              writes=[f"stgD{pgi}"])
                        s.op("gp",
                             lambda b, cc=cc, pgi=pgi, slot=slot:
                             b.tensor_copy(
                                 out=w1h[slot][:, cc * St:(cc + 1) * St],
                                 in_=stg[pgi]),
                             reads=[f"stgD{pgi}"], writes=[f"w1h{slot}"])
                    for hc in range(NCH):
                        pgi = hc % 2
                        s.dma("qs", stg[pgi],
                              w2[e_, half * 1024 + hc * 128:
                                 half * 1024 + (hc + 1) * 128, :],
                              writes=[f"stgD{pgi}"])
                        s.op("gp",
                             lambda b, hc=hc, pgi=pgi, slot=slot:
                             b.tensor_copy(
                                 out=w2h[slot][:, hc * St:(hc + 1) * St],
                                 in_=stg[pgi]),
                             reads=[f"stgD{pgi}"], writes=[f"w2h{slot}"])

                def _experts():
                  if it == 0:
                      loadW(0, 0, 0)
                  for e_ in range(8):
                      for half in range(2):
                          idx = it * 16 + e_ * 2 + half
                          slot = idx % 2
                          wbuf = w1h[slot]
                          wkey = f"w1h{slot}"
                          w2b = w2h[slot]
                          w2key = f"w2h{slot}"
                          for hc in range(NCH):
                              for sl in range(2):
                                  ph = pb[(hc * 2 + sl) % 2]
                                  bk = f"bank{(hc * 2 + sl) % 2}"

                                  def hmm(b, hc=hc, sl=sl, ph=ph, wbuf=wbuf):
                                      for cc2 in range(NCH):
                                          i = b.matmul(
                                              ph[:, 0:512],
                                              wbuf[:, cc2 * St + hc * 128:
                                                   cc2 * St + hc * 128 + 128],
                                              xL[:, cc2 * St + sl * 512:
                                                 cc2 * St + (sl + 1) * 512],
                                              start=(cc2 == 0),
                                              stop=(cc2 == NCH - 1))
                                      return i
                                  s.op("te", hmm, reads=xkeys + [wkey],
                                       writes=[bk])
                                  bcol = e_ * 16 + half * 8 + hc
                                  s.op("se", lambda b, hc=hc, sl=sl, ph=ph,
                                       bcol=bcol: b.activation(
                                           out=hbuf[:, hc * St + sl * 512:
                                                    hc * St + (sl + 1) * 512],
                                           in_=ph[:, 0:512], func=AF.Gelu,
                                           bias=b1v[:, bcol:bcol + 1]),
                                       reads=[bk, "b1v"], writes=[f"hb{hc}"])
                          nidx = idx + 1
                          if nidx < NT * 16:
                              j = nidx % 16
                              loadW(j // 2, j % 2, nidx % 2)
                          for dc2 in range(NCH):
                              for sl in range(2):
                                  py = pb[2 + (dc2 * 2 + sl) % 2]
                                  bk = f"bank{2 + (dc2 * 2 + sl) % 2}"

                                  def ymm(b, dc2=dc2, sl=sl, py=py, w2b=w2b):
                                      for hc2 in range(NCH):
                                          i = b.matmul(
                                              py[:, 0:512],
                                              w2b[:, hc2 * St + dc2 * 128:
                                                  hc2 * St + dc2 * 128 + 128],
                                              hbuf[:, hc2 * St + sl * 512:
                                                   hc2 * St + (sl + 1) * 512],
                                              start=(hc2 == 0),
                                              stop=(hc2 == NCH - 1))
                                      return i
                                  hkeys = [f"hb{hc2}" for hc2 in range(NCH)]
                                  s.op("te", ymm, reads=hkeys + [w2key],
                                       writes=[bk])
                                  ysl = ybuf[:, dc2 * St + sl * 512:
                                             dc2 * St + (sl + 1) * 512]
                                  if e_ == 0 and half == 0:
                                      s.op("ve", lambda b, ysl=ysl, py=py:
                                           b.tensor_copy(out=ysl,
                                                         in_=py[:, 0:512]),
                                           reads=[bk], writes=[f"yb{dc2}"])
                                  else:
                                      s.op("ve", lambda b, ysl=ysl, py=py:
                                           b.tensor_add(out=ysl, in0=py[:, 0:512],
                                                        in1=ysl),
                                           reads=[bk, f"yb{dc2}"],
                                           writes=[f"yb{dc2}"])
                if "e" in D_PARTS:
                    _experts()
                else:
                    for dc2 in range(NCH):
                        s.op("ve", lambda b, dc2=dc2: b.memset(ybuf[:, dc2 * St:(dc2 + 1) * St], 0.0), writes=[f"yb{dc2}"])
                if DEBUG_D and it == 0:
                    s.dma("qg", dbg_y, ybuf[:, 0:8 * 1024],
                          reads=[f"yb{cc}" for cc in range(NCH)])
                    s.dma("qg", dbg_xl, xL[:, 0:8 * 1024],
                          reads=[f"xL{cc}" for cc in range(NCH)])
                for cc in range(NCH):
                    cb = cmb[cc % 2]
                    ck = f"cmb{cc % 2}"
                    s.op("ve", lambda b, cc=cc, cb=cb: b.tensor_scalar_add(
                        out=cb[:, 0:St],
                        in0=ybuf[:, cc * St:(cc + 1) * St],
                        scalar1=b2s[:, cc:cc + 1]),
                        reads=[f"yb{cc}", "b2s"], writes=[ck])
                    s.op("ve", lambda b, cb=cb: b.tensor_mul(
                        out=cb[:, 0:St], in0=cb[:, 0:St], in1=ngr),
                        reads=[ck, "ngr"], writes=[ck])
                    s.op("ve", lambda b, cc=cc, cb=cb: b.tensor_tensor(
                        out=cb[:, 0:St], in0=cb[:, 0:St],
                        in1=xb[:, cc * St:(cc + 1) * St], op=ALU.add),
                        reads=[ck, f"xb{cc}"], writes=[ck])
                    s.dma("qg", out_fm[cc * 128:(cc + 1) * 128, t0:t0 + St],
                          cb[:, 0:St], reads=[ck])

        if "A" in phases:
            phase_A()
            s.barrier()
        if "B" in phases:
            phase_B()
            s.barrier()
        if "C" in phases:
            phase_C()
            s.barrier()
        if "D" in phases:
            phase_D()
            s.barrier()

        last = {"A": x1, "B": x2, "C": x3, "D": None}[phases[-1]]
        if last is not None:
            for cc in range(NCH):
                s.dma("qg", out_fm[cc * 128:(cc + 1) * 128, :],
                      last[cc * 128:(cc + 1) * 128, :], reads=[])
        s.finalize()

        block = stack.enter_context(nc.Block())

        @block.tensor
        def _(b):
            for f in s.prog["te"]:
                f(b)

        @block.vector
        def _(b):
            for f in s.prog["ve"]:
                f(b)

        @block.scalar
        def _(b):
            for f in s.prog["se"]:
                f(b)

        @block.gpsimd
        def _(b):
            for f in s.prog["gp"]:
                f(b)

        @block.sync
        def _(b):
            for f in s.prog["sy"]:
                f(b)

    return nc


def _host_inputs(inputs):
    f32 = np.float32
    bf = ml_dtypes.bfloat16
    g = lambda k: np.asarray(inputs[k], f32)

    def vec_fm(v):
        return np.ascontiguousarray(np.asarray(v, f32).reshape(NCH, 128).T)

    tmix = np.concatenate([vec_fm(inputs["L0_attn_time_mix_k"]),
                           vec_fm(inputs["L0_attn_time_mix_v"]),
                           vec_fm(inputs["L0_attn_time_mix_r"])], axis=1)
    common = dict(
        wkT=np.ascontiguousarray(g("L0_attn_Wk").T),
        wvT=np.ascontiguousarray(g("L0_attn_Wv").T),
        wrT=np.ascontiguousarray(g("L0_attn_Wr").T),
        woT=np.ascontiguousarray(g("L0_attn_Wo").T),
        tmix=np.ascontiguousarray(tmix),
        tdec=vec_fm(inputs["L0_attn_time_decay"]),
        tfirst=vec_fm(inputs["L0_attn_time_first"]),
        theta=g("L0_ffn_theta"), phi=g("L0_ffn_phi"),
        theta_n=g("L0_ffn_theta_noise"), phi_n=g("L0_ffn_phi_noise"),
        wattnT=np.ascontiguousarray(g("L1_attn_w_attn").T),
        wprojT=np.ascontiguousarray(g("L1_attn_w_proj").T),
        gwT=np.ascontiguousarray(g("L1_ffn_gate_w").T),
        gb=np.ascontiguousarray(g("L1_ffn_gate_b").reshape(8, 1)),
        rmsw=np.ascontiguousarray(
            np.broadcast_to(g("L1_ffn_rms_w").reshape(1, 8), (128, 8))),
        w1=g("L1_ffn_w1"), w2=g("L1_ffn_w2"),
        b1=np.ascontiguousarray(
            g("L1_ffn_b1").reshape(8, 16, 128).transpose(2, 0, 1).reshape(128, 128)),
        b2=np.ascontiguousarray(
            g("L1_ffn_b2").reshape(8, 8, 128).transpose(2, 1, 0).reshape(128, 64)),
        ident_in=np.eye(128, dtype=bf),
    )
    i = np.arange(128)[:, None]
    j = np.arange(128)[None, :]
    common["m1_in"] = np.where(j >= i, 0.0, NEG).astype(bf)
    common["m2_in"] = np.where(j >= i, NEG, 0.0).astype(bf)
    qg = np.arange(512)[None, :]
    kk = np.arange(4)[:, None]
    common["mglob_in"] = np.where(qg < 256 + kk, NEG, 0.0).astype(bf)
    sel = np.zeros((8, 8 * 128), np.float32)
    for q_ in range(8):
        sel[q_, q_ * 128:(q_ + 1) * 128] = 1.0
    common["sel8_in"] = sel.astype(bf)

    x = np.asarray(inputs["L0_attn_x"], f32)
    maps = []
    for i_core in range(8):
        m = dict(common)
        m["xT"] = np.ascontiguousarray(x[i_core].T)
        maps.append(m)
    return maps


_NC_CACHE = {}


def kernel(**inputs):
    phases = PHASES
    key = tuple(phases)
    if key not in _NC_CACHE:
        _NC_CACHE[key] = build_program(phases)
    nc = _NC_CACHE[key]
    in_maps = _host_inputs(inputs)
    res = run_bass_kernel_spmd(nc, in_maps, core_ids=list(range(8)))
    out = np.stack([np.asarray(r["out_fm"], np.float32).T
                    for r in res.results], axis=0)
    return np.ascontiguousarray(out)
